# version 09
# baseline (speedup 1.0000x reference)
import numpy as np
from contextlib import ExitStack
import concourse.bass as bass
import concourse.mybir as mybir
from concourse.bass_utils import run_bass_kernel_spmd
from concourse import library_config
from concourse.library_overlay import lower_extended_insts

F32 = mybir.dt.float32
BF16 = mybir.dt.bfloat16
AF = mybir.ActivationFunctionType
OP = mybir.AluOpType
AXX = mybir.AxisListType.X

D = 2048
TOWN = 1024
CTX = 3
T = TOWN + CTX
NF = 16
NFO = 37
PROJ = 4640
PROJ_PAD = NFO * 128
NH, PDIM, G, S = 32, 64, 8, 32
NCONV = 20
NCHUNK = 8
Q = 128
KEXP, RRANK = 8, 64
KR = 512
NKR = 4
BW = 520

SBLK = [(0, 515, 0, 512), (515, 512, 512, 512)]
PIECES = {0: [(0, 387), (387, 128)], 1: [(0, 512)]}

_DEBUG = ()


class SemT:
    def __init__(self, nc, ctx, name):
        self.h = ctx.enter_context(nc.semaphore(name))
        self.n = 0

    def inc(self, bi, k=1):
        bi.then_inc(self.h, k)
        self.n += k
        return self.n

    def wait(self, eng, n=None):
        eng.wait_ge(self.h, self.n if n is None else n)

    def chain(self, eng, bi):
        c = self.inc(bi)
        eng.wait_ge(self.h, c)


def owned_map(a0, n):
    so = max(0, CTX - a0)
    return so, a0 + so - CTX, n - so


class Build:
    def __init__(self, debug=()):
        self.debug = tuple(debug)
        self.nc = bass.Bass("TRN2", target_bir_lowering=False, debug=False)
        self.ctx = ExitStack()
        self.dumps = []

    def sb(self, name, shape, dt=F32):
        return self.ctx.enter_context(self.nc.sbuf_tensor(name, shape, dt))

    def sem(self, name):
        return SemT(self.nc, self.ctx, name)

    def din(self, name, shape, dt=F32):
        return self.nc.dram_tensor(name, shape, dt, kind="ExternalInput").ap()

    def dout(self, name, shape, dt=F32):
        return self.nc.dram_tensor(name, shape, dt, kind="ExternalOutput").ap()

    def want(self, name):
        return name in self.debug

    def dump(self, name, tile, width):
        if not self.want(name):
            return
        nparts = tile[:].shape[0]
        dr = self.dout(f"dbg_{name}", [nparts, width], F32)
        blk = self.blk
        vn, tn, cn, gn = self.v_sem.n, self.t_sem.n, self.c_sem.n, self.g_sem.n

        @blk.gpsimd
        def _(g):
            self.v_sem.wait(g, vn)
            self.t_sem.wait(g, tn)
            self.c_sem.wait(g, cn)
            self.g_sem.wait(g, gn)
            bi = g.dma_start(out=dr, in_=tile[:, 0:width])
            self.g_ld.inc(bi, 16)
        self.dumps.append((f"dbg_{name}", name))

    def barrier(self, engines):
        blk = self.blk
        sems = [self.v_sem, self.t_sem, self.c_sem, self.g_sem]
        cts = [s.n for s in sems]

        spn = self.s_sp.n

        def mk(e):
            for s, c in zip(sems, cts):
                s.wait(e, c)
            e.wait_ge(self.s_sp.h, spn)
        for en in engines:
            if en == "t":
                @blk.tensor
                def _(t):
                    mk(t)
            elif en == "v":
                @blk.vector
                def _(v):
                    mk(v)
            elif en == "c":
                @blk.scalar
                def _(sc):
                    mk(sc)
            elif en == "g":
                @blk.gpsimd
                def _(g):
                    mk(g)

    def build(self):
        nc = self.nc
        self.declare_io()
        self.alloc()
        self.blk = self.ctx.enter_context(nc.Block())
        self.ph_init()
        with ExitStack() as ffn1_scope:
            self._scope = ffn1_scope
            self.VXW = self.scoped("VXW", [128, NKR * BW], BF16)
            for b in range(2):
                self.ph_ln_block(b, self.X, self.XNW)
                self.ph_ffn_block(b, 0.5)
            self.barrier("tvcg")
        self.dump("x1", self.X, NF * T)
        with ExitStack() as prep_scope:
            self._scope = prep_scope
            self.DTLD = self.scoped("DTLD", [NH, TOWN], F32)
            self.LDC = self.scoped("LDC", [NH, 128], F32)
            self.DTOK = self.scoped("DTOK", [128, 64], F32)
            self.NRT = self.scoped("NRT", [128, S], F32)
            self.WGT = self.scoped("WGT", [128, NH], F32)
            self.NLR = self.scoped("NLR", [1, 64], F32)
            self.BTW = self.scoped("BTW", [128, 2 * 512], BF16)
            self.CARR = self.scoped("CARR", [128, NCONV * 4], F32)
            self.CACC = self.scoped("CACC", [128, 512], F32)
            self.CSTR = self.scoped("CSTR", [128, 528], F32)
            self.XTOK = self.scoped("XTOK", [128, 2 * D], BF16)
            self.BW4 = self.scoped("BW4", [128, 128], BF16)
            self.ERS = self.scoped("ERS", [S, Q], F32)
            self.RB4 = self.scoped("RB4", [128, 512], F32)
            self.MT = self.ABC[:, 256:512].bitcast(BF16)
            self.RFMs = self.CSTR
            self.DTT = self.CACC
            for b in range(2):
                self.barrier("tvcg")
                self.ph_ln_block(b, self.X, self.XNW)
                self.ph_win_block(b)
                self.ph_dt_block(b)
                for c in range(4 * b, 4 * b + 4):
                    self.ph_ssd_chunk(b, c)
                self.ph_spill_block(b)
            self.dump("dtf", self.DTLD, TOWN)
            self.dump("convx", self.HW, NF * BW)
            self.dump("ct", self.CT, 2 * TOWN)
            self.dump("sst", self.SST, NCHUNK * 512)
            self.barrier("tvcg")
        self.ph_state_collective()
        with ExitStack() as epi_scope:
            self._scope = epi_scope
            self.SCAR = self.scoped("SCAR", [128, 512], F32)
            self.EBT = self.scoped("EBT", [128, 512], F32)
            self.CDT = self.scoped("CDT", [128, 512], BF16)
            self.CDT2 = self.scoped("CDT2", [128, 512], BF16)
            self.EBT2 = self.scoped("EBT2", [128, 512], F32)
            self.DTT = self.scoped("DTT2", [128, 512], F32)
            self.SUSED = self.scoped("SUSED", [128, 512], BF16)
            self.SUSED2 = self.scoped("SUSED2", [128, 512], BF16)
            self.RSTD = self.scoped("RSTD", [KEXP, 512], F32)
            self.RSTDB = self.scoped("RSTDB", [KEXP, 512], BF16)
            self.UNB = self.scoped("UNB", [128, NKR * BW], BF16)
            self.SILZC = self.UNB
            self.VXW = self.UNB
            self.ph_cc_recv()
            for b in range(2):
                self.barrier("tvcg")
                self.ph_y1_load(b)
                for c in range(4 * b, 4 * b + 4):
                    self.ph_epi_chunk(b, c)
                self.ph_epi_flush()
                if b == 0:
                    self.dump("ebt", self.EBT, 512)
                    self.dump("scar", self.SCAR, 512)
                    self.dump("y1b0", self.Y1B, NF * 512)
                    self.dump("sused", self.SUSED, 512)
                    self.dump("cdt", self.CDT, 512)
                self.ph_norm_wout_block(b)
                if b == 0:
                    self.dump("rstd", self.RSTD, 512)
                    self.dump("ynw", self.XNW, NF * BW)
                if b == 1:
                    self.dump("x2", self.X, NF * T)
                self.barrier("tvcg")
                self.ph_ln_block(b, self.X, self.XNW)
                self.ph_ffn_block(b, 0.5)
                self.ph_store_block(b)
        self.ph_finish()
        self.ctx.close()
        return nc

    def scoped(self, name, shape, dt):
        return self._scope.enter_context(self.nc.sbuf_tensor(name, shape, dt))

    def declare_io(self):
        d = self.din
        self.i_x = d("x_fm", [D, T])
        self.i_Win = d("Win", [NFO, 128, NF * 128])
        self.i_W1 = d("W1", [NF, 128, NF * 128])
        self.i_W2 = d("W2", [128, NF * KEXP])
        self.i_Wout = d("Wout", [NF, 128, NF * 128])
        self.i_Vc = d("Vc", [NKR, 128, NF * 128])
        self.i_Uc = d("Uc", [NF, 128, NKR * 128])
        self.i_bmup = d("bmup", [128, NFO])
        self.i_mup = d("mup", [128, NFO])
        self.i_b1 = d("b1col", [128, NF])
        self.i_b2 = d("b2col", [KEXP, 1])
        self.i_bout = d("boutcol", [128, NF])
        self.i_nw = d("nwcol", [128, NF])
        self.i_convb = d("convbcol", [128, NCONV])
        self.i_convw = d("convwcol", [128, NCONV * 4])
        self.i_Dcol = d("Dcol", [128, NF])
        self.i_Alog = d("Alogcol", [NH, 1])
        self.i_dtb = d("dtbcol", [NH, 1])
        self.i_mask = d("maskinf", [Q, Q])
        self.i_triu = d("triu01", [Q, Q])
        self.i_I128 = d("I128", [128, 128])
        self.i_IB128 = d("IB128", [128, 128], BF16)
        self.i_e127 = d("e127", [128, 1])
        self.i_flag = d("flagcol", [128, 1])
        self.i_flaginv = d("flaginvcol", [128, 1])
        self.i_misc = d("misccol", [128, 8])
        self.i_REPS = d("REPS", [KEXP, NKR * 128])
        self.i_GSEL = d("GSEL", [KEXP, G * 128], BF16)
        self.i_GIND = d("GIND", [128, NF * KEXP], BF16)
        self.h_r = self.nc.dram_tensor("r_scratch", [NCHUNK, NH * Q], F32,
                                       kind="Internal").ap()
        self.h_er = self.nc.dram_tensor("er_scratch", [NCHUNK, NH * Q], F32,
                                        kind="Internal").ap()
        self.h_silz = self.nc.dram_tensor("silz_hbm", [D, TOWN], BF16,
                                          kind="Internal").ap()
        self.h_y1 = self.nc.dram_tensor("y1_hbm", [D, TOWN], BF16,
                                        kind="Internal").ap()
        self.h_ccin = self.nc.dram_tensor("cc_in", [128, 512], F32,
                                          kind="Internal").ap()
        self.h_ccout = self.nc.dram_tensor("cc_out", [128, 512], F32,
                                           kind="Internal").ap()
        self.o_out = self.dout("out_fm", [D, TOWN])

    def alloc(self):
        sb = self.sb
        self.X = sb("X", [128, NF * T], F32)
        self.SILZB = sb("SILZB", [128, NF * 512], BF16)
        self.Y1B = sb("Y1B", [128, NF * 512], BF16)
        self.SST = sb("SST", [128, NCHUNK * 512], BF16)
        self.CT = sb("CT", [128, 2 * TOWN], BF16)
        self.DECROW = sb("DECROW", [1, NCHUNK * NH], F32)
        self.WA = sb("WA", [128, NF * 128], BF16)
        self.WB = sb("WB", [128, NF * 128], BF16)
        self.W2B = sb("W2B", [128, NF * KEXP], BF16)
        self.MASK = sb("MASK", [Q, Q], F32)
        self.TRIU = sb("TRIU", [Q, Q], F32)
        self.I128s = sb("I128s", [128, 128], F32)
        self.IB128s = sb("IB128s", [128, 128], BF16)
        self.E127 = sb("E127", [128, 1], F32)
        self.ONESB = sb("ONESB", [128, 1], BF16)
        self.FLAG = sb("FLAG", [128, 1], F32)
        self.FLAGI = sb("FLAGI", [128, 1], F32)
        self.MISC = sb("MISC", [128, 8], F32)
        self.BMUP = sb("BMUP", [128, NFO], F32)
        self.MUP = sb("MUPc", [128, NFO], F32)
        self.B1 = sb("B1c", [128, NF], F32)
        self.B2 = sb("B2c", [KEXP, 1], F32)
        self.BOUT = sb("BOUTc", [128, NF], F32)
        self.NW = sb("NWc", [128, NF], F32)
        self.CONVB = sb("CONVBc", [128, NCONV], F32)
        self.CONVW = sb("CONVWc", [128, NCONV * 4], F32)
        self.DCOL = sb("DCOLc", [128, NF], F32)
        self.ALOG = sb("ALOGc", [NH, 1], F32)
        self.DTB = sb("DTBc", [NH, 1], F32)
        self.REPSs = sb("REPSs", [KEXP, NKR * 128], F32)
        self.GSELs = sb("GSELs", [KEXP, G * 128], BF16)
        self.GINDs = sb("GINDs", [128, NF * KEXP], BF16)
        self.ROWS = sb("ROWSr", [1, 2048], F32)
        self.ARS = sb("ARSr", [KEXP, BW], F32)
        self.SMX = sb("SMXr", [128, 64], F32)
        self.ABC = sb("ABC", [128, 2 * 512], F32)
        self.SCR = sb("SCRs", [128, 512], F32)
        self.SCR2a = sb("SCR2a", [128, 2 * 512], BF16)
        self.XNW = sb("XNW", [128, NF * BW], BF16)
        self.HW = sb("HW", [128, NF * BW], BF16)
        self.YSQ = sb("YSQ", [128, 2 * 512], BF16)
        self.RCHL0 = sb("RCHL0", [S, 2048], F32)
        self.pb = [self.ctx.enter_context(
            self.nc.psum_tensor(f"pb{i}", [128, 512], F32)) for i in range(8)]
        self.s_ld = self.sem("s_ld")
        self.s_st = self.sem("s_st")
        self.g_ld = self.sem("g_ld")
        self.s_wa = self.sem("s_wa")
        self.s_wb = self.sem("s_wb")
        self.s_w2 = self.sem("s_w2")
        self.s_r = self.sem("s_r")
        self.s_sp = self.sem("s_sp")
        self.s_sz = self.sem("s_sz")
        self.s_sz2 = self.sem("s_sz2")
        self.t_sem = self.sem("t_sem")
        self.v_sem = self.sem("v_sem")
        self.c_sem = self.sem("c_sem")
        self.g_sem = self.sem("g_sem")

    def ph_init(self):
        blk = self.blk

        @blk.gpsimd
        def _(g):
            g.load_library(library_config.proxy)

        @blk.sync
        def _(s):
            bi = s.dma_start(
                out=self.X[:].rearrange("p (c t) -> p c t", c=NF),
                in_=self.i_x.rearrange("(c p) t -> p c t", p=128))
            self.s_ld.inc(bi, 16)
            for ap_in, tile in [
                (self.i_mask, self.MASK), (self.i_triu, self.TRIU),
                (self.i_I128, self.I128s), (self.i_IB128, self.IB128s),
                (self.i_e127, self.E127),
                (self.i_flag, self.FLAG), (self.i_flaginv, self.FLAGI),
                (self.i_misc, self.MISC),
                (self.i_bmup, self.BMUP), (self.i_mup, self.MUP),
                (self.i_b1, self.B1), (self.i_b2, self.B2),
                (self.i_bout, self.BOUT), (self.i_nw, self.NW),
                (self.i_convb, self.CONVB), (self.i_convw, self.CONVW),
                (self.i_Dcol, self.DCOL), (self.i_Alog, self.ALOG),
                (self.i_dtb, self.DTB), (self.i_REPS, self.REPSs),
                (self.i_GSEL, self.GSELs), (self.i_GIND, self.GINDs),
            ]:
                bi = s.dma_start(out=tile[:], in_=ap_in)
                self.s_ld.inc(bi, 16)
        self.init_ld = self.s_ld.n

        @blk.vector
        def _(v):
            self.s_ld.wait(v)
            w = v.memset(self.ONESB[:], 1.0)
            self.v_sem.inc(w)

        @blk.scalar
        def _(sc):
            self.s_ld.wait(sc, self.init_ld)
            w = sc.activation(self.MISC[0:NH, 4:5], self.ALOG[0:NH, 0:1],
                              AF.Exp, bias=0.0, scale=1.0)
            self.c_sem.inc(w)

    def ph_ln_block(self, b, src, win):
        blk = self.blk
        pb = self.pb
        col0 = SBLK[b][0]
        RM, RE, RR, RB_ = 0, 512, 1024, 1536
        self.ln_win_g = {}
        for (po, n) in PIECES[b]:
            a0 = col0 + po

            @blk.vector
            def _(v, a0=a0, po=po, n=n):
                self.v_sem.wait(v)
                self.t_sem.wait(v)
                self.g_sem.wait(v)
                for fc in range(NF):
                    w = v.tensor_copy(win[:, fc * BW + po: fc * BW + po + n],
                                      src[:, fc * T + a0: fc * T + a0 + n])
                self.v_sem.inc(w)
            cast_done = self.v_sem.n

            @blk.tensor
            def _(t, po=po, n=n, cast_done=cast_done):
                self.v_sem.wait(t, cast_done)
                for fc in range(NF):
                    w = t.matmul(pb[6][0:1, 0:n], self.ONESB[:],
                                 win[:, fc * BW + po: fc * BW + po + n],
                                 start=(fc == 0), stop=(fc == NF - 1))
                self.t_sem.inc(w)

            sq_mms = [None] * NF
            for fc in range(NF):
                @blk.vector
                def _(v, fc=fc, po=po, n=n, cast_done=cast_done):
                    if fc == 0:
                        self.v_sem.wait(v, cast_done)
                    elif fc >= 2:
                        self.t_sem.wait(v, sq_mms[fc - 2])
                    w = v.tensor_tensor(self.YSQ[:, (fc % 2) * 512:(fc % 2) * 512 + n],
                                        win[:, fc * BW + po: fc * BW + po + n],
                                        win[:, fc * BW + po: fc * BW + po + n],
                                        OP.mult)
                    self.v_sem.inc(w)
                sq_done = self.v_sem.n

                @blk.tensor
                def _(t, fc=fc, n=n, sq_done=sq_done):
                    self.v_sem.wait(t, sq_done)
                    w = t.matmul(pb[7][0:1, 0:n], self.ONESB[:],
                                 self.YSQ[:, (fc % 2) * 512:(fc % 2) * 512 + n],
                                 start=(fc == 0), stop=(fc == NF - 1))
                    self.t_sem.inc(w)
                sq_mms[fc] = self.t_sem.n
            sq_mm = self.t_sem.n

            @blk.vector
            def _(v, n=n, sq_mm=sq_mm):
                self.t_sem.wait(v, sq_mm)
                w = v.tensor_scalar(self.ROWS[0:1, RM:RM + n],
                                    pb[6][0:1, 0:n], 1.0 / D, None, OP.mult)
                self.v_sem.chain(v, w)
                w = v.tensor_scalar(self.ROWS[0:1, RE:RE + n],
                                    pb[7][0:1, 0:n], 1.0 / D, None, OP.mult)
                self.v_sem.chain(v, w)
                w = v.tensor_tensor(self.ROWS[0:1, RR:RR + n],
                                    self.ROWS[0:1, RM:RM + n],
                                    self.ROWS[0:1, RM:RM + n], OP.mult)
                self.v_sem.chain(v, w)
                w = v.tensor_tensor(self.ROWS[0:1, RE:RE + n],
                                    self.ROWS[0:1, RE:RE + n],
                                    self.ROWS[0:1, RR:RR + n], OP.subtract)
                self.v_sem.inc(w)
            var_done = self.v_sem.n

            @blk.scalar
            def _(sc, n=n, var_done=var_done):
                self.v_sem.wait(sc, var_done)
                w = sc.activation(self.ROWS[0:1, RR:RR + n],
                                  self.ROWS[0:1, RE:RE + n],
                                  AF.Ln, bias=self.MISC[0:1, 0:1], scale=1.0)
                self.c_sem.chain(sc, w)
                w = sc.activation(self.ROWS[0:1, RR:RR + n],
                                  self.ROWS[0:1, RR:RR + n],
                                  AF.Exp, bias=0.0, scale=-0.5)
                self.c_sem.inc(w)
            rstd_done = self.c_sem.n

            @blk.vector
            def _(v, n=n, rstd_done=rstd_done):
                self.c_sem.wait(v, rstd_done)
                w = v.scalar_tensor_tensor(self.ROWS[0:1, RB_:RB_ + n],
                                           self.ROWS[0:1, RM:RM + n], -1.0,
                                           self.ROWS[0:1, RR:RR + n],
                                           OP.mult, OP.mult)
                self.v_sem.inc(w)
            rows_done = self.v_sem.n

            @blk.gpsimd
            def _(g, n=n, rows_done=rows_done):
                self.v_sem.wait(g, rows_done)
                self.g_sem.wait(g)
                g.partition_broadcast(self.ABC[:, 0:n],
                                      self.ROWS[0:1, RR:RR + n], 128)
                w = g.partition_broadcast(self.ABC[:, 512:512 + n],
                                          self.ROWS[0:1, RB_:RB_ + n], 128)
                self.g_sem.inc(w)
            bc_done = self.g_sem.n

            v_base = self.v_sem.n
            g_base = self.g_sem.n

            @blk.vector
            def _(v, a0=a0, po=po, n=n, bc_done=bc_done, g_base=g_base):
                self.g_sem.wait(v, bc_done)
                for fc in range(NF):
                    if fc >= 2:
                        v.wait_ge(self.g_sem.h, g_base + fc - 1)
                    w = v.tensor_tensor(self.SCR2a[:, (fc % 2) * 512:(fc % 2) * 512 + n],
                                        src[:, fc * T + a0: fc * T + a0 + n],
                                        self.ABC[:, 0:n], OP.mult)
                    self.v_sem.inc(w)

            @blk.gpsimd
            def _(g_, a0=a0, po=po, n=n, v_base=v_base):
                for fc in range(NF):
                    g_.wait_ge(self.v_sem.h, v_base + fc + 1)
                    w = g_.tensor_tensor(win[:, fc * BW + po: fc * BW + po + n],
                                         self.SCR2a[:, (fc % 2) * 512:(fc % 2) * 512 + n],
                                         self.ABC[:, 512:512 + n], OP.add)
                    self.g_sem.inc(w)
                    self.ln_win_g[fc] = self.g_sem.n
        self.ln_done = self.v_sem.n
        self.ln_done_g = self.g_sem.n

    def stream_w(self, hbm_tile_ap, buf_slice, sem, prior_t=None):
        blk = self.blk

        @blk.gpsimd
        def _(g):
            if prior_t is not None:
                self.t_sem.wait(g, prior_t)
            w = g.dma_start(out=buf_slice, in_=hbm_tile_ap)
            sem.inc(w, 16)
        return sem.n

    def ph_ffn_block(self, b, scale):
        blk = self.blk
        pb = self.pb
        col0, ncols, _, _ = SBLK[b]
        pieces = PIECES[b]
        xn_ready = self.ln_done

        mm_done = [None] * NF
        ev_done = [None] * NF
        for fo in range(NF):
            buf = (self.WA, self.WB)[fo % 2]
            bsem = (self.s_wa, self.s_wb)[fo % 2]
            ld = self.stream_w(
                self.i_W1[fo], buf[:], bsem,
                prior_t=mm_done[fo - 2] if fo >= 2 else
                getattr(self, ('last_wa_t', 'last_wb_t')[fo % 2], 0))

            @blk.tensor
            def _(t, fo=fo, buf=buf, bsem=bsem, ld=ld):
                bsem.wait(t, ld)
                if fo >= 2:
                    self.v_sem.wait(t, ev_done[fo - 2])
                for pi, (po, n) in enumerate(pieces):
                    tgt = pb[2 * (fo % 2) + pi]
                    for kc in range(NF):
                        if fo == 0 and pi == 0:
                            self.g_sem.wait(t, self.ln_win_g[kc])
                        w = t.matmul(tgt[:, 0:n],
                                     buf[:, kc * 128:(kc + 1) * 128],
                                     self.XNW[:, kc * BW + po: kc * BW + po + n],
                                     start=(kc == 0), stop=(kc == NF - 1))
                self.t_sem.inc(w)
            mm_done[fo] = self.t_sem.n

            @blk.vector
            def _(v, fo=fo):
                self.t_sem.wait(v, mm_done[fo])
                for pi, (po, n) in enumerate(pieces):
                    tgt = pb[2 * (fo % 2) + pi]
                    w = v.tensor_scalar(
                        self.HW[:, fo * BW + po: fo * BW + po + n],
                        tgt[:, 0:n], self.B1[:, fo:fo + 1], 0.0, OP.add, OP.max)
                self.v_sem.inc(w)
            ev_done[fo] = self.v_sem.n
        h_done = self.v_sem.n

        w2_ld = self.stream_w(
            self.i_W2.rearrange("(c p) m -> p c m", p=128),
            self.W2B[:].rearrange("p (c m) -> p c m", c=NF), self.s_w2)

        @blk.tensor
        def _(t):
            self.s_w2.wait(t, w2_ld)
            self.v_sem.wait(t, h_done)
            for pi, (po, n) in enumerate(pieces):
                tgt = (pb[4], pb[6])[pi]
                for kc in range(NF):
                    w = t.matmul(tgt[0:KEXP, 0:n],
                                 self.W2B[:, kc * KEXP:(kc + 1) * KEXP],
                                 self.HW[:, kc * BW + po: kc * BW + po + n],
                                 start=(kc == 0), stop=(kc == NF - 1))
            self.t_sem.inc(w)
        w2mm = self.t_sem.n

        @blk.vector
        def _(v):
            self.t_sem.wait(v, w2mm)
            for pi, (po, n) in enumerate(pieces):
                tgt = (pb[4], pb[6])[pi]
                w = v.tensor_scalar(self.ARS[0:KEXP, po:po + n], tgt[0:KEXP, 0:n],
                                    self.B2[0:KEXP, 0:1], None, OP.add)
            self.v_sem.inc(w)
        ar_done = self.v_sem.n

        nj = (ncols + 127) // 128

        @blk.tensor
        def _(t):
            self.v_sem.wait(t, ar_done)
            for j in range(nj):
                nt = min(128, ncols - j * 128)
                w = t.transpose(pb[5][0:nt, j * 8:(j + 1) * 8],
                                self.ARS[0:KEXP, j * 128:j * 128 + nt],
                                self.I128s[0:KEXP, 0:KEXP])
            self.t_sem.inc(w)
        tf_done = self.t_sem.n

        @blk.vector
        def _(v):
            self.t_sem.wait(v, tf_done)
            for j in range(nj):
                nt = min(128, ncols - j * 128)
                w = v.reduce_max(self.SMX[0:nt, 32 + j:33 + j],
                                 pb[5][0:nt, j * 8:(j + 1) * 8], AXX)
            self.v_sem.chain(v, w)
            for j in range(nj):
                nt = min(128, ncols - j * 128)
                w = v.tensor_scalar(self.SMX[0:nt, 40 + j:41 + j],
                                    self.SMX[0:nt, 32 + j:33 + j],
                                    -1.0, None, OP.mult)
            self.v_sem.inc(w)
        neg_done = self.v_sem.n

        @blk.scalar
        def _(sc):
            self.v_sem.wait(sc, neg_done)
            for j in range(nj):
                nt = min(128, ncols - j * 128)
                w = sc.activation(self.SMX[0:nt, j * 8:(j + 1) * 8],
                                  pb[5][0:nt, j * 8:(j + 1) * 8], AF.Exp,
                                  bias=self.SMX[0:nt, 40 + j:41 + j], scale=1.0)
            self.c_sem.inc(w)
        exp_done = self.c_sem.n

        @blk.vector
        def _(v):
            self.c_sem.wait(v, exp_done)
            for j in range(nj):
                nt = min(128, ncols - j * 128)
                w = v.reduce_sum(self.SMX[0:nt, 48 + j:49 + j],
                                 self.SMX[0:nt, j * 8:(j + 1) * 8], AXX)
            self.v_sem.chain(v, w)
            for j in range(nj):
                nt = min(128, ncols - j * 128)
                w = v.reciprocal(self.SMX[0:nt, 56 + j:57 + j],
                                 self.SMX[0:nt, 48 + j:49 + j])
            self.v_sem.chain(v, w)
            for j in range(nj):
                nt = min(128, ncols - j * 128)
                w = v.tensor_scalar(self.SMX[0:nt, j * 8:(j + 1) * 8],
                                    self.SMX[0:nt, j * 8:(j + 1) * 8],
                                    self.SMX[0:nt, 56 + j:57 + j], None, OP.mult)
            self.v_sem.inc(w)
        at_done = self.v_sem.n

        @blk.tensor
        def _(t):
            self.v_sem.wait(t, at_done)
            for j in range(nj):
                nt = min(128, ncols - j * 128)
                if j < 4:
                    w = t.transpose(pb[7][0:KEXP, j * 128:j * 128 + nt],
                                    self.SMX[0:nt, j * 8:(j + 1) * 8],
                                    self.I128s[0:nt, 0:nt])
                else:
                    w = t.transpose(pb[6][0:KEXP, 128:128 + nt],
                                    self.SMX[0:nt, j * 8:(j + 1) * 8],
                                    self.I128s[0:nt, 0:nt])
            self.t_sem.inc(w)
        tb_done = self.t_sem.n

        @blk.vector
        def _(v):
            self.t_sem.wait(v, tb_done)
            w = v.tensor_copy(self.ARS[0:KEXP, 0:min(ncols, 512)],
                              pb[7][0:KEXP, 0:min(ncols, 512)])
            if ncols > 512:
                self.v_sem.chain(v, w)
                w = v.tensor_copy(self.ARS[0:KEXP, 512:ncols],
                                  pb[6][0:KEXP, 128:128 + ncols - 512])
            self.v_sem.inc(w)
        alpha_done = self.v_sem.n

        vx_mms = []
        vx_evs = []
        for rc in range(NKR):
            buf = (self.WA, self.WB)[rc % 2]
            bsem = (self.s_wa, self.s_wb)[rc % 2]
            ld = self.stream_w(
                self.i_Vc[rc], buf[:], bsem,
                prior_t=vx_mms[rc - 2] if rc >= 2 else mm_done[NF - 2 + rc])

            @blk.tensor
            def _(t, rc=rc, buf=buf, bsem=bsem, ld=ld):
                bsem.wait(t, ld)
                if rc == 0:
                    self.v_sem.wait(t, h_done)
                if rc >= 2:
                    self.v_sem.wait(t, vx_evs[rc - 2])
                for pi, (po, n) in enumerate(pieces):
                    tgt = pb[2 * (rc % 2) + pi]
                    for kc in range(NF):
                        w = t.matmul(tgt[:, 0:n],
                                     buf[:, kc * 128:(kc + 1) * 128],
                                     self.XNW[:, kc * BW + po: kc * BW + po + n],
                                     start=(kc == 0), stop=(kc == NF - 1))
                self.t_sem.inc(w)
            vx_mms.append(self.t_sem.n)

            for pi, (po, n) in enumerate(pieces):
                @blk.tensor
                def _(t, rc=rc, po=po, n=n, pi=pi):
                    if rc == 0 and pi == 0:
                        self.v_sem.wait(t, alpha_done)
                    else:
                        self.v_sem.wait(t)
                    w = t.matmul(pb[4][:, 0:n],
                                 self.REPSs[0:KEXP, rc * 128:(rc + 1) * 128],
                                 self.ARS[0:KEXP, po:po + n],
                                 start=True, stop=True)
                    self.t_sem.inc(w)
                ab_mm = self.t_sem.n

                @blk.vector
                def _(v, rc=rc, po=po, n=n, pi=pi, ab_mm=ab_mm):
                    self.t_sem.wait(v, ab_mm)
                    w = v.tensor_copy(self.SCR[:, 0:n], pb[4][:, 0:n])
                    self.v_sem.chain(v, w)
                    tgt = pb[2 * (rc % 2) + pi]
                    w = v.tensor_tensor(
                        self.VXW[:, rc * BW + po: rc * BW + po + n],
                        tgt[:, 0:n], self.SCR[:, 0:n], OP.mult)
                    self.v_sem.inc(w)
            vx_evs.append(self.v_sem.n)
        vxs_done = self.v_sem.n

        s4_mm = [None] * NF
        s4_ev = [None] * NF
        for fo in range(NF):
            buf = (self.WA, self.WB)[fo % 2]
            bsem = (self.s_wa, self.s_wb)[fo % 2]
            ld = self.stream_w(
                self.i_Uc[fo], buf[:, 0:NKR * 128], bsem,
                prior_t=s4_mm[fo - 2] if fo >= 2 else vx_mms[NKR - 2 + fo])

            @blk.tensor
            def _(t, fo=fo, buf=buf, bsem=bsem, ld=ld):
                bsem.wait(t, ld)
                if fo == 0:
                    self.v_sem.wait(t, vxs_done)
                if fo >= 2:
                    self.v_sem.wait(t, s4_ev[fo - 2])
                for pi, (po, n) in enumerate(pieces):
                    tgt = pb[2 * (fo % 2) + pi]
                    for rc in range(NKR):
                        w = t.matmul(tgt[:, 0:n],
                                     buf[:, rc * 128:(rc + 1) * 128],
                                     self.VXW[:, rc * BW + po: rc * BW + po + n],
                                     start=(rc == 0), stop=(rc == NKR - 1))
                self.t_sem.inc(w)
            s4_mm[fo] = self.t_sem.n

            @blk.vector
            def _(v, fo=fo):
                self.t_sem.wait(v, s4_mm[fo])
                for pi, (po, n) in enumerate(pieces):
                    a0 = col0 + po
                    tgt = pb[2 * (fo % 2) + pi]
                    w = v.scalar_tensor_tensor(
                        self.X[:, fo * T + a0: fo * T + a0 + n],
                        tgt[:, 0:n], scale,
                        self.X[:, fo * T + a0: fo * T + a0 + n],
                        OP.mult, OP.add)
                self.v_sem.inc(w)
            s4_ev[fo] = self.v_sem.n
        self.last_wa_t = s4_mm[NF - 2]
        self.last_wb_t = s4_mm[NF - 1]

    def ph_win_block(self, b):
        blk = self.blk
        pb = self.pb
        col0, ncols, own0, nown = SBLK[b]
        pieces = PIECES[b]
        xn_ready = self.ln_done
        wi_mm = [None] * NFO
        wi_ev = [None] * NFO
        for wo in range(NFO):
            buf = (self.WA, self.WB)[wo % 2]
            bsem = (self.s_wa, self.s_wb)[wo % 2]
            ld = self.stream_w(
                self.i_Win[wo], buf[:], bsem,
                prior_t=wi_mm[wo - 2] if wo >= 2 else
                (self.last_wa_t, self.last_wb_t)[wo % 2])

            @blk.tensor
            def _(t, wo=wo, buf=buf, bsem=bsem, ld=ld):
                bsem.wait(t, ld)
                if wo >= 2 and wi_ev[wo - 2] is not None:
                    vv, cc_ = wi_ev[wo - 2]
                    self.v_sem.wait(t, vv)
                    self.c_sem.wait(t, cc_)
                for pi, (po, n) in enumerate(pieces):
                    tgt = pb[2 * (wo % 2) + pi]
                    for kc in range(NF):
                        if wo == 0 and pi == 0:
                            self.g_sem.wait(t, self.ln_win_g[kc])
                        w = t.matmul(tgt[:, 0:n],
                                     buf[:, kc * 128:(kc + 1) * 128],
                                     self.XNW[:, kc * BW + po: kc * BW + po + n],
                                     start=(kc == 0), stop=(kc == NF - 1))
                self.t_sem.inc(w)
            wi_mm[wo] = self.t_sem.n
            self.win_evac(b, wo, pieces, col0, own0, wi_mm[wo])
            wi_ev[wo] = (self.v_sem.n, self.c_sem.n)
        self.last_wa_t = wi_mm[NFO - 1 - (NFO - 1) % 2]
        self.last_wb_t = wi_mm[NFO - 1 - (NFO % 2)]

    def win_evac(self, b, wo, pieces, col0, own0, mm):
        blk = self.blk
        pb = self.pb
        if wo < NF:
            @blk.scalar
            def _(sc, wo=wo):
                self.t_sem.wait(sc, mm)
                for pi, (po, n) in enumerate(pieces):
                    a0 = col0 + po
                    so, o0, ln = owned_map(a0, n)
                    w = sc.activation(
                        self.SILZB[:, wo * 512 + (o0 - own0):
                                   wo * 512 + (o0 - own0) + ln],
                        pb[2 * (wo % 2) + pi][:, so:so + ln],
                        AF.Silu, bias=self.BMUP[:, wo:wo + 1],
                        scale=self.MUP[:, wo:wo + 1])
                self.c_sem.inc(w)
        elif wo == NFO - 1:
            @blk.vector
            def _(v, wo=wo):
                self.t_sem.wait(v, mm)
                for pi, (po, n) in enumerate(pieces):
                    a0 = col0 + po
                    so, o0, ln = owned_map(a0, n)
                    w = v.tensor_scalar(self.DTLD[0:NH, o0:o0 + ln],
                                        pb[2 * (wo % 2) + pi][0:NH, so:so + ln],
                                        self.MUP[0:NH, wo:wo + 1],
                                        self.BMUP[0:NH, wo:wo + 1],
                                        OP.mult, OP.add)
                self.v_sem.inc(w)
        else:
            cc = wo - NF
            for pi, (po, n) in enumerate(pieces):
                a0 = col0 + po
                tgt = pb[2 * (wo % 2) + pi]
                so, o0, ln = owned_map(a0, n)
                sbase = o0 - a0 + CTX

                @blk.vector
                def _(v, cc=cc, wo=wo, a0=a0, n=n, tgt=tgt, so=so, o0=o0,
                      ln=ln, sbase=sbase):
                    self.t_sem.wait(v, mm)
                    self.c_sem.wait(v)
                    if a0 > 0:
                        w = v.tensor_copy(self.CSTR[:, 0:CTX],
                                          self.CARR[:, cc * 4:cc * 4 + CTX])
                        self.v_sem.chain(v, w)
                    w = v.tensor_scalar(self.CSTR[:, CTX:CTX + n], tgt[:, 0:n],
                                        self.MUP[:, wo:wo + 1],
                                        self.BMUP[:, wo:wo + 1], OP.mult, OP.add)
                    self.v_sem.chain(v, w)
                    if a0 == 0:
                        w = v.tensor_scalar(self.CSTR[:, CTX:2 * CTX],
                                            self.CSTR[:, CTX:2 * CTX],
                                            self.FLAG[:], None, OP.mult)
                        self.v_sem.chain(v, w)
                    w = v.tensor_copy(self.CARR[:, cc * 4:cc * 4 + CTX],
                                      self.CSTR[:, n:n + CTX])
                    self.v_sem.chain(v, w)
                    w = v.tensor_scalar(self.CACC[:, 0:ln],
                                        self.CSTR[:, sbase:sbase + ln],
                                        self.CONVW[:, cc * 4:cc * 4 + 1], None,
                                        OP.mult)
                    self.v_sem.chain(v, w)
                    for k in range(1, 4):
                        w = v.scalar_tensor_tensor(
                            self.CACC[:, 0:ln],
                            self.CSTR[:, sbase + k:sbase + k + ln],
                            self.CONVW[:, cc * 4 + k:cc * 4 + k + 1],
                            self.CACC[:, 0:ln], OP.mult, OP.add)
                        self.v_sem.chain(v, w)
                vdone = self.v_sem.n

                @blk.scalar
                def _(sc, cc=cc, o0=o0, ln=ln, vdone=vdone, own0=own0):
                    self.v_sem.wait(sc, vdone)
                    if cc < NF:
                        dst = self.HW[:, cc * BW + (o0 - own0):
                                      cc * BW + (o0 - own0) + ln]
                    elif cc < 18:
                        dst = self.BTW[:, (cc - NF) * 512 + (o0 - own0):
                                       (cc - NF) * 512 + (o0 - own0) + ln]
                    else:
                        dst = self.CT[:, (cc - 18) * TOWN + o0:
                                      (cc - 18) * TOWN + o0 + ln]
                    w = sc.activation(dst, self.CACC[:, 0:ln], AF.Silu,
                                      bias=self.CONVB[:, cc:cc + 1], scale=1.0)
                    self.c_sem.inc(w)

    def ph_dt_block(self, b):
        blk = self.blk
        col0, ncols, own0, nown = SBLK[b]

        @blk.vector
        def _(v):
            self.c_sem.wait(v)
            self.v_sem.wait(v)
            w = v.tensor_scalar(self.SCR[0:NH, 0:nown],
                                self.DTLD[0:NH, own0:own0 + nown],
                                self.DTB[0:NH, 0:1], 80.0, OP.add, OP.min)
            self.v_sem.inc(w)
        sp1 = self.v_sem.n

        @blk.scalar
        def _(sc):
            self.v_sem.wait(sc, sp1)
            w = sc.activation(self.ABC[0:NH, 0:nown], self.SCR[0:NH, 0:nown],
                              AF.Exp, bias=0.0, scale=1.0)
            self.c_sem.chain(sc, w)
            w = sc.activation(self.DTLD[0:NH, own0:own0 + nown],
                              self.ABC[0:NH, 0:nown],
                              AF.Ln, bias=self.MISC[0:NH, 1:2], scale=1.0)
            self.c_sem.inc(w)
        self.ld_done = self.c_sem.n

    def ph_ssd_chunk(self, b, c):
        blk = self.blk
        pb = self.pb
        col0, ncols, own0, nown = SBLK[b]
        j0 = c * 128 - own0
        c_base = self.c_sem.n
        dec_pred = c_base + 1
        ers_pred = c_base + 2
        wg2_pred = c_base + 3
        e_pred = lambda g, hp: c_base + 3 + 4 * g + hp + 1

        @blk.vector
        def _(v):
            self.c_sem.wait(v, self.ld_done)
            self.v_sem.wait(v)
            w = v.tensor_scalar(self.LDC[0:NH, 0:128],
                                self.DTLD[0:NH, c * 128:c * 128 + 128],
                                self.MISC[0:NH, 4:5], -1.0, OP.mult, OP.mult)
            self.v_sem.inc(w)
        ldc_done = self.v_sem.n

        @blk.tensor
        def _(t):
            self.v_sem.wait(t, ldc_done)
            t.transpose(pb[4][:, 0:S], self.DTLD[0:NH, c * 128:c * 128 + 128],
                        self.I128s[0:NH, 0:NH])
            w = t.transpose(pb[4][:, S:2 * S], self.LDC[0:NH, 0:128],
                            self.I128s[0:NH, 0:NH])
            self.t_sem.inc(w)
        tt1 = self.t_sem.n

        @blk.vector
        def _(v):
            self.t_sem.wait(v, tt1)
            w = v.tensor_copy(self.DTOK[:, 0:64], pb[4][:, 0:64])
            self.v_sem.inc(w)
        dtok_done = self.v_sem.n

        @blk.tensor
        def _(t):
            self.v_sem.wait(t, dtok_done)
            t.matmul(pb[5][:, 0:S], self.TRIU[:],
                     self.DTOK[:, S:64],
                     start=True, stop=True)
            w = t.matmul(pb[5][0:S, 64:64 + 128],
                         self.DTOK[:, S:64],
                         self.TRIU[:], start=True, stop=True)
            self.t_sem.inc(w)
        rmm = self.t_sem.n

        @blk.vector
        def _(v):
            self.t_sem.wait(v, rmm)
            w = v.tensor_scalar(self.NRT[:, 0:S], pb[5][:, 0:S],
                                -1.0, None, OP.mult)
            self.v_sem.chain(v, w)
            w = v.tensor_copy(self.RFMs[0:S, 0:Q], pb[5][0:S, 64:64 + 128])
            self.v_sem.inc(w)
        rev_done = self.v_sem.n

        @blk.tensor
        def _(t):
            self.v_sem.wait(t, rev_done)
            w = t.matmul(pb[4][0:1, 64:64 + S], self.E127[:],
                         self.NRT[:, 0:S], start=True, stop=True)
            self.t_sem.inc(w)
        nl_mm = self.t_sem.n

        @blk.vector
        def _(v):
            self.t_sem.wait(v, nl_mm)
            w = v.tensor_copy(self.NLR[0:1, 0:S], pb[4][0:1, 64:64 + S])
            self.v_sem.inc(w)
        nl_done = self.v_sem.n

        @blk.sync
        def _(s):
            self.c_sem.wait(s, ers_pred)
            w = s.dma_start(
                out=self.h_r[c:c + 1, :].rearrange("r (s q) -> (r s) q", s=S),
                in_=self.RFMs[0:S, 0:Q])
            self.s_r.inc(w, 16)
            w = s.dma_start(
                out=self.h_er[c:c + 1, :].rearrange("r (s q) -> (r s) q", s=S),
                in_=self.ERS[0:S, 0:Q])
            self.s_r.inc(w, 16)
        rbounce_done = self.s_r.n

        @blk.gpsimd
        def _(g):
            self.c_sem.wait(g, dec_pred)
            self.v_sem.wait(g)
            w = g.partition_broadcast(self.ABC[:, 512:512 + S],
                                      self.NLR[0:1, 0:S], 128)
            self.g_sem.inc(w)
        nlb_done = self.g_sem.n

        @blk.vector
        def _(v):
            self.g_sem.wait(v, nlb_done)
            w = v.tensor_tensor(self.SCR[:, 0:S], self.NRT[:, 0:S],
                                self.ABC[:, 512:512 + S], OP.subtract)
            self.v_sem.inc(w)
        wg1 = self.v_sem.n

        @blk.vector
        def _(v):
            self.c_sem.wait(v, wg2_pred)
            w = v.tensor_tensor(self.WGT[:, 0:NH],
                                self.SCR[:, S:2 * S],
                                self.DTOK[:, 0:S], OP.mult)
            self.v_sem.inc(w)
        wgt_done = self.v_sem.n

        xb = (c % 2) * D
        for jj in range(4):
            @blk.tensor
            def _(t, jj=jj):
                if jj == 0:
                    self.c_sem.wait(t)
                    self.v_sem.wait(t)
                else:
                    self.v_sem.wait(t)
                pbv = pb[4][:, 0:256].bitcast(BF16)
                for fc in range(4 * jj, 4 * jj + 4):
                    w = t.transpose(pbv[:, (fc % 4) * 128:(fc % 4) * 128 + 128],
                                    self.HW[:, fc * BW + j0: fc * BW + j0 + 128],
                                    self.IB128s[:])
                self.t_sem.inc(w)
            xt_mm = self.t_sem.n

            @blk.vector
            def _(v, jj=jj, xt_mm=xt_mm):
                self.t_sem.wait(v, xt_mm)
                w = v.tensor_copy(self.XTOK[:, xb + jj * 512: xb + (jj + 1) * 512],
                                  pb[4][:, 0:256].bitcast(BF16))
                self.v_sem.inc(w)
        xtok_done = self.v_sem.n

        st_bank = pb[6] if c % 2 == 0 else pb[3]
        bw_prev = None
        mt_last = {}
        gt_mms = {}
        rbm_dones = {}
        mtds = {}
        vcur0, ccur0 = self.v_sem.n, self.c_sem.n
        for g in range(G):
            i, gq = g % 4, g // 4
            gt_bank = pb[5] if g % 2 == 0 else pb[2]
            y_bank = pb[7] if g % 2 == 0 else pb[0]

            @blk.tensor
            def _(t, g=g, i=i, gq=gq, bw_prev=bw_prev):
                if g == 0:
                    self.v_sem.wait(t, vcur0)
                    self.c_sem.wait(t, ccur0)
                else:
                    self.v_sem.wait(t, bw_prev)
                w = t.transpose(pb[1][:, 0:S // 2].bitcast(BF16),
                                self.BTW[32 * i:32 * i + 32,
                                         gq * 512 + j0: gq * 512 + j0 + 128],
                                self.IB128s[32 * i:32 * i + 32, 32 * i:32 * i + 32],
                                tile_position=(32 * i, 0))
                self.t_sem.inc(w)
            bt_mm = self.t_sem.n

            @blk.vector
            def _(v, g=g, bt_mm=bt_mm):
                self.t_sem.wait(v, bt_mm)
                self.v_sem.wait(v, wgt_done)
                w = v.tensor_tensor(
                    self.BW4[:].rearrange("p (a f) -> p a f", a=4),
                    pb[1][:, 0:S // 2].bitcast(BF16)
                        .unsqueeze(1).broadcast_to([128, 4, S]),
                    self.WGT[:, 4 * g:4 * g + 4]
                        .unsqueeze(2).broadcast_to([128, 4, S]),
                    OP.mult)
                self.v_sem.inc(w)
            bw_done = self.v_sem.n
            bw_prev = bw_done

            @blk.tensor
            def _(t, g=g, i=i, gq=gq, gt_bank=gt_bank):
                if g >= 2 and mt_last.get(g - 2):
                    self.v_sem.wait(t, mt_last[g - 2])
                w = t.matmul(gt_bank[:, 0:128],
                             self.BTW[32 * i:32 * i + 32,
                                      gq * 512 + j0: gq * 512 + j0 + 128],
                             self.CT[32 * i:32 * i + 32,
                                     gq * TOWN + c * 128: gq * TOWN + c * 128 + 128],
                             start=True, stop=True,
                             tile_position=(32 * i, 0))
                self.t_sem.inc(w)
            gt_mm = self.t_sem.n
            gt_mms[g] = gt_mm

            if g % 4 == 0:
                @blk.sync
                def _(s, g=g, rbounce_done=rbounce_done):
                    self.s_r.wait(s, rbounce_done)
                    self.g_sem.wait(s)
                    w = s.dma_start(
                        out=self.RCHL0[0:1, :],
                        in_=self.h_r[c:c + 1, (g // 4) * 2048:
                                     (g // 4) * 2048 + 2048])
                    self.s_r.inc(w, 16)
                self.rq_ld = self.s_r.n

            @blk.gpsimd
            def _(g_, g=g):
                g_.wait_ge(self.s_r.h, self.rq_ld)
                self.c_sem.wait(g_, c_base + 3 + 4 * g)
                w = g_.partition_broadcast(
                    self.RB4[:],
                    self.RCHL0[0:1, (g % 4) * 512:(g % 4) * 512 + 512], 128)
                self.g_sem.inc(w)
            rb_done = self.g_sem.n

            @blk.vector
            def _(v, g=g, rb_done=rb_done):
                self.g_sem.wait(v, rb_done)
                w = v.tensor_tensor(
                    self.RB4[:].rearrange("p (a f) -> p a f", a=4),
                    self.MASK[:].unsqueeze(1).broadcast_to([128, 4, 128]),
                    self.RB4[:].rearrange("p (a f) -> p a f", a=4),
                    OP.add)
                self.v_sem.inc(w)
            rbm_done = self.v_sem.n
            rbm_dones[g] = rbm_done

            mtd = []
            for hp in range(4):
                h = 4 * g + hp

                @blk.vector
                def _(v, g=g, hp=hp, h=h, gt_bank=gt_bank,
                      gt_mm=gt_mm):
                    self.c_sem.wait(v, e_pred(g, hp))
                    self.t_sem.wait(v, gt_mm)
                    w = v.scalar_tensor_tensor(
                        self.MT[:, hp * 128:(hp + 1) * 128],
                        self.ABC[:, (hp % 2) * 128:(hp % 2) * 128 + 128],
                        self.DTOK[:, h:h + 1],
                        gt_bank[:, 0:128], OP.mult, OP.mult)
                    self.v_sem.inc(w)
                mt_done = self.v_sem.n
                mtd.append(mt_done)
                mtds[(g, hp)] = mt_done

                @blk.tensor
                def _(t, g=g, i=i, gq=gq, hp=hp, h=h, mt_done=mt_done, xb=xb,
                      y_bank=y_bank):
                    self.v_sem.wait(t, mt_done)
                    self.v_sem.wait(t, xtok_done)
                    pr = hp // 2
                    ph = h % 2
                    t.matmul(y_bank[64 * ph:64 * ph + 64, pr * 128:pr * 128 + 128],
                             self.XTOK[:, xb + h * 64: xb + h * 64 + 64],
                             self.MT[:, hp * 128:(hp + 1) * 128],
                             start=True, stop=True,
                             tile_position=(0, 64 * ph))
                    w = t.matmul(
                        st_bank[32 * i:32 * i + 32,
                                gq * 256 + hp * 64: gq * 256 + hp * 64 + 64],
                        self.BW4[:, hp * S:(hp + 1) * S],
                        self.XTOK[:, xb + h * 64: xb + h * 64 + 64],
                        start=True, stop=True,
                        tile_position=(0, 32 * i))
                    self.t_sem.inc(w)
            mt_last[g] = self.v_sem.n
            y_mm = self.t_sem.n

            @blk.vector
            def _(v, g=g, c=c, j0=j0, y_mm=y_mm, y_bank=y_bank):
                self.t_sem.wait(v, y_mm)
                if g == 0 and c % 4 == 0:
                    self.s_sp.wait(v)
                for pr in range(2):
                    p_ = 2 * g + pr
                    w = v.scalar_tensor_tensor(
                        self.SCR[:, 0:128],
                        self.HW[:, p_ * BW + j0: p_ * BW + j0 + 128],
                        self.DCOL[:, p_:p_ + 1],
                        y_bank[:, pr * 128:pr * 128 + 128], OP.mult, OP.add)
                    self.v_sem.chain(v, w)
                    w = v.tensor_tensor(
                        self.Y1B[:, p_ * 512 + j0: p_ * 512 + j0 + 128],
                        self.SCR[:, 0:128],
                        self.SILZB[:, p_ * 512 + j0: p_ * 512 + j0 + 128],
                        OP.mult)
                    self.v_sem.chain(v, w)

        @blk.scalar
        def _(sc):
            self.v_sem.wait(sc, nl_done)
            w = sc.activation(self.DECROW[0:1, c * NH:(c + 1) * NH],
                              self.NLR[0:1, 0:S], AF.Exp, bias=0.0, scale=-1.0)
            self.c_sem.inc(w)
            self.v_sem.wait(sc, rev_done)
            w = sc.activation(self.ERS[0:S, 0:Q], self.RFMs[0:S, 0:Q],
                              AF.Exp, bias=0.0, scale=1.0)
            self.c_sem.inc(w)
            self.v_sem.wait(sc, wg1)
            w = sc.activation(self.SCR[:, S:2 * S], self.SCR[:, 0:S],
                              AF.Exp, bias=0.0, scale=1.0)
            self.c_sem.inc(w)
            for g in range(G):
                for hp in range(4):
                    h = 4 * g + hp
                    if hp == 0:
                        self.v_sem.wait(sc, rbm_dones[g])
                    if hp >= 2:
                        self.v_sem.wait(sc, mtds[(g, hp - 2)])
                    w = sc.activation(
                        self.ABC[:, (hp % 2) * 128:(hp % 2) * 128 + 128],
                        self.RB4[:, hp * 128:(hp + 1) * 128],
                        AF.Exp, bias=self.NRT[:, h:h + 1], scale=1.0)
                    self.c_sem.inc(w)
        assert self.c_sem.n == c_base + 3 + 4 * G

        all_tmm = self.t_sem.n

        dtt_done = self.build_dtt(c)

        @blk.vector
        def _(v, c=c, dtt_done=dtt_done):
            self.v_sem.wait(v, dtt_done)
            self.t_sem.wait(v, all_tmm)
            if c == 0:
                w = v.tensor_copy(self.SST[:, 0:512], st_bank[:, 0:512])
            else:
                w = v.tensor_tensor(self.SCR[:, 0:512],
                                    self.SST[:, (c - 1) * 512: c * 512],
                                    self.DTT[:], OP.mult)
                self.v_sem.chain(v, w)
                w = v.tensor_tensor(self.SST[:, c * 512:(c + 1) * 512],
                                    self.SCR[:, 0:512], st_bank[:, 0:512], OP.add)
            self.v_sem.inc(w)

    def build_dtt(self, c):
        blk = self.blk

        @blk.gpsimd
        def _(g_):
            self.c_sem.wait(g_)
            prev = getattr(self, "_dtt_prev", None)
            if prev is not None:
                self.v_sem.wait(g_, prev)
            else:
                self.v_sem.wait(g_)
                self.g_sem.wait(g_)
            w = g_.partition_broadcast(self.SCR[:, 128:128 + NH],
                                       self.DECROW[0:1, c * NH:(c + 1) * NH], 128)
            self.g_sem.inc(w)
        d128 = self.g_sem.n

        @blk.vector
        def _(v):
            self.g_sem.wait(v, d128)
            base = self.SCR[:]
            for i in range(4):
                src_ap = bass.AP(base.tensor, 32 * i * 512 + 4 * i + 128,
                                 [[512, 32], [16, 2], [1, 4], [0, 64]])
                w = v.tensor_copy(
                    self.DTT[32 * i:32 * i + 32, :]
                        .rearrange("p (a b f) -> p a b f", a=2, b=4),
                    src_ap)
                self.v_sem.chain(v, w)
        self._dtt_prev = self.v_sem.n
        return self.v_sem.n

    def ph_spill_block(self, b):
        blk = self.blk
        col0, ncols, own0, nown = SBLK[b]
        vn, cn = self.v_sem.n, self.c_sem.n

        @blk.sync
        def _(s):
            self.v_sem.wait(s, vn)
            self.c_sem.wait(s, cn)
            w = s.dma_start(
                out=self.h_silz.rearrange("(c p) t -> p c t", p=128)
                    [:, :, own0:own0 + 512],
                in_=self.SILZB[:].rearrange("p (c t) -> p c t", c=NF))
            self.s_sp.inc(w, 16)
            w = s.dma_start(
                out=self.h_y1.rearrange("(c p) t -> p c t", p=128)
                    [:, :, own0:own0 + 512],
                in_=self.Y1B[:].rearrange("p (c t) -> p c t", c=NF))
            self.s_sp.inc(w, 16)

    def ph_y1_load(self, b):
        blk = self.blk
        col0, ncols, own0, nown = SBLK[b]
        vn = self.v_sem.n

        @blk.sync
        def _(s):
            self.v_sem.wait(s, vn)
            w = s.dma_start(
                out=self.Y1B[:].rearrange("p (c t) -> p c t", c=NF),
                in_=self.h_y1.rearrange("(c p) t -> p c t", p=128)
                    [:, :, own0:own0 + 512])
            self.s_sp.inc(w, 16)
        self.y1_ld = self.s_sp.n

    def ph_state_collective(self):
        blk = self.blk

        @blk.vector
        def _(v):
            self.v_sem.wait(v)
            w = v.tensor_scalar(self.SCR[:, 0:512], self.SST[:, 7 * 512:8 * 512],
                                self.FLAGI[:], None, OP.mult)
            self.v_sem.inc(w)
        contrib = self.v_sem.n

        @blk.scalar
        def _(sc):
            self.v_sem.wait(sc, contrib)
            w = sc.dma_start(out=self.h_ccin, in_=self.SCR[:, 0:512])
            self.s_r.inc(w, 16)
        ccw = self.s_r.n

        @blk.gpsimd
        def _(g_):
            g_.wait_ge(self.s_r.h, ccw)
            w = g_.collective_compute(
                "AllReduce", OP.add,
                replica_groups=[[0, 1], [2, 3], [4, 5], [6, 7]],
                ins=[self.h_ccin], outs=[self.h_ccout])
            self.g_sem.inc(w)
        cc_done = self.g_sem.n

        self.cc_done = cc_done

    def ph_cc_recv(self):
        blk = self.blk

        @blk.scalar
        def _(sc):
            self.g_sem.wait(sc, self.cc_done)
            w = sc.dma_start(out=self.SCAR[:], in_=self.h_ccout)
            self.s_r.inc(w, 16)
        scar_ld = self.s_r.n

        @blk.vector
        def _(v):
            v.wait_ge(self.s_r.h, scar_ld)
            w = v.tensor_scalar(self.SCAR[:], self.SCAR[:], self.FLAG[:],
                                None, OP.mult)
            self.v_sem.inc(w)

    def ph_epi_chunk(self, b, c):
        blk = self.blk
        pb = self.pb
        own0 = SBLK[b][2]
        j0 = c * 128 - own0
        if not hasattr(self, "_epi_cdt_prev"):
            self._epi_cdt_prev = [None, None]
            self._epi_yi_prev = [None, None]
            self._epi_chunk_yi = [None, None]
            self._epi_ev_g3 = None
            self._epi_ev_chunk = None
            self._epi_pending = []

        silzc = self.SILZC
        sused = (self.SUSED, self.SUSED2)[c % 2]

        @blk.sync
        def _(s, c=c, silzc=silzc):
            prev = self._epi_ev_g3
            if prev is not None:
                self.v_sem.wait(s, prev)
            elif c == 0:
                self.v_sem.wait(s)
            if c == 4:
                self.t_sem.wait(s)
            w = s.dma_start(
                out=silzc[:, 0:8 * 128].rearrange("p (c t) -> p c t", c=8),
                in_=self.h_silz.rearrange("(c p) t -> p c t", p=128)
                    [:, 0:8, c * 128:c * 128 + 128])
            self.s_sz.inc(w, 16)
        silz_lo = self.s_sz.n
        silz_hi = None

        dtt_done = self.build_dtt(c)

        @blk.vector
        def _(v, c=c, dtt_done=dtt_done, sused=sused):
            self.v_sem.wait(v, dtt_done)
            prev_yi = self._epi_chunk_yi[c % 2]
            if prev_yi is not None:
                self.t_sem.wait(v, prev_yi)
            if c == 0:
                w = v.tensor_copy(sused[:], self.SCAR[:])
            else:
                w = v.tensor_tensor(sused[:], self.SCAR[:],
                                    self.SST[:, (c - 1) * 512: c * 512], OP.add)
            self.v_sem.chain(v, w)
            w = v.tensor_tensor(self.SCAR[:], self.SCAR[:], self.DTT[:], OP.mult)
            self.v_sem.inc(w)
        sused_done = self.v_sem.n

        YROT = (pb[7], pb[0], pb[1], pb[2])
        for g in range(G):
            i, gq = g % 4, g // 4

            if g == 2:
                @blk.sync
                def _(s, c=c, silzc=silzc):
                    prev = self._epi_ev_chunk
                    if prev is not None:
                        self.v_sem.wait(s, prev)
                    elif c == 0:
                        self.v_sem.wait(s)
                    if c == 4:
                        self.t_sem.wait(s)
                    w = s.dma_start(
                        out=silzc[:, 8 * 128:16 * 128]
                            .rearrange("p (c t) -> p c t", c=8),
                        in_=self.h_silz.rearrange("(c p) t -> p c t", p=128)
                            [:, 8:16, c * 128:c * 128 + 128])
                    self.s_sz2.inc(w, 16)
                silz_hi = self.s_sz2.n

            if g % 4 == 0:
                @blk.scalar
                def _(sc, g=g, c=c):
                    self.g_sem.wait(sc)
                    w = sc.dma_start(
                        out=self.RCHL0[0:1, :],
                        in_=self.h_er[c:c + 1, (g // 4) * 2048:
                                      (g // 4) * 2048 + 2048])
                    self.s_r.inc(w, 16)
                self.rq_ld = self.s_r.n

            ebt_b = (self.EBT, self.EBT2)[g % 2]
            cdt_b = (self.CDT, self.CDT2)[g % 2]

            @blk.gpsimd
            def _(g_, g=g, ebt_b=ebt_b):
                g_.wait_ge(self.s_r.h, self.rq_ld)
                prev = self._epi_cdt_prev[g % 2]
                if prev is None:
                    self.v_sem.wait(g_)
                else:
                    self.v_sem.wait(g_, prev)
                w = g_.partition_broadcast(
                    ebt_b[:],
                    self.RCHL0[0:1, (g % 4) * 512:(g % 4) * 512 + 512], 128)
                self.g_sem.inc(w)
            eb_done = self.g_sem.n

            @blk.vector
            def _(v, g=g, i=i, gq=gq, c=c, eb_done=eb_done, ebt_b=ebt_b,
                  cdt_b=cdt_b):
                self.g_sem.wait(v, eb_done)
                prev_yi = self._epi_yi_prev[g % 2]
                if prev_yi is not None:
                    self.t_sem.wait(v, prev_yi)
                w = v.tensor_tensor(
                    cdt_b[32 * i:32 * i + 32, :]
                        .rearrange("p (a f) -> p a f", a=4),
                    self.CT[32 * i:32 * i + 32,
                            gq * TOWN + c * 128: gq * TOWN + c * 128 + 128]
                        .unsqueeze(1).broadcast_to([32, 4, 128]),
                    ebt_b[32 * i:32 * i + 32, :]
                        .rearrange("p (a f) -> p a f", a=4),
                    OP.mult)
                self.v_sem.inc(w)
            cdt_done = self.v_sem.n
            self._epi_cdt_prev[g % 2] = cdt_done

            ye_bank = YROT[g % 4]

            @blk.tensor
            def _(t, g=g, i=i, gq=gq, cdt_done=cdt_done, ye_bank=ye_bank,
                  cdt_b=cdt_b, sused=sused):
                self.v_sem.wait(t, cdt_done)
                self.v_sem.wait(t, sused_done)
                for hp in range(4):
                    h = 4 * g + hp
                    pr, ph = hp // 2, h % 2
                    w = t.matmul(
                        ye_bank[64 * ph:64 * ph + 64, pr * 128:pr * 128 + 128],
                        sused[32 * i:32 * i + 32,
                              gq * 256 + hp * 64: gq * 256 + hp * 64 + 64],
                        cdt_b[32 * i:32 * i + 32, hp * 128:(hp + 1) * 128],
                        start=True, stop=True,
                        tile_position=(32 * i, 64 * ph))
                self.t_sem.inc(w)
            yi_mm = self.t_sem.n
            self._epi_yi_prev[g % 2] = yi_mm
            if g == G - 1:
                self._epi_chunk_yi[c % 2] = yi_mm

            def emit_evac(g=g, c=c, j0=j0, yi_mm=yi_mm,
                          silz_ld=(silz_lo if g < 4 else silz_hi),
                          ye_bank=ye_bank, silzc=silzc):
                @blk.vector
                def _(v):
                    self.t_sem.wait(v, yi_mm)
                    (self.s_sz if g < 4 else self.s_sz2).wait(v, silz_ld)
                    self.s_sp.wait(v, self.y1_ld)
                    for pr in range(2):
                        p_ = 2 * g + pr
                        sl = slice(p_ * 512 + j0, p_ * 512 + j0 + 128)
                        w = v.tensor_tensor(self.SCR[:, 0:128],
                                            ye_bank[:, pr * 128:pr * 128 + 128],
                                            silzc[:, p_ * 128:p_ * 128 + 128],
                                            OP.mult)
                        self.v_sem.chain(v, w)
                        w = v.tensor_tensor(self.Y1B[:, sl], self.Y1B[:, sl],
                                            self.SCR[:, 0:128], OP.add)
                        self.v_sem.chain(v, w)
                if g == 3:
                    self._epi_ev_g3 = self.v_sem.n
                elif g == G - 1:
                    self._epi_ev_chunk = self.v_sem.n

            self._epi_pending.append(emit_evac)
            while len(self._epi_pending) > 2:
                self._epi_pending.pop(0)()

    def ph_epi_flush(self):
        while self._epi_pending:
            self._epi_pending.pop(0)()

    def ph_norm_wout_block(self, b):
        blk = self.blk
        pb = self.pb
        col0, ncols, own0, nown = SBLK[b]

        for fc in range(NF):
            @blk.vector
            def _(v, fc=fc):
                if fc == 0:
                    self.v_sem.wait(v)
                else:
                    self.t_sem.wait(v)
                w = v.tensor_tensor(self.YSQ[:, 0:512],
                                    self.Y1B[:, fc * 512:(fc + 1) * 512],
                                    self.Y1B[:, fc * 512:(fc + 1) * 512],
                                    OP.mult)
                self.v_sem.inc(w)
            ysq_done = self.v_sem.n

            @blk.tensor
            def _(t, fc=fc, ysq_done=ysq_done):
                self.v_sem.wait(t, ysq_done)
                w = t.matmul(pb[6][0:KEXP, 0:512],
                             self.GINDs[:, fc * KEXP:(fc + 1) * KEXP],
                             self.YSQ[:, 0:512],
                             start=(fc == 0), stop=(fc == NF - 1))
                self.t_sem.inc(w)
        var_mm = self.t_sem.n

        @blk.vector
        def _(v):
            self.t_sem.wait(v, var_mm)
            w = v.tensor_scalar(self.RSTD[0:KEXP, 0:512], pb[6][0:KEXP, 0:512],
                                1.0 / 256.0, None, OP.mult)
            self.v_sem.inc(w)
        var_done = self.v_sem.n

        @blk.scalar
        def _(sc):
            self.v_sem.wait(sc, var_done)
            w = sc.activation(self.RSTD[0:KEXP, 0:512], self.RSTD[0:KEXP, 0:512],
                              AF.Ln, bias=self.MISC[0:KEXP, 0:1], scale=1.0)
            self.c_sem.chain(sc, w)
            w = sc.activation(self.RSTD[0:KEXP, 0:512], self.RSTD[0:KEXP, 0:512],
                              AF.Exp, bias=0.0, scale=-0.5)
            self.c_sem.inc(w)
        rstd_done0 = self.c_sem.n

        @blk.vector
        def _(v):
            self.c_sem.wait(v, rstd_done0)
            w = v.tensor_copy(self.RSTDB[0:KEXP, 0:512], self.RSTD[0:KEXP, 0:512])
            self.v_sem.inc(w)
        rstd_done = self.v_sem.n

        for fc in range(NF):
            @blk.tensor
            def _(t, fc=fc):
                if fc == 0:
                    self.v_sem.wait(t, rstd_done)
                else:
                    self.v_sem.wait(t)
                w = t.matmul(pb[4 + fc % 2][:, 0:512],
                             self.GSELs[0:KEXP, (fc // 2) * 128:(fc // 2) * 128 + 128],
                             self.RSTDB[0:KEXP, 0:512], start=True, stop=True)
                self.t_sem.inc(w)
            rb_mm = self.t_sem.n

            @blk.vector
            def _(v, fc=fc, rb_mm=rb_mm):
                self.t_sem.wait(v, rb_mm)
                w = v.scalar_tensor_tensor(
                    self.XNW[:, fc * BW: fc * BW + 512],
                    self.Y1B[:, fc * 512:(fc + 1) * 512],
                    self.NW[:, fc:fc + 1],
                    pb[4 + fc % 2][:, 0:512], OP.mult, OP.mult)
                self.v_sem.inc(w)
        yn_done = self.v_sem.n

        wo_mm = [None] * NF
        wo_ev = [None] * NF
        for wo in range(NF):
            buf = (self.WA, self.WB)[wo % 2]
            bsem = (self.s_wa, self.s_wb)[wo % 2]
            ld = self.stream_w(
                self.i_Wout[wo], buf[:], bsem,
                prior_t=wo_mm[wo - 2] if wo >= 2 else
                (self.last_wa_t, self.last_wb_t)[wo % 2])

            @blk.tensor
            def _(t, wo=wo, buf=buf, bsem=bsem, ld=ld):
                bsem.wait(t, ld)
                if wo == 0:
                    self.v_sem.wait(t, yn_done)
                if wo >= 2:
                    self.v_sem.wait(t, wo_ev[wo - 2])
                tgt = pb[2 * (wo % 2)]
                for kc in range(NF):
                    w = t.matmul(tgt[:, 0:512],
                                 buf[:, kc * 128:(kc + 1) * 128],
                                 self.XNW[:, kc * BW: kc * BW + 512],
                                 start=(kc == 0), stop=(kc == NF - 1))
                self.t_sem.inc(w)
            wo_mm[wo] = self.t_sem.n

            @blk.vector
            def _(v, wo=wo):
                self.t_sem.wait(v, wo_mm[wo])
                tgt = pb[2 * (wo % 2)]
                a = wo * T + CTX + own0
                w = v.scalar_tensor_tensor(self.X[:, a:a + 512], tgt[:, 0:512],
                                           self.BOUT[:, wo:wo + 1],
                                           self.X[:, a:a + 512], OP.add, OP.add)
                self.v_sem.inc(w)
            wo_ev[wo] = self.v_sem.n
        self.last_wa_t = wo_mm[NF - 2]
        self.last_wb_t = wo_mm[NF - 1]

    def ph_store_block(self, b):
        blk = self.blk
        col0, ncols, own0, nown = SBLK[b]
        vn = self.v_sem.n

        @blk.sync
        def _(s):
            self.v_sem.wait(s, vn)
            w = s.dma_start(
                out=self.o_out.rearrange("(c p) t -> p c t", p=128)
                    [:, :, own0:own0 + nown],
                in_=self.X[:].rearrange("p (c t) -> p c t", c=NF)
                    [:, :, CTX + own0:CTX + own0 + nown])
            self.s_st.inc(w, 16)

    def ph_finish(self):
        blk = self.blk

        @blk.sync
        def _(s):
            self.s_st.wait(s)
            if self.g_ld.n:
                self.g_ld.wait(s)



def _host_prepare(inputs):
    x = np.asarray(inputs["L0_attn_hidden_states"])
    Win = np.asarray(inputs["L0_attn_W_in"])
    b_in = np.asarray(inputs["L0_attn_b_in"])
    mup = np.asarray(inputs["L0_attn_mup_vector"])
    Win_pad = np.zeros((D, PROJ_PAD), np.float32)
    Win_pad[:, :PROJ] = Win
    b_pad = np.zeros((PROJ_PAD,), np.float32)
    b_pad[:PROJ] = b_in
    mup_pad = np.zeros((PROJ_PAD,), np.float32)
    mup_pad[:PROJ] = mup
    bmup = np.ascontiguousarray((b_pad * mup_pad).reshape(NFO, 128).T)
    mupc = np.ascontiguousarray(mup_pad.reshape(NFO, 128).T)

    V = np.asarray(inputs["L0_ffn_V"])
    U = np.asarray(inputs["L0_ffn_U"])
    Vc = np.ascontiguousarray(V.transpose(2, 0, 1).reshape(D, KR))
    Uc = np.ascontiguousarray(U.transpose(0, 2, 1).reshape(KR, D))

    conv_w = np.asarray(inputs["L0_attn_conv_w"])
    convw = np.ascontiguousarray(
        conv_w.reshape(NCONV, 128, 4).transpose(1, 0, 2).reshape(128, NCONV * 4))
    convb = np.ascontiguousarray(
        np.asarray(inputs["L0_attn_conv_b"]).reshape(NCONV, 128).T)

    def col(vec, n):
        return np.ascontiguousarray(np.asarray(vec).reshape(n, 128).T)

    REPS = np.zeros((KEXP, NKR * 128), np.float32)
    for rc in range(NKR):
        for p in range(128):
            REPS[2 * rc + p // 64, rc * 128 + p] = 1.0
    GSEL = np.zeros((KEXP, G * 128), np.float32)
    for g in range(G):
        GSEL[g, g * 128:(g + 1) * 128] = 1.0
    import ml_dtypes
    GIND = np.zeros((128, NF * KEXP), np.float32)
    for fc in range(NF):
        GIND[:, fc * KEXP + fc // 2] = 1.0
    MISCC = np.zeros((128, 8), np.float32)
    MISCC[:, 0] = 1e-5
    MISCC[:, 1] = 1.0
    MISCC[:, 2] = 80.0
    MISCC[:, 3] = 0.5
    E127 = np.zeros((128, 1), np.float32)
    E127[127, 0] = 1.0

    def tile_lhsT(W, nk, nm):
        return np.ascontiguousarray(
            W.reshape(nk, 128, nm, 128).transpose(2, 1, 0, 3).reshape(nm, 128, nk * 128))

    common = dict(
        Win=tile_lhsT(Win_pad, NF, NFO),
        W1=tile_lhsT(np.asarray(inputs["L0_ffn_W1"]), NF, NF),
        W2=np.ascontiguousarray(
            np.asarray(inputs["L0_ffn_W2"]).reshape(NF, 128, KEXP)
            .transpose(1, 0, 2).reshape(128, NF * KEXP)),
        Wout=tile_lhsT(np.asarray(inputs["L0_attn_W_out"]), NF, NF),
        Vc=tile_lhsT(Vc, NF, NKR), Uc=tile_lhsT(Uc, NKR, NF),
        bmup=bmup, mup=mupc,
        b1col=col(inputs["L0_ffn_b1"], NF),
        b2col=np.asarray(inputs["L0_ffn_b2"]).reshape(KEXP, 1).astype(np.float32),
        boutcol=col(inputs["L0_attn_b_out"], NF),
        nwcol=col(inputs["L0_attn_norm_w"], NF),
        convbcol=convb, convwcol=convw,
        Dcol=np.ascontiguousarray(
            np.repeat(np.asarray(inputs["L0_attn_D_param"]), PDIM)
            .reshape(NF, 128).T).astype(np.float32),
        Alogcol=np.asarray(inputs["L0_attn_A_log"]).reshape(NH, 1).astype(np.float32),
        dtbcol=np.asarray(inputs["L0_attn_dt_bias"]).reshape(NH, 1).astype(np.float32),
        maskinf=np.where(np.arange(Q)[:, None] <= np.arange(Q)[None, :], 0.0,
                         -1e30).astype(np.float32),
        triu01=np.where(np.arange(Q)[:, None] <= np.arange(Q)[None, :], 1.0,
                        0.0).astype(np.float32),
        I128=np.eye(128, dtype=np.float32),
        IB128=np.eye(128, dtype=np.float32).astype(ml_dtypes.bfloat16),
        e127=E127, misccol=MISCC, REPS=REPS, GSEL=GSEL.astype(ml_dtypes.bfloat16),
        GIND=GIND.astype(ml_dtypes.bfloat16),
    )

    in_maps = []
    for cidx in range(8):
        bb, hf = cidx // 2, cidx % 2
        lo = hf * TOWN
        xs = x[bb]
        if hf == 0:
            ctx_cols = np.zeros((CTX, D), np.float32)
        else:
            ctx_cols = xs[lo - CTX:lo]
        xl = np.concatenate([ctx_cols, xs[lo:lo + TOWN]], axis=0)
        m = dict(common)
        m["x_fm"] = np.ascontiguousarray(xl.T)
        m["flagcol"] = np.full((128, 1), 1.0 if hf == 1 else 0.0, np.float32)
        m["flaginvcol"] = np.full((128, 1), 0.0 if hf == 1 else 1.0, np.float32)
        in_maps.append(m)
    return in_maps


def _run(debug=()):
    b = Build(debug)
    nc = b.build()
    lower_extended_insts(nc)
    return b, nc


def kernel_dev(**inputs):
    in_maps = _host_prepare(inputs)
    b, nc = _run(_DEBUG)
    res = run_bass_kernel_spmd(nc, in_maps, core_ids=list(range(8)))
    dumps = {}
    for dram_name, name in b.dumps:
        dumps[name] = [res.results[c][dram_name] for c in range(8)]
    return _assemble(res), dumps


def _assemble(res):
    out = np.empty((4, 2048, D), np.float32)
    for c in range(8):
        b, hf = c // 2, c % 2
        out[b, hf * TOWN:(hf + 1) * TOWN] = res.results[c]["out_fm"].T
    return out


def kernel(**inputs):
    in_maps = _host_prepare(inputs)
    b, nc = _run(())
    res = run_bass_kernel_spmd(nc, in_maps, core_ids=list(range(8)))
    return _assemble(res)


# version 10
# speedup vs baseline: 1.0061x; 1.0061x over previous
import numpy as np
from contextlib import ExitStack
import concourse.bass as bass
import concourse.mybir as mybir
from concourse.bass_utils import run_bass_kernel_spmd
from concourse import library_config
from concourse.library_overlay import lower_extended_insts

F32 = mybir.dt.float32
BF16 = mybir.dt.bfloat16
AF = mybir.ActivationFunctionType
OP = mybir.AluOpType
AXX = mybir.AxisListType.X

D = 2048
TOWN = 1024
CTX = 3
T = TOWN + CTX
NF = 16
NFO = 37
PROJ = 4640
PROJ_PAD = NFO * 128
NH, PDIM, G, S = 32, 64, 8, 32
NCONV = 20
NCHUNK = 8
Q = 128
KEXP, RRANK = 8, 64
KR = 512
NKR = 4
BW = 520

SBLK = [(0, 515, 0, 512), (515, 512, 512, 512)]
PIECES = {0: [(0, 387), (387, 128)], 1: [(0, 512)]}

_DEBUG = ()


class SemT:
    def __init__(self, nc, ctx, name):
        self.h = ctx.enter_context(nc.semaphore(name))
        self.n = 0

    def inc(self, bi, k=1):
        bi.then_inc(self.h, k)
        self.n += k
        return self.n

    def wait(self, eng, n=None):
        eng.wait_ge(self.h, self.n if n is None else n)

    def chain(self, eng, bi):
        c = self.inc(bi)
        eng.wait_ge(self.h, c)


def owned_map(a0, n):
    so = max(0, CTX - a0)
    return so, a0 + so - CTX, n - so


class Build:
    def __init__(self, debug=()):
        self.debug = tuple(debug)
        self.nc = bass.Bass("TRN2", target_bir_lowering=False, debug=False)
        self.ctx = ExitStack()
        self.dumps = []

    def sb(self, name, shape, dt=F32):
        return self.ctx.enter_context(self.nc.sbuf_tensor(name, shape, dt))

    def sem(self, name):
        return SemT(self.nc, self.ctx, name)

    def din(self, name, shape, dt=F32):
        return self.nc.dram_tensor(name, shape, dt, kind="ExternalInput").ap()

    def dout(self, name, shape, dt=F32):
        return self.nc.dram_tensor(name, shape, dt, kind="ExternalOutput").ap()

    def want(self, name):
        return name in self.debug

    def dump(self, name, tile, width):
        if not self.want(name):
            return
        nparts = tile[:].shape[0]
        dr = self.dout(f"dbg_{name}", [nparts, width], F32)
        blk = self.blk
        vn, tn, cn, gn = self.v_sem.n, self.t_sem.n, self.c_sem.n, self.g_sem.n

        @blk.gpsimd
        def _(g):
            self.v_sem.wait(g, vn)
            self.t_sem.wait(g, tn)
            self.c_sem.wait(g, cn)
            self.g_sem.wait(g, gn)
            bi = g.dma_start(out=dr, in_=tile[:, 0:width])
            self.g_ld.inc(bi, 16)
        self.dumps.append((f"dbg_{name}", name))

    def barrier(self, engines):
        blk = self.blk
        sems = [self.v_sem, self.t_sem, self.c_sem, self.g_sem]
        cts = [s.n for s in sems]

        spn = self.s_sp.n

        def mk(e):
            for s, c in zip(sems, cts):
                s.wait(e, c)
            e.wait_ge(self.s_sp.h, spn)
        for en in engines:
            if en == "t":
                @blk.tensor
                def _(t):
                    mk(t)
            elif en == "v":
                @blk.vector
                def _(v):
                    mk(v)
            elif en == "c":
                @blk.scalar
                def _(sc):
                    mk(sc)
            elif en == "g":
                @blk.gpsimd
                def _(g):
                    mk(g)

    def build(self):
        nc = self.nc
        self.declare_io()
        self.alloc()
        self.blk = self.ctx.enter_context(nc.Block())
        self.ph_init()
        with ExitStack() as ffn1_scope:
            self._scope = ffn1_scope
            self.VXW = self.scoped("VXW", [128, NKR * BW], BF16)
            for b in range(2):
                self.ph_ln_block(b, self.X, self.XNW)
                self.ph_ffn_block(b, 0.5)
            self.barrier("tvcg")
        self.dump("x1", self.X, NF * T)
        with ExitStack() as prep_scope:
            self._scope = prep_scope
            self.DTLD = self.scoped("DTLD", [NH, TOWN], F32)
            self.LDC = self.scoped("LDC", [NH, 128], F32)
            self.DTOK = self.scoped("DTOK", [128, 4 * 64], F32)
            self.NRT = self.scoped("NRT", [128, 4 * S], F32)
            self.WGT = self.scoped("WGT", [128, 4 * NH], F32)
            self.NLR = self.scoped("NLR", [1, 64], F32)
            self.BTW = self.scoped("BTW", [128, 2 * 512], BF16)
            self.CARR = self.scoped("CARR", [128, NCONV * 4], F32)
            self.CACC = self.scoped("CACC", [128, 512], F32)
            self.CSTR = self.scoped("CSTR", [128, 528], F32)
            self.XTOK = self.scoped("XTOK", [128, 2 * D], BF16)
            self.BW4 = self.scoped("BW4", [128, 128], BF16)
            self.ERS = self.scoped("ERS", [S, Q], F32)
            self.RB4 = self.scoped("RB4", [128, 512], F32)
            self.MT = self.ABC[:, 256:512].bitcast(BF16)
            self.RFMs = self.CSTR
            self.DTT = self.CACC
            for b in range(2):
                self.barrier("tvcg")
                self.ph_ln_block(b, self.X, self.XNW)
                self.ph_win_block(b)
                self.ph_dt_block(b)
                self.ph_r_block(b)
                for c in range(4 * b, 4 * b + 4):
                    self.ph_ssd_chunk(b, c)
                self.ph_spill_block(b)
            self.dump("dtf", self.DTLD, TOWN)
            self.dump("convx", self.HW, NF * BW)
            self.dump("ct", self.CT, 2 * TOWN)
            self.dump("sst", self.SST, NCHUNK * 512)
            self.barrier("tvcg")
        self.ph_state_collective()
        with ExitStack() as epi_scope:
            self._scope = epi_scope
            self.SCAR = self.scoped("SCAR", [128, 512], F32)
            self.EBT = self.scoped("EBT", [128, 512], F32)
            self.CDT = self.scoped("CDT", [128, 512], BF16)
            self.CDT2 = self.scoped("CDT2", [128, 512], BF16)
            self.EBT2 = self.scoped("EBT2", [128, 512], F32)
            self.DTT = self.scoped("DTT2", [128, 512], F32)
            self.SUSED = self.scoped("SUSED", [128, 512], BF16)
            self.SUSED2 = self.scoped("SUSED2", [128, 512], BF16)
            self.RSTD = self.scoped("RSTD", [KEXP, 512], F32)
            self.RSTDB = self.scoped("RSTDB", [KEXP, 512], BF16)
            self.UNB = self.scoped("UNB", [128, NKR * BW], BF16)
            self.SILZC = self.UNB
            self.VXW = self.UNB
            self.ph_cc_recv()
            for b in range(2):
                self.barrier("tvcg")
                self.ph_y1_load(b)
                for c in range(4 * b, 4 * b + 4):
                    self.ph_epi_chunk(b, c)
                self.ph_epi_flush()
                if b == 0:
                    self.dump("ebt", self.EBT, 512)
                    self.dump("scar", self.SCAR, 512)
                    self.dump("y1b0", self.Y1B, NF * 512)
                    self.dump("sused", self.SUSED, 512)
                    self.dump("cdt", self.CDT, 512)
                self.ph_norm_wout_block(b)
                if b == 0:
                    self.dump("rstd", self.RSTD, 512)
                    self.dump("ynw", self.XNW, NF * BW)
                if b == 1:
                    self.dump("x2", self.X, NF * T)
                self.barrier("tvcg")
                self.ph_ln_block(b, self.X, self.XNW)
                self.ph_ffn_block(b, 0.5)
                self.ph_store_block(b)
        self.ph_finish()
        self.ctx.close()
        return nc

    def scoped(self, name, shape, dt):
        return self._scope.enter_context(self.nc.sbuf_tensor(name, shape, dt))

    def declare_io(self):
        d = self.din
        self.i_x = d("x_fm", [D, T])
        self.i_Win = d("Win", [NFO, 128, NF * 128])
        self.i_W1 = d("W1", [NF, 128, NF * 128])
        self.i_W2 = d("W2", [128, NF * KEXP])
        self.i_Wout = d("Wout", [NF, 128, NF * 128])
        self.i_Vc = d("Vc", [NKR, 128, NF * 128])
        self.i_Uc = d("Uc", [NF, 128, NKR * 128])
        self.i_bmup = d("bmup", [128, NFO])
        self.i_mup = d("mup", [128, NFO])
        self.i_b1 = d("b1col", [128, NF])
        self.i_b2 = d("b2col", [KEXP, 1])
        self.i_bout = d("boutcol", [128, NF])
        self.i_nw = d("nwcol", [128, NF])
        self.i_convb = d("convbcol", [128, NCONV])
        self.i_convw = d("convwcol", [128, NCONV * 4])
        self.i_Dcol = d("Dcol", [128, NF])
        self.i_Alog = d("Alogcol", [NH, 1])
        self.i_dtb = d("dtbcol", [NH, 1])
        self.i_mask = d("maskinf", [Q, Q])
        self.i_triu = d("triu01", [Q, Q])
        self.i_I128 = d("I128", [128, 128])
        self.i_IB128 = d("IB128", [128, 128], BF16)
        self.i_e127 = d("e127", [128, 1])
        self.i_flag = d("flagcol", [128, 1])
        self.i_flaginv = d("flaginvcol", [128, 1])
        self.i_misc = d("misccol", [128, 8])
        self.i_REPS = d("REPS", [KEXP, NKR * 128])
        self.i_GSEL = d("GSEL", [KEXP, G * 128], BF16)
        self.i_GIND = d("GIND", [128, NF * KEXP], BF16)
        self.h_r = self.nc.dram_tensor("r_scratch", [NCHUNK, NH * Q], F32,
                                       kind="Internal").ap()
        self.h_er = self.nc.dram_tensor("er_scratch", [NCHUNK, NH * Q], F32,
                                        kind="Internal").ap()
        self.h_silz = self.nc.dram_tensor("silz_hbm", [D, TOWN], BF16,
                                          kind="Internal").ap()
        self.h_y1 = self.nc.dram_tensor("y1_hbm", [D, TOWN], BF16,
                                        kind="Internal").ap()
        self.h_ccin = self.nc.dram_tensor("cc_in", [128, 512], F32,
                                          kind="Internal").ap()
        self.h_ccout = self.nc.dram_tensor("cc_out", [128, 512], F32,
                                           kind="Internal").ap()
        self.o_out = self.dout("out_fm", [D, TOWN])

    def alloc(self):
        sb = self.sb
        self.X = sb("X", [128, NF * T], F32)
        self.SILZB = sb("SILZB", [128, NF * 512], BF16)
        self.Y1B = sb("Y1B", [128, NF * 512], BF16)
        self.SST = sb("SST", [128, NCHUNK * 512], BF16)
        self.CT = sb("CT", [128, 2 * TOWN], BF16)
        self.DECROW = sb("DECROW", [1, NCHUNK * NH], F32)
        self.WA = sb("WA", [128, NF * 128], BF16)
        self.WB = sb("WB", [128, NF * 128], BF16)
        self.W2B = sb("W2B", [128, NF * KEXP], BF16)
        self.MASK = sb("MASK", [Q, Q], F32)
        self.TRIU = sb("TRIU", [Q, Q], F32)
        self.I128s = sb("I128s", [128, 128], F32)
        self.IB128s = sb("IB128s", [128, 128], BF16)
        self.E127 = sb("E127", [128, 1], F32)
        self.ONESB = sb("ONESB", [128, 1], BF16)
        self.FLAG = sb("FLAG", [128, 1], F32)
        self.FLAGI = sb("FLAGI", [128, 1], F32)
        self.MISC = sb("MISC", [128, 8], F32)
        self.BMUP = sb("BMUP", [128, NFO], F32)
        self.MUP = sb("MUPc", [128, NFO], F32)
        self.B1 = sb("B1c", [128, NF], F32)
        self.B2 = sb("B2c", [KEXP, 1], F32)
        self.BOUT = sb("BOUTc", [128, NF], F32)
        self.NW = sb("NWc", [128, NF], F32)
        self.CONVB = sb("CONVBc", [128, NCONV], F32)
        self.CONVW = sb("CONVWc", [128, NCONV * 4], F32)
        self.DCOL = sb("DCOLc", [128, NF], F32)
        self.ALOG = sb("ALOGc", [NH, 1], F32)
        self.DTB = sb("DTBc", [NH, 1], F32)
        self.REPSs = sb("REPSs", [KEXP, NKR * 128], F32)
        self.GSELs = sb("GSELs", [KEXP, G * 128], BF16)
        self.GINDs = sb("GINDs", [128, NF * KEXP], BF16)
        self.ROWS = sb("ROWSr", [1, 2048], F32)
        self.ARS = sb("ARSr", [KEXP, BW], F32)
        self.SMX = sb("SMXr", [128, 64], F32)
        self.ABC = sb("ABC", [128, 2 * 512], F32)
        self.SCR = sb("SCRs", [128, 512], F32)
        self.SCR2a = sb("SCR2a", [128, 2 * 512], BF16)
        self.XNW = sb("XNW", [128, NF * BW], BF16)
        self.HW = sb("HW", [128, NF * BW], BF16)
        self.YSQ = sb("YSQ", [128, 2 * 512], BF16)
        self.RCHL0 = sb("RCHL0", [S, 1024], F32)
        self.pb = [self.ctx.enter_context(
            self.nc.psum_tensor(f"pb{i}", [128, 512], F32)) for i in range(8)]
        self.s_ld = self.sem("s_ld")
        self.s_st = self.sem("s_st")
        self.g_ld = self.sem("g_ld")
        self.s_wa = self.sem("s_wa")
        self.s_wb = self.sem("s_wb")
        self.s_w2 = self.sem("s_w2")
        self.s_r = self.sem("s_r")
        self.s_rq = self.sem("s_rq")
        self.s_sp = self.sem("s_sp")
        self.s_sz = self.sem("s_sz")
        self.s_sz2 = self.sem("s_sz2")
        self.t_sem = self.sem("t_sem")
        self.v_sem = self.sem("v_sem")
        self.c_sem = self.sem("c_sem")
        self.g_sem = self.sem("g_sem")

    def ph_init(self):
        blk = self.blk

        @blk.gpsimd
        def _(g):
            g.load_library(library_config.proxy)

        @blk.sync
        def _(s):
            bi = s.dma_start(
                out=self.X[:].rearrange("p (c t) -> p c t", c=NF),
                in_=self.i_x.rearrange("(c p) t -> p c t", p=128))
            self.s_ld.inc(bi, 16)
            for ap_in, tile in [
                (self.i_mask, self.MASK), (self.i_triu, self.TRIU),
                (self.i_I128, self.I128s), (self.i_IB128, self.IB128s),
                (self.i_e127, self.E127),
                (self.i_flag, self.FLAG), (self.i_flaginv, self.FLAGI),
                (self.i_misc, self.MISC),
                (self.i_bmup, self.BMUP), (self.i_mup, self.MUP),
                (self.i_b1, self.B1), (self.i_b2, self.B2),
                (self.i_bout, self.BOUT), (self.i_nw, self.NW),
                (self.i_convb, self.CONVB), (self.i_convw, self.CONVW),
                (self.i_Dcol, self.DCOL), (self.i_Alog, self.ALOG),
                (self.i_dtb, self.DTB), (self.i_REPS, self.REPSs),
                (self.i_GSEL, self.GSELs), (self.i_GIND, self.GINDs),
            ]:
                bi = s.dma_start(out=tile[:], in_=ap_in)
                self.s_ld.inc(bi, 16)
        self.init_ld = self.s_ld.n

        @blk.vector
        def _(v):
            self.s_ld.wait(v)
            w = v.memset(self.ONESB[:], 1.0)
            self.v_sem.inc(w)

        @blk.scalar
        def _(sc):
            self.s_ld.wait(sc, self.init_ld)
            w = sc.activation(self.MISC[0:NH, 4:5], self.ALOG[0:NH, 0:1],
                              AF.Exp, bias=0.0, scale=1.0)
            self.c_sem.inc(w)

    def ph_ln_block(self, b, src, win):
        blk = self.blk
        pb = self.pb
        col0 = SBLK[b][0]
        RM, RE, RR, RB_ = 0, 512, 1024, 1536
        self.ln_win_g = {}
        for (po, n) in PIECES[b]:
            a0 = col0 + po

            @blk.vector
            def _(v, a0=a0, po=po, n=n):
                self.v_sem.wait(v)
                self.t_sem.wait(v)
                self.g_sem.wait(v)
                for fc in range(NF):
                    w = v.tensor_copy(win[:, fc * BW + po: fc * BW + po + n],
                                      src[:, fc * T + a0: fc * T + a0 + n])
                self.v_sem.inc(w)
            cast_done = self.v_sem.n

            @blk.tensor
            def _(t, po=po, n=n, cast_done=cast_done):
                self.v_sem.wait(t, cast_done)
                for fc in range(NF):
                    w = t.matmul(pb[6][0:1, 0:n], self.ONESB[:],
                                 win[:, fc * BW + po: fc * BW + po + n],
                                 start=(fc == 0), stop=(fc == NF - 1))
                self.t_sem.inc(w)

            sq_mms = [None] * NF
            for fc in range(NF):
                @blk.vector
                def _(v, fc=fc, po=po, n=n, cast_done=cast_done):
                    if fc == 0:
                        self.v_sem.wait(v, cast_done)
                    elif fc >= 2:
                        self.t_sem.wait(v, sq_mms[fc - 2])
                    w = v.tensor_tensor(self.YSQ[:, (fc % 2) * 512:(fc % 2) * 512 + n],
                                        win[:, fc * BW + po: fc * BW + po + n],
                                        win[:, fc * BW + po: fc * BW + po + n],
                                        OP.mult)
                    self.v_sem.inc(w)
                sq_done = self.v_sem.n

                @blk.tensor
                def _(t, fc=fc, n=n, sq_done=sq_done):
                    self.v_sem.wait(t, sq_done)
                    w = t.matmul(pb[7][0:1, 0:n], self.ONESB[:],
                                 self.YSQ[:, (fc % 2) * 512:(fc % 2) * 512 + n],
                                 start=(fc == 0), stop=(fc == NF - 1))
                    self.t_sem.inc(w)
                sq_mms[fc] = self.t_sem.n
            sq_mm = self.t_sem.n

            @blk.vector
            def _(v, n=n, sq_mm=sq_mm):
                self.t_sem.wait(v, sq_mm)
                w = v.tensor_scalar(self.ROWS[0:1, RM:RM + n],
                                    pb[6][0:1, 0:n], 1.0 / D, None, OP.mult)
                self.v_sem.chain(v, w)
                w = v.tensor_scalar(self.ROWS[0:1, RE:RE + n],
                                    pb[7][0:1, 0:n], 1.0 / D, None, OP.mult)
                self.v_sem.chain(v, w)
                w = v.tensor_tensor(self.ROWS[0:1, RR:RR + n],
                                    self.ROWS[0:1, RM:RM + n],
                                    self.ROWS[0:1, RM:RM + n], OP.mult)
                self.v_sem.chain(v, w)
                w = v.tensor_tensor(self.ROWS[0:1, RE:RE + n],
                                    self.ROWS[0:1, RE:RE + n],
                                    self.ROWS[0:1, RR:RR + n], OP.subtract)
                self.v_sem.inc(w)
            var_done = self.v_sem.n

            @blk.scalar
            def _(sc, n=n, var_done=var_done):
                self.v_sem.wait(sc, var_done)
                w = sc.activation(self.ROWS[0:1, RR:RR + n],
                                  self.ROWS[0:1, RE:RE + n],
                                  AF.Ln, bias=self.MISC[0:1, 0:1], scale=1.0)
                self.c_sem.chain(sc, w)
                w = sc.activation(self.ROWS[0:1, RR:RR + n],
                                  self.ROWS[0:1, RR:RR + n],
                                  AF.Exp, bias=0.0, scale=-0.5)
                self.c_sem.inc(w)
            rstd_done = self.c_sem.n

            @blk.vector
            def _(v, n=n, rstd_done=rstd_done):
                self.c_sem.wait(v, rstd_done)
                w = v.scalar_tensor_tensor(self.ROWS[0:1, RB_:RB_ + n],
                                           self.ROWS[0:1, RM:RM + n], -1.0,
                                           self.ROWS[0:1, RR:RR + n],
                                           OP.mult, OP.mult)
                self.v_sem.inc(w)
            rows_done = self.v_sem.n

            @blk.gpsimd
            def _(g, n=n, rows_done=rows_done):
                self.v_sem.wait(g, rows_done)
                self.g_sem.wait(g)
                g.partition_broadcast(self.ABC[:, 0:n],
                                      self.ROWS[0:1, RR:RR + n], 128)
                w = g.partition_broadcast(self.ABC[:, 512:512 + n],
                                          self.ROWS[0:1, RB_:RB_ + n], 128)
                self.g_sem.inc(w)
            bc_done = self.g_sem.n

            v_base = self.v_sem.n
            g_base = self.g_sem.n

            @blk.vector
            def _(v, a0=a0, po=po, n=n, bc_done=bc_done, g_base=g_base):
                self.g_sem.wait(v, bc_done)
                for fc in range(NF):
                    if fc >= 2:
                        v.wait_ge(self.g_sem.h, g_base + fc - 1)
                    w = v.tensor_tensor(self.SCR2a[:, (fc % 2) * 512:(fc % 2) * 512 + n],
                                        src[:, fc * T + a0: fc * T + a0 + n],
                                        self.ABC[:, 0:n], OP.mult)
                    self.v_sem.inc(w)

            @blk.gpsimd
            def _(g_, a0=a0, po=po, n=n, v_base=v_base):
                for fc in range(NF):
                    g_.wait_ge(self.v_sem.h, v_base + fc + 1)
                    w = g_.tensor_tensor(win[:, fc * BW + po: fc * BW + po + n],
                                         self.SCR2a[:, (fc % 2) * 512:(fc % 2) * 512 + n],
                                         self.ABC[:, 512:512 + n], OP.add)
                    self.g_sem.inc(w)
                    self.ln_win_g[fc] = self.g_sem.n
        self.ln_done = self.v_sem.n
        self.ln_done_g = self.g_sem.n

    def stream_w(self, hbm_tile_ap, buf_slice, sem, prior_t=None):
        blk = self.blk

        @blk.gpsimd
        def _(g):
            if prior_t is not None:
                self.t_sem.wait(g, prior_t)
            w = g.dma_start(out=buf_slice, in_=hbm_tile_ap)
            sem.inc(w, 16)
        return sem.n

    def ph_ffn_block(self, b, scale):
        blk = self.blk
        pb = self.pb
        col0, ncols, _, _ = SBLK[b]
        pieces = PIECES[b]
        xn_ready = self.ln_done

        mm_done = [None] * NF
        ev_done = [None] * NF
        for fo in range(NF):
            buf = (self.WA, self.WB)[fo % 2]
            bsem = (self.s_wa, self.s_wb)[fo % 2]
            ld = self.stream_w(
                self.i_W1[fo], buf[:], bsem,
                prior_t=mm_done[fo - 2] if fo >= 2 else
                getattr(self, ('last_wa_t', 'last_wb_t')[fo % 2], 0))

            @blk.tensor
            def _(t, fo=fo, buf=buf, bsem=bsem, ld=ld):
                bsem.wait(t, ld)
                if fo >= 2:
                    self.v_sem.wait(t, ev_done[fo - 2])
                for pi, (po, n) in enumerate(pieces):
                    tgt = pb[2 * (fo % 2) + pi]
                    for kc in range(NF):
                        if fo == 0 and pi == 0:
                            self.g_sem.wait(t, self.ln_win_g[kc])
                        w = t.matmul(tgt[:, 0:n],
                                     buf[:, kc * 128:(kc + 1) * 128],
                                     self.XNW[:, kc * BW + po: kc * BW + po + n],
                                     start=(kc == 0), stop=(kc == NF - 1))
                self.t_sem.inc(w)
            mm_done[fo] = self.t_sem.n

            @blk.vector
            def _(v, fo=fo):
                self.t_sem.wait(v, mm_done[fo])
                for pi, (po, n) in enumerate(pieces):
                    tgt = pb[2 * (fo % 2) + pi]
                    w = v.tensor_scalar(
                        self.HW[:, fo * BW + po: fo * BW + po + n],
                        tgt[:, 0:n], self.B1[:, fo:fo + 1], 0.0, OP.add, OP.max)
                self.v_sem.inc(w)
            ev_done[fo] = self.v_sem.n
        h_done = self.v_sem.n

        w2_ld = self.stream_w(
            self.i_W2.rearrange("(c p) m -> p c m", p=128),
            self.W2B[:].rearrange("p (c m) -> p c m", c=NF), self.s_w2)

        @blk.tensor
        def _(t):
            self.s_w2.wait(t, w2_ld)
            self.v_sem.wait(t, h_done)
            for pi, (po, n) in enumerate(pieces):
                tgt = (pb[4], pb[6])[pi]
                for kc in range(NF):
                    w = t.matmul(tgt[0:KEXP, 0:n],
                                 self.W2B[:, kc * KEXP:(kc + 1) * KEXP],
                                 self.HW[:, kc * BW + po: kc * BW + po + n],
                                 start=(kc == 0), stop=(kc == NF - 1))
            self.t_sem.inc(w)
        w2mm = self.t_sem.n

        @blk.vector
        def _(v):
            self.t_sem.wait(v, w2mm)
            for pi, (po, n) in enumerate(pieces):
                tgt = (pb[4], pb[6])[pi]
                w = v.tensor_scalar(self.ARS[0:KEXP, po:po + n], tgt[0:KEXP, 0:n],
                                    self.B2[0:KEXP, 0:1], None, OP.add)
            self.v_sem.inc(w)
        ar_done = self.v_sem.n

        nj = (ncols + 127) // 128

        @blk.tensor
        def _(t):
            self.v_sem.wait(t, ar_done)
            for j in range(nj):
                nt = min(128, ncols - j * 128)
                w = t.transpose(pb[5][0:nt, j * 8:(j + 1) * 8],
                                self.ARS[0:KEXP, j * 128:j * 128 + nt],
                                self.I128s[0:KEXP, 0:KEXP])
            self.t_sem.inc(w)
        tf_done = self.t_sem.n

        @blk.vector
        def _(v):
            self.t_sem.wait(v, tf_done)
            for j in range(nj):
                nt = min(128, ncols - j * 128)
                w = v.reduce_max(self.SMX[0:nt, 32 + j:33 + j],
                                 pb[5][0:nt, j * 8:(j + 1) * 8], AXX)
            self.v_sem.chain(v, w)
            for j in range(nj):
                nt = min(128, ncols - j * 128)
                w = v.tensor_scalar(self.SMX[0:nt, 40 + j:41 + j],
                                    self.SMX[0:nt, 32 + j:33 + j],
                                    -1.0, None, OP.mult)
            self.v_sem.inc(w)
        neg_done = self.v_sem.n

        @blk.scalar
        def _(sc):
            self.v_sem.wait(sc, neg_done)
            for j in range(nj):
                nt = min(128, ncols - j * 128)
                w = sc.activation(self.SMX[0:nt, j * 8:(j + 1) * 8],
                                  pb[5][0:nt, j * 8:(j + 1) * 8], AF.Exp,
                                  bias=self.SMX[0:nt, 40 + j:41 + j], scale=1.0)
            self.c_sem.inc(w)
        exp_done = self.c_sem.n

        @blk.vector
        def _(v):
            self.c_sem.wait(v, exp_done)
            for j in range(nj):
                nt = min(128, ncols - j * 128)
                w = v.reduce_sum(self.SMX[0:nt, 48 + j:49 + j],
                                 self.SMX[0:nt, j * 8:(j + 1) * 8], AXX)
            self.v_sem.chain(v, w)
            for j in range(nj):
                nt = min(128, ncols - j * 128)
                w = v.reciprocal(self.SMX[0:nt, 56 + j:57 + j],
                                 self.SMX[0:nt, 48 + j:49 + j])
            self.v_sem.chain(v, w)
            for j in range(nj):
                nt = min(128, ncols - j * 128)
                w = v.tensor_scalar(self.SMX[0:nt, j * 8:(j + 1) * 8],
                                    self.SMX[0:nt, j * 8:(j + 1) * 8],
                                    self.SMX[0:nt, 56 + j:57 + j], None, OP.mult)
            self.v_sem.inc(w)
        at_done = self.v_sem.n

        @blk.tensor
        def _(t):
            self.v_sem.wait(t, at_done)
            for j in range(nj):
                nt = min(128, ncols - j * 128)
                if j < 4:
                    w = t.transpose(pb[7][0:KEXP, j * 128:j * 128 + nt],
                                    self.SMX[0:nt, j * 8:(j + 1) * 8],
                                    self.I128s[0:nt, 0:nt])
                else:
                    w = t.transpose(pb[6][0:KEXP, 128:128 + nt],
                                    self.SMX[0:nt, j * 8:(j + 1) * 8],
                                    self.I128s[0:nt, 0:nt])
            self.t_sem.inc(w)
        tb_done = self.t_sem.n

        @blk.vector
        def _(v):
            self.t_sem.wait(v, tb_done)
            w = v.tensor_copy(self.ARS[0:KEXP, 0:min(ncols, 512)],
                              pb[7][0:KEXP, 0:min(ncols, 512)])
            if ncols > 512:
                self.v_sem.chain(v, w)
                w = v.tensor_copy(self.ARS[0:KEXP, 512:ncols],
                                  pb[6][0:KEXP, 128:128 + ncols - 512])
            self.v_sem.inc(w)
        alpha_done = self.v_sem.n

        vx_mms = []
        vx_evs = []
        for rc in range(NKR):
            buf = (self.WA, self.WB)[rc % 2]
            bsem = (self.s_wa, self.s_wb)[rc % 2]
            ld = self.stream_w(
                self.i_Vc[rc], buf[:], bsem,
                prior_t=vx_mms[rc - 2] if rc >= 2 else mm_done[NF - 2 + rc])

            @blk.tensor
            def _(t, rc=rc, buf=buf, bsem=bsem, ld=ld):
                bsem.wait(t, ld)
                if rc == 0:
                    self.v_sem.wait(t, h_done)
                if rc >= 2:
                    self.v_sem.wait(t, vx_evs[rc - 2])
                for pi, (po, n) in enumerate(pieces):
                    tgt = pb[2 * (rc % 2) + pi]
                    for kc in range(NF):
                        w = t.matmul(tgt[:, 0:n],
                                     buf[:, kc * 128:(kc + 1) * 128],
                                     self.XNW[:, kc * BW + po: kc * BW + po + n],
                                     start=(kc == 0), stop=(kc == NF - 1))
                self.t_sem.inc(w)
            vx_mms.append(self.t_sem.n)

            for pi, (po, n) in enumerate(pieces):
                @blk.tensor
                def _(t, rc=rc, po=po, n=n, pi=pi):
                    if rc == 0 and pi == 0:
                        self.v_sem.wait(t, alpha_done)
                    else:
                        self.v_sem.wait(t)
                    w = t.matmul(pb[4][:, 0:n],
                                 self.REPSs[0:KEXP, rc * 128:(rc + 1) * 128],
                                 self.ARS[0:KEXP, po:po + n],
                                 start=True, stop=True)
                    self.t_sem.inc(w)
                ab_mm = self.t_sem.n

                @blk.vector
                def _(v, rc=rc, po=po, n=n, pi=pi, ab_mm=ab_mm):
                    self.t_sem.wait(v, ab_mm)
                    w = v.tensor_copy(self.SCR[:, 0:n], pb[4][:, 0:n])
                    self.v_sem.chain(v, w)
                    tgt = pb[2 * (rc % 2) + pi]
                    w = v.tensor_tensor(
                        self.VXW[:, rc * BW + po: rc * BW + po + n],
                        tgt[:, 0:n], self.SCR[:, 0:n], OP.mult)
                    self.v_sem.inc(w)
            vx_evs.append(self.v_sem.n)
        vxs_done = self.v_sem.n

        s4_mm = [None] * NF
        s4_ev = [None] * NF
        for fo in range(NF):
            buf = (self.WA, self.WB)[fo % 2]
            bsem = (self.s_wa, self.s_wb)[fo % 2]
            ld = self.stream_w(
                self.i_Uc[fo], buf[:, 0:NKR * 128], bsem,
                prior_t=s4_mm[fo - 2] if fo >= 2 else vx_mms[NKR - 2 + fo])

            @blk.tensor
            def _(t, fo=fo, buf=buf, bsem=bsem, ld=ld):
                bsem.wait(t, ld)
                if fo == 0:
                    self.v_sem.wait(t, vxs_done)
                if fo >= 2:
                    self.v_sem.wait(t, s4_ev[fo - 2])
                for pi, (po, n) in enumerate(pieces):
                    tgt = pb[2 * (fo % 2) + pi]
                    for rc in range(NKR):
                        w = t.matmul(tgt[:, 0:n],
                                     buf[:, rc * 128:(rc + 1) * 128],
                                     self.VXW[:, rc * BW + po: rc * BW + po + n],
                                     start=(rc == 0), stop=(rc == NKR - 1))
                self.t_sem.inc(w)
            s4_mm[fo] = self.t_sem.n

            @blk.vector
            def _(v, fo=fo):
                self.t_sem.wait(v, s4_mm[fo])
                for pi, (po, n) in enumerate(pieces):
                    a0 = col0 + po
                    tgt = pb[2 * (fo % 2) + pi]
                    w = v.scalar_tensor_tensor(
                        self.X[:, fo * T + a0: fo * T + a0 + n],
                        tgt[:, 0:n], scale,
                        self.X[:, fo * T + a0: fo * T + a0 + n],
                        OP.mult, OP.add)
                self.v_sem.inc(w)
            s4_ev[fo] = self.v_sem.n
        self.last_wa_t = s4_mm[NF - 2]
        self.last_wb_t = s4_mm[NF - 1]

    def ph_win_block(self, b):
        blk = self.blk
        pb = self.pb
        col0, ncols, own0, nown = SBLK[b]
        pieces = PIECES[b]
        xn_ready = self.ln_done
        wi_mm = [None] * NFO
        wi_ev = [None] * NFO
        for wo in range(NFO):
            buf = (self.WA, self.WB)[wo % 2]
            bsem = (self.s_wa, self.s_wb)[wo % 2]
            ld = self.stream_w(
                self.i_Win[wo], buf[:], bsem,
                prior_t=wi_mm[wo - 2] if wo >= 2 else
                (self.last_wa_t, self.last_wb_t)[wo % 2])

            @blk.tensor
            def _(t, wo=wo, buf=buf, bsem=bsem, ld=ld):
                bsem.wait(t, ld)
                if wo >= 2 and wi_ev[wo - 2] is not None:
                    vv, cc_ = wi_ev[wo - 2]
                    self.v_sem.wait(t, vv)
                    self.c_sem.wait(t, cc_)
                for pi, (po, n) in enumerate(pieces):
                    tgt = pb[2 * (wo % 2) + pi]
                    for kc in range(NF):
                        if wo == 0 and pi == 0:
                            self.g_sem.wait(t, self.ln_win_g[kc])
                        w = t.matmul(tgt[:, 0:n],
                                     buf[:, kc * 128:(kc + 1) * 128],
                                     self.XNW[:, kc * BW + po: kc * BW + po + n],
                                     start=(kc == 0), stop=(kc == NF - 1))
                self.t_sem.inc(w)
            wi_mm[wo] = self.t_sem.n
            self.win_evac(b, wo, pieces, col0, own0, wi_mm[wo])
            wi_ev[wo] = (self.v_sem.n, self.c_sem.n)
        self.last_wa_t = wi_mm[NFO - 1 - (NFO - 1) % 2]
        self.last_wb_t = wi_mm[NFO - 1 - (NFO % 2)]

    def win_evac(self, b, wo, pieces, col0, own0, mm):
        blk = self.blk
        pb = self.pb
        if wo < NF:
            @blk.scalar
            def _(sc, wo=wo):
                self.t_sem.wait(sc, mm)
                for pi, (po, n) in enumerate(pieces):
                    a0 = col0 + po
                    so, o0, ln = owned_map(a0, n)
                    w = sc.activation(
                        self.SILZB[:, wo * 512 + (o0 - own0):
                                   wo * 512 + (o0 - own0) + ln],
                        pb[2 * (wo % 2) + pi][:, so:so + ln],
                        AF.Silu, bias=self.BMUP[:, wo:wo + 1],
                        scale=self.MUP[:, wo:wo + 1])
                self.c_sem.inc(w)
        elif wo == NFO - 1:
            @blk.vector
            def _(v, wo=wo):
                self.t_sem.wait(v, mm)
                for pi, (po, n) in enumerate(pieces):
                    a0 = col0 + po
                    so, o0, ln = owned_map(a0, n)
                    w = v.tensor_scalar(self.DTLD[0:NH, o0:o0 + ln],
                                        pb[2 * (wo % 2) + pi][0:NH, so:so + ln],
                                        self.MUP[0:NH, wo:wo + 1],
                                        self.BMUP[0:NH, wo:wo + 1],
                                        OP.mult, OP.add)
                self.v_sem.inc(w)
        else:
            cc = wo - NF
            for pi, (po, n) in enumerate(pieces):
                a0 = col0 + po
                tgt = pb[2 * (wo % 2) + pi]
                so, o0, ln = owned_map(a0, n)
                sbase = o0 - a0 + CTX

                @blk.vector
                def _(v, cc=cc, wo=wo, a0=a0, n=n, tgt=tgt, so=so, o0=o0,
                      ln=ln, sbase=sbase):
                    self.t_sem.wait(v, mm)
                    self.c_sem.wait(v)
                    if a0 > 0:
                        w = v.tensor_copy(self.CSTR[:, 0:CTX],
                                          self.CARR[:, cc * 4:cc * 4 + CTX])
                        self.v_sem.chain(v, w)
                    w = v.tensor_scalar(self.CSTR[:, CTX:CTX + n], tgt[:, 0:n],
                                        self.MUP[:, wo:wo + 1],
                                        self.BMUP[:, wo:wo + 1], OP.mult, OP.add)
                    self.v_sem.chain(v, w)
                    if a0 == 0:
                        w = v.tensor_scalar(self.CSTR[:, CTX:2 * CTX],
                                            self.CSTR[:, CTX:2 * CTX],
                                            self.FLAG[:], None, OP.mult)
                        self.v_sem.chain(v, w)
                    w = v.tensor_copy(self.CARR[:, cc * 4:cc * 4 + CTX],
                                      self.CSTR[:, n:n + CTX])
                    self.v_sem.chain(v, w)
                    w = v.tensor_scalar(self.CACC[:, 0:ln],
                                        self.CSTR[:, sbase:sbase + ln],
                                        self.CONVW[:, cc * 4:cc * 4 + 1], None,
                                        OP.mult)
                    self.v_sem.chain(v, w)
                    for k in range(1, 4):
                        w = v.scalar_tensor_tensor(
                            self.CACC[:, 0:ln],
                            self.CSTR[:, sbase + k:sbase + k + ln],
                            self.CONVW[:, cc * 4 + k:cc * 4 + k + 1],
                            self.CACC[:, 0:ln], OP.mult, OP.add)
                        self.v_sem.chain(v, w)
                vdone = self.v_sem.n

                @blk.scalar
                def _(sc, cc=cc, o0=o0, ln=ln, vdone=vdone, own0=own0):
                    self.v_sem.wait(sc, vdone)
                    if cc < NF:
                        dst = self.HW[:, cc * BW + (o0 - own0):
                                      cc * BW + (o0 - own0) + ln]
                    elif cc < 18:
                        dst = self.BTW[:, (cc - NF) * 512 + (o0 - own0):
                                       (cc - NF) * 512 + (o0 - own0) + ln]
                    else:
                        dst = self.CT[:, (cc - 18) * TOWN + o0:
                                      (cc - 18) * TOWN + o0 + ln]
                    w = sc.activation(dst, self.CACC[:, 0:ln], AF.Silu,
                                      bias=self.CONVB[:, cc:cc + 1], scale=1.0)
                    self.c_sem.inc(w)

    def ph_dt_block(self, b):
        blk = self.blk
        col0, ncols, own0, nown = SBLK[b]

        @blk.vector
        def _(v):
            self.c_sem.wait(v)
            self.v_sem.wait(v)
            w = v.tensor_scalar(self.SCR[0:NH, 0:nown],
                                self.DTLD[0:NH, own0:own0 + nown],
                                self.DTB[0:NH, 0:1], 80.0, OP.add, OP.min)
            self.v_sem.inc(w)
        sp1 = self.v_sem.n

        @blk.scalar
        def _(sc):
            self.v_sem.wait(sc, sp1)
            w = sc.activation(self.ABC[0:NH, 0:nown], self.SCR[0:NH, 0:nown],
                              AF.Exp, bias=0.0, scale=1.0)
            self.c_sem.chain(sc, w)
            w = sc.activation(self.DTLD[0:NH, own0:own0 + nown],
                              self.ABC[0:NH, 0:nown],
                              AF.Ln, bias=self.MISC[0:NH, 1:2], scale=1.0)
            self.c_sem.inc(w)
        self.ld_done = self.c_sem.n

    def ph_r_block(self, b):
        blk = self.blk
        pb = self.pb
        c_base = self.c_sem.n
        dec_pred = lambda c4: c_base + 3 * c4 + 1
        ers_pred = lambda c4: c_base + 3 * c4 + 2
        wg2_pred = lambda c4: c_base + 3 * c4 + 3
        if not hasattr(self, "_rb_done"):
            self._rb_done = {}
            self._wgt_done = {}
        nl_dones, rev_dones, wg1s, wgt_tts, nlbs = {}, {}, {}, {}, {}
        tt_prev = None
        for c4 in range(4):
            c = 4 * b + c4

            @blk.vector
            def _(v, c=c, c4=c4, tt_prev=tt_prev):
                self.c_sem.wait(v, self.ld_done)
                if c4 == 0:
                    self.v_sem.wait(v)
                    self.t_sem.wait(v)
                else:
                    self.t_sem.wait(v, tt_prev)
                w = v.tensor_scalar(self.LDC[0:NH, 0:128],
                                    self.DTLD[0:NH, c * 128:c * 128 + 128],
                                    self.MISC[0:NH, 4:5], -1.0, OP.mult, OP.mult)
                self.v_sem.inc(w)
            ldc_done = self.v_sem.n

            @blk.tensor
            def _(t, c=c, c4=c4, ldc_done=ldc_done):
                self.v_sem.wait(t, ldc_done)
                t.transpose(pb[4][:, 0:S], self.DTLD[0:NH, c * 128:c * 128 + 128],
                            self.I128s[0:NH, 0:NH])
                w = t.transpose(pb[4][:, S:2 * S], self.LDC[0:NH, 0:128],
                                self.I128s[0:NH, 0:NH])
                self.t_sem.inc(w)
            tt1 = self.t_sem.n
            tt_prev = tt1

            @blk.vector
            def _(v, c4=c4, tt1=tt1):
                self.t_sem.wait(v, tt1)
                w = v.tensor_copy(self.DTOK[:, c4 * 64:c4 * 64 + 64],
                                  pb[4][:, 0:64])
                self.v_sem.inc(w)
            dtok_done = self.v_sem.n

            @blk.tensor
            def _(t, c4=c4, dtok_done=dtok_done):
                self.v_sem.wait(t, dtok_done)
                t.matmul(pb[5][:, 0:S], self.TRIU[:],
                         self.DTOK[:, c4 * 64 + S:c4 * 64 + 64],
                         start=True, stop=True)
                w = t.matmul(pb[5][0:S, 64:64 + 128],
                             self.DTOK[:, c4 * 64 + S:c4 * 64 + 64],
                             self.TRIU[:], start=True, stop=True)
                self.t_sem.inc(w)
            rmm = self.t_sem.n

            @blk.vector
            def _(v, c=c, c4=c4, rmm=rmm):
                self.t_sem.wait(v, rmm)
                if c4 > 0:
                    self.c_sem.wait(v, ers_pred(c4 - 1))
                    self.s_r.wait(v, self._rb_done[c - 1])
                w = v.tensor_scalar(self.NRT[:, c4 * S:(c4 + 1) * S],
                                    pb[5][:, 0:S], -1.0, None, OP.mult)
                self.v_sem.chain(v, w)
                w = v.tensor_copy(self.RFMs[0:S, 0:Q], pb[5][0:S, 64:64 + 128])
                self.v_sem.inc(w)
            rev_done = self.v_sem.n
            rev_dones[c4] = rev_done

            @blk.tensor
            def _(t, c4=c4, rev_done=rev_done):
                self.v_sem.wait(t, rev_done)
                w = t.matmul(pb[4][0:1, 64:64 + S], self.E127[:],
                             self.NRT[:, c4 * S:(c4 + 1) * S],
                             start=True, stop=True)
                self.t_sem.inc(w)
            nl_mm = self.t_sem.n

            @blk.vector
            def _(v, c4=c4, nl_mm=nl_mm):
                self.t_sem.wait(v, nl_mm)
                if c4 > 0:
                    self.g_sem.wait(v, nlbs[c4 - 1])
                w = v.tensor_copy(self.NLR[0:1, 0:S], pb[4][0:1, 64:64 + S])
                self.v_sem.inc(w)
            nl_done = self.v_sem.n
            nl_dones[c4] = nl_done

            @blk.sync
            def _(s, c=c, c4=c4):
                self.c_sem.wait(s, ers_pred(c4))
                w = s.dma_start(
                    out=self.h_r[c:c + 1, :].rearrange("r (s q) -> (r s) q", s=S),
                    in_=self.RFMs[0:S, 0:Q])
                self.s_r.inc(w, 16)
                w = s.dma_start(
                    out=self.h_er[c:c + 1, :].rearrange("r (s q) -> (r s) q", s=S),
                    in_=self.ERS[0:S, 0:Q])
                self.s_r.inc(w, 16)
            self._rb_done[c] = self.s_r.n

            @blk.gpsimd
            def _(g, c4=c4):
                self.c_sem.wait(g, dec_pred(c4))
                if c4 > 0:
                    self.v_sem.wait(g, wg1s[c4 - 1])
                else:
                    self.v_sem.wait(g)
                w = g.partition_broadcast(self.ABC[:, 512:512 + S],
                                          self.NLR[0:1, 0:S], 128)
                self.g_sem.inc(w)
            nlb_done = self.g_sem.n
            nlbs[c4] = nlb_done

            @blk.vector
            def _(v, c4=c4, nlb_done=nlb_done):
                self.g_sem.wait(v, nlb_done)
                if c4 > 0:
                    self.c_sem.wait(v, wg2_pred(c4 - 1))
                w = v.tensor_tensor(self.SCR[:, 0:S],
                                    self.NRT[:, c4 * S:(c4 + 1) * S],
                                    self.ABC[:, 512:512 + S], OP.subtract)
                self.v_sem.inc(w)
            wg1 = self.v_sem.n
            wg1s[c4] = wg1

            @blk.vector
            def _(v, c=c, c4=c4):
                self.c_sem.wait(v, wg2_pred(c4))
                w = v.tensor_tensor(self.WGT[:, c4 * NH:(c4 + 1) * NH],
                                    self.SCR[:, S:2 * S],
                                    self.DTOK[:, c4 * 64:c4 * 64 + S], OP.mult)
                self.v_sem.inc(w)
            self._wgt_done[c] = self.v_sem.n
            wgt_tts[c4] = self.v_sem.n

        @blk.scalar
        def _(sc):
            for c4 in range(4):
                c = 4 * b + c4
                sc.wait_ge(self.v_sem.h, nl_dones[c4])
                w = sc.activation(self.DECROW[0:1, c * NH:(c + 1) * NH],
                                  self.NLR[0:1, 0:S], AF.Exp, bias=0.0,
                                  scale=-1.0)
                self.c_sem.inc(w)
                sc.wait_ge(self.v_sem.h, rev_dones[c4])
                if c4 > 0:
                    self.s_r.wait(sc, self._rb_done[c - 1])
                w = sc.activation(self.ERS[0:S, 0:Q], self.RFMs[0:S, 0:Q],
                                  AF.Exp, bias=0.0, scale=1.0)
                self.c_sem.inc(w)
                sc.wait_ge(self.v_sem.h, wg1s[c4])
                if c4 > 0:
                    sc.wait_ge(self.v_sem.h, wgt_tts[c4 - 1])
                w = sc.activation(self.SCR[:, S:2 * S], self.SCR[:, 0:S],
                                  AF.Exp, bias=0.0, scale=1.0)
                self.c_sem.inc(w)
        assert self.c_sem.n == c_base + 12

    def ph_ssd_chunk(self, b, c):
        blk = self.blk
        pb = self.pb
        col0, ncols, own0, nown = SBLK[b]
        j0 = c * 128 - own0
        c4 = c % 4
        c_base = self.c_sem.n
        e_pred = lambda g, hp: c_base + 4 * g + hp + 1

        xb = (c % 2) * D
        for jj in range(4):
            @blk.tensor
            def _(t, jj=jj):
                if jj == 0:
                    self.c_sem.wait(t)
                    self.v_sem.wait(t)
                else:
                    self.v_sem.wait(t)
                pbv = pb[4][:, 0:256].bitcast(BF16)
                for fc in range(4 * jj, 4 * jj + 4):
                    w = t.transpose(pbv[:, (fc % 4) * 128:(fc % 4) * 128 + 128],
                                    self.HW[:, fc * BW + j0: fc * BW + j0 + 128],
                                    self.IB128s[:])
                self.t_sem.inc(w)
            xt_mm = self.t_sem.n

            @blk.vector
            def _(v, jj=jj, xt_mm=xt_mm):
                self.t_sem.wait(v, xt_mm)
                w = v.tensor_copy(self.XTOK[:, xb + jj * 512: xb + (jj + 1) * 512],
                                  pb[4][:, 0:256].bitcast(BF16))
                self.v_sem.inc(w)
        xtok_done = self.v_sem.n

        st_bank = pb[6] if c % 2 == 0 else pb[3]
        bw_prev = None
        mt_last = {}
        gt_mms = {}
        rbm_dones = {}
        mtds = {}
        vcur0, ccur0 = self.v_sem.n, self.c_sem.n
        for g in range(G):
            i, gq = g % 4, g // 4
            gt_bank = pb[5] if g % 2 == 0 else pb[2]
            y_bank = pb[7] if g % 2 == 0 else pb[0]

            @blk.tensor
            def _(t, g=g, i=i, gq=gq, bw_prev=bw_prev):
                if g == 0:
                    self.v_sem.wait(t, vcur0)
                    self.c_sem.wait(t, ccur0)
                else:
                    self.v_sem.wait(t, bw_prev)
                w = t.transpose(pb[1][:, 0:S // 2].bitcast(BF16),
                                self.BTW[32 * i:32 * i + 32,
                                         gq * 512 + j0: gq * 512 + j0 + 128],
                                self.IB128s[32 * i:32 * i + 32, 32 * i:32 * i + 32],
                                tile_position=(32 * i, 0))
                self.t_sem.inc(w)
            bt_mm = self.t_sem.n

            @blk.vector
            def _(v, g=g, bt_mm=bt_mm):
                self.t_sem.wait(v, bt_mm)
                self.v_sem.wait(v, self._wgt_done[c])
                w = v.tensor_tensor(
                    self.BW4[:].rearrange("p (a f) -> p a f", a=4),
                    pb[1][:, 0:S // 2].bitcast(BF16)
                        .unsqueeze(1).broadcast_to([128, 4, S]),
                    self.WGT[:, c4 * NH + 4 * g:c4 * NH + 4 * g + 4]
                        .unsqueeze(2).broadcast_to([128, 4, S]),
                    OP.mult)
                self.v_sem.inc(w)
            bw_done = self.v_sem.n
            bw_prev = bw_done

            @blk.tensor
            def _(t, g=g, i=i, gq=gq, gt_bank=gt_bank):
                if g >= 2 and mt_last.get(g - 2):
                    self.v_sem.wait(t, mt_last[g - 2])
                w = t.matmul(gt_bank[:, 0:128],
                             self.BTW[32 * i:32 * i + 32,
                                      gq * 512 + j0: gq * 512 + j0 + 128],
                             self.CT[32 * i:32 * i + 32,
                                     gq * TOWN + c * 128: gq * TOWN + c * 128 + 128],
                             start=True, stop=True,
                             tile_position=(32 * i, 0))
                self.t_sem.inc(w)
            gt_mm = self.t_sem.n
            gt_mms[g] = gt_mm

            if g % 2 == 0:
                @blk.sync
                def _(s, g=g):
                    self.s_r.wait(s, self._rb_done[c])
                    self.g_sem.wait(s)
                    w = s.dma_start(
                        out=self.RCHL0[0:1, :],
                        in_=self.h_r[c:c + 1, (g // 2) * 1024:
                                     (g // 2) * 1024 + 1024])
                    self.s_rq.inc(w, 16)
                self.rq_ld = self.s_rq.n

            @blk.gpsimd
            def _(g_, g=g):
                g_.wait_ge(self.s_rq.h, self.rq_ld)
                self.c_sem.wait(g_, c_base + 4 * g)
                w = g_.partition_broadcast(
                    self.RB4[:],
                    self.RCHL0[0:1, (g % 2) * 512:(g % 2) * 512 + 512], 128)
                self.g_sem.inc(w)
            rb_done = self.g_sem.n

            @blk.vector
            def _(v, g=g, rb_done=rb_done):
                self.g_sem.wait(v, rb_done)
                w = v.tensor_tensor(
                    self.RB4[:].rearrange("p (a f) -> p a f", a=4),
                    self.MASK[:].unsqueeze(1).broadcast_to([128, 4, 128]),
                    self.RB4[:].rearrange("p (a f) -> p a f", a=4),
                    OP.add)
                self.v_sem.inc(w)
            rbm_done = self.v_sem.n
            rbm_dones[g] = rbm_done

            mtd = []
            for hp in range(4):
                h = 4 * g + hp

                @blk.vector
                def _(v, g=g, hp=hp, h=h, gt_bank=gt_bank,
                      gt_mm=gt_mm):
                    self.c_sem.wait(v, e_pred(g, hp))
                    self.t_sem.wait(v, gt_mm)
                    w = v.scalar_tensor_tensor(
                        self.MT[:, hp * 128:(hp + 1) * 128],
                        self.ABC[:, (hp % 2) * 128:(hp % 2) * 128 + 128],
                        self.DTOK[:, c4 * 64 + h:c4 * 64 + h + 1],
                        gt_bank[:, 0:128], OP.mult, OP.mult)
                    self.v_sem.inc(w)
                mt_done = self.v_sem.n
                mtd.append(mt_done)
                mtds[(g, hp)] = mt_done

                @blk.tensor
                def _(t, g=g, i=i, gq=gq, hp=hp, h=h, mt_done=mt_done, xb=xb,
                      y_bank=y_bank):
                    self.v_sem.wait(t, mt_done)
                    self.v_sem.wait(t, xtok_done)
                    pr = hp // 2
                    ph = h % 2
                    t.matmul(y_bank[64 * ph:64 * ph + 64, pr * 128:pr * 128 + 128],
                             self.XTOK[:, xb + h * 64: xb + h * 64 + 64],
                             self.MT[:, hp * 128:(hp + 1) * 128],
                             start=True, stop=True,
                             tile_position=(0, 64 * ph))
                    w = t.matmul(
                        st_bank[32 * i:32 * i + 32,
                                gq * 256 + hp * 64: gq * 256 + hp * 64 + 64],
                        self.BW4[:, hp * S:(hp + 1) * S],
                        self.XTOK[:, xb + h * 64: xb + h * 64 + 64],
                        start=True, stop=True,
                        tile_position=(0, 32 * i))
                    self.t_sem.inc(w)
            mt_last[g] = self.v_sem.n
            y_mm = self.t_sem.n

            @blk.vector
            def _(v, g=g, c=c, j0=j0, y_mm=y_mm, y_bank=y_bank):
                self.t_sem.wait(v, y_mm)
                if g == 0 and c % 4 == 0:
                    self.s_sp.wait(v)
                for pr in range(2):
                    p_ = 2 * g + pr
                    w = v.scalar_tensor_tensor(
                        self.SCR[:, 0:128],
                        self.HW[:, p_ * BW + j0: p_ * BW + j0 + 128],
                        self.DCOL[:, p_:p_ + 1],
                        y_bank[:, pr * 128:pr * 128 + 128], OP.mult, OP.add)
                    self.v_sem.chain(v, w)
                    w = v.tensor_tensor(
                        self.Y1B[:, p_ * 512 + j0: p_ * 512 + j0 + 128],
                        self.SCR[:, 0:128],
                        self.SILZB[:, p_ * 512 + j0: p_ * 512 + j0 + 128],
                        OP.mult)
                    self.v_sem.chain(v, w)

        @blk.scalar
        def _(sc):
            for g in range(G):
                for hp in range(4):
                    h = 4 * g + hp
                    if hp == 0:
                        self.v_sem.wait(sc, rbm_dones[g])
                    if hp >= 2:
                        self.v_sem.wait(sc, mtds[(g, hp - 2)])
                    w = sc.activation(
                        self.ABC[:, (hp % 2) * 128:(hp % 2) * 128 + 128],
                        self.RB4[:, hp * 128:(hp + 1) * 128],
                        AF.Exp, bias=self.NRT[:, c4 * S + h:c4 * S + h + 1],
                        scale=1.0)
                    self.c_sem.inc(w)
        assert self.c_sem.n == c_base + 4 * G

        all_tmm = self.t_sem.n

        dtt_done = self.build_dtt(c)

        @blk.vector
        def _(v, c=c, dtt_done=dtt_done):
            self.v_sem.wait(v, dtt_done)
            self.t_sem.wait(v, all_tmm)
            if c == 0:
                w = v.tensor_copy(self.SST[:, 0:512], st_bank[:, 0:512])
            else:
                w = v.tensor_tensor(self.SCR[:, 0:512],
                                    self.SST[:, (c - 1) * 512: c * 512],
                                    self.DTT[:], OP.mult)
                self.v_sem.chain(v, w)
                w = v.tensor_tensor(self.SST[:, c * 512:(c + 1) * 512],
                                    self.SCR[:, 0:512], st_bank[:, 0:512], OP.add)
            self.v_sem.inc(w)

    def build_dtt(self, c):
        blk = self.blk

        @blk.gpsimd
        def _(g_):
            self.c_sem.wait(g_)
            prev = getattr(self, "_dtt_prev", None)
            if prev is not None:
                self.v_sem.wait(g_, prev)
            else:
                self.v_sem.wait(g_)
                self.g_sem.wait(g_)
            w = g_.partition_broadcast(self.SCR[:, 128:128 + NH],
                                       self.DECROW[0:1, c * NH:(c + 1) * NH], 128)
            self.g_sem.inc(w)
        d128 = self.g_sem.n

        @blk.vector
        def _(v):
            self.g_sem.wait(v, d128)
            base = self.SCR[:]
            for i in range(4):
                src_ap = bass.AP(base.tensor, 32 * i * 512 + 4 * i + 128,
                                 [[512, 32], [16, 2], [1, 4], [0, 64]])
                w = v.tensor_copy(
                    self.DTT[32 * i:32 * i + 32, :]
                        .rearrange("p (a b f) -> p a b f", a=2, b=4),
                    src_ap)
                self.v_sem.chain(v, w)
        self._dtt_prev = self.v_sem.n
        return self.v_sem.n

    def ph_spill_block(self, b):
        blk = self.blk
        col0, ncols, own0, nown = SBLK[b]
        vn, cn = self.v_sem.n, self.c_sem.n

        @blk.sync
        def _(s):
            self.v_sem.wait(s, vn)
            self.c_sem.wait(s, cn)
            w = s.dma_start(
                out=self.h_silz.rearrange("(c p) t -> p c t", p=128)
                    [:, :, own0:own0 + 512],
                in_=self.SILZB[:].rearrange("p (c t) -> p c t", c=NF))
            self.s_sp.inc(w, 16)
            w = s.dma_start(
                out=self.h_y1.rearrange("(c p) t -> p c t", p=128)
                    [:, :, own0:own0 + 512],
                in_=self.Y1B[:].rearrange("p (c t) -> p c t", c=NF))
            self.s_sp.inc(w, 16)

    def ph_y1_load(self, b):
        blk = self.blk
        col0, ncols, own0, nown = SBLK[b]
        vn = self.v_sem.n

        @blk.sync
        def _(s):
            self.v_sem.wait(s, vn)
            w = s.dma_start(
                out=self.Y1B[:].rearrange("p (c t) -> p c t", c=NF),
                in_=self.h_y1.rearrange("(c p) t -> p c t", p=128)
                    [:, :, own0:own0 + 512])
            self.s_sp.inc(w, 16)
        self.y1_ld = self.s_sp.n

    def ph_state_collective(self):
        blk = self.blk

        @blk.vector
        def _(v):
            self.v_sem.wait(v)
            w = v.tensor_scalar(self.SCR[:, 0:512], self.SST[:, 7 * 512:8 * 512],
                                self.FLAGI[:], None, OP.mult)
            self.v_sem.inc(w)
        contrib = self.v_sem.n

        @blk.scalar
        def _(sc):
            self.v_sem.wait(sc, contrib)
            w = sc.dma_start(out=self.h_ccin, in_=self.SCR[:, 0:512])
            self.s_r.inc(w, 16)
        ccw = self.s_r.n

        @blk.gpsimd
        def _(g_):
            g_.wait_ge(self.s_r.h, ccw)
            w = g_.collective_compute(
                "AllReduce", OP.add,
                replica_groups=[[0, 1], [2, 3], [4, 5], [6, 7]],
                ins=[self.h_ccin], outs=[self.h_ccout])
            self.g_sem.inc(w)
        cc_done = self.g_sem.n

        self.cc_done = cc_done

    def ph_cc_recv(self):
        blk = self.blk

        @blk.scalar
        def _(sc):
            self.g_sem.wait(sc, self.cc_done)
            w = sc.dma_start(out=self.SCAR[:], in_=self.h_ccout)
            self.s_r.inc(w, 16)
        scar_ld = self.s_r.n

        @blk.vector
        def _(v):
            v.wait_ge(self.s_r.h, scar_ld)
            w = v.tensor_scalar(self.SCAR[:], self.SCAR[:], self.FLAG[:],
                                None, OP.mult)
            self.v_sem.inc(w)

    def ph_epi_chunk(self, b, c):
        blk = self.blk
        pb = self.pb
        own0 = SBLK[b][2]
        j0 = c * 128 - own0
        if not hasattr(self, "_epi_cdt_prev"):
            self._epi_cdt_prev = [None, None]
            self._epi_yi_prev = [None, None]
            self._epi_chunk_yi = [None, None]
            self._epi_ev_g3 = None
            self._epi_ev_chunk = None
            self._epi_pending = []

        silzc = self.SILZC
        sused = (self.SUSED, self.SUSED2)[c % 2]

        @blk.sync
        def _(s, c=c, silzc=silzc):
            prev = self._epi_ev_g3
            if prev is not None:
                self.v_sem.wait(s, prev)
            elif c == 0:
                self.v_sem.wait(s)
            if c == 4:
                self.t_sem.wait(s)
            w = s.dma_start(
                out=silzc[:, 0:8 * 128].rearrange("p (c t) -> p c t", c=8),
                in_=self.h_silz.rearrange("(c p) t -> p c t", p=128)
                    [:, 0:8, c * 128:c * 128 + 128])
            self.s_sz.inc(w, 16)
        silz_lo = self.s_sz.n
        silz_hi = None

        dtt_done = self.build_dtt(c)

        @blk.vector
        def _(v, c=c, dtt_done=dtt_done, sused=sused):
            self.v_sem.wait(v, dtt_done)
            prev_yi = self._epi_chunk_yi[c % 2]
            if prev_yi is not None:
                self.t_sem.wait(v, prev_yi)
            if c == 0:
                w = v.tensor_copy(sused[:], self.SCAR[:])
            else:
                w = v.tensor_tensor(sused[:], self.SCAR[:],
                                    self.SST[:, (c - 1) * 512: c * 512], OP.add)
            self.v_sem.chain(v, w)
            w = v.tensor_tensor(self.SCAR[:], self.SCAR[:], self.DTT[:], OP.mult)
            self.v_sem.inc(w)
        sused_done = self.v_sem.n

        YROT = (pb[7], pb[0], pb[1], pb[2])
        for g in range(G):
            i, gq = g % 4, g // 4

            if g == 2:
                @blk.sync
                def _(s, c=c, silzc=silzc):
                    prev = self._epi_ev_chunk
                    if prev is not None:
                        self.v_sem.wait(s, prev)
                    elif c == 0:
                        self.v_sem.wait(s)
                    if c == 4:
                        self.t_sem.wait(s)
                    w = s.dma_start(
                        out=silzc[:, 8 * 128:16 * 128]
                            .rearrange("p (c t) -> p c t", c=8),
                        in_=self.h_silz.rearrange("(c p) t -> p c t", p=128)
                            [:, 8:16, c * 128:c * 128 + 128])
                    self.s_sz2.inc(w, 16)
                silz_hi = self.s_sz2.n

            if g % 2 == 0:
                @blk.scalar
                def _(sc, g=g, c=c):
                    self.g_sem.wait(sc)
                    w = sc.dma_start(
                        out=self.RCHL0[0:1, :],
                        in_=self.h_er[c:c + 1, (g // 2) * 1024:
                                      (g // 2) * 1024 + 1024])
                    self.s_rq.inc(w, 16)
                self.rq_ld = self.s_rq.n

            ebt_b = (self.EBT, self.EBT2)[g % 2]
            cdt_b = (self.CDT, self.CDT2)[g % 2]

            @blk.gpsimd
            def _(g_, g=g, ebt_b=ebt_b):
                g_.wait_ge(self.s_rq.h, self.rq_ld)
                prev = self._epi_cdt_prev[g % 2]
                if prev is None:
                    self.v_sem.wait(g_)
                else:
                    self.v_sem.wait(g_, prev)
                w = g_.partition_broadcast(
                    ebt_b[:],
                    self.RCHL0[0:1, (g % 2) * 512:(g % 2) * 512 + 512], 128)
                self.g_sem.inc(w)
            eb_done = self.g_sem.n

            @blk.vector
            def _(v, g=g, i=i, gq=gq, c=c, eb_done=eb_done, ebt_b=ebt_b,
                  cdt_b=cdt_b):
                self.g_sem.wait(v, eb_done)
                prev_yi = self._epi_yi_prev[g % 2]
                if prev_yi is not None:
                    self.t_sem.wait(v, prev_yi)
                w = v.tensor_tensor(
                    cdt_b[32 * i:32 * i + 32, :]
                        .rearrange("p (a f) -> p a f", a=4),
                    self.CT[32 * i:32 * i + 32,
                            gq * TOWN + c * 128: gq * TOWN + c * 128 + 128]
                        .unsqueeze(1).broadcast_to([32, 4, 128]),
                    ebt_b[32 * i:32 * i + 32, :]
                        .rearrange("p (a f) -> p a f", a=4),
                    OP.mult)
                self.v_sem.inc(w)
            cdt_done = self.v_sem.n
            self._epi_cdt_prev[g % 2] = cdt_done

            ye_bank = YROT[g % 4]

            @blk.tensor
            def _(t, g=g, i=i, gq=gq, cdt_done=cdt_done, ye_bank=ye_bank,
                  cdt_b=cdt_b, sused=sused):
                self.v_sem.wait(t, cdt_done)
                self.v_sem.wait(t, sused_done)
                for hp in range(4):
                    h = 4 * g + hp
                    pr, ph = hp // 2, h % 2
                    w = t.matmul(
                        ye_bank[64 * ph:64 * ph + 64, pr * 128:pr * 128 + 128],
                        sused[32 * i:32 * i + 32,
                              gq * 256 + hp * 64: gq * 256 + hp * 64 + 64],
                        cdt_b[32 * i:32 * i + 32, hp * 128:(hp + 1) * 128],
                        start=True, stop=True,
                        tile_position=(32 * i, 64 * ph))
                self.t_sem.inc(w)
            yi_mm = self.t_sem.n
            self._epi_yi_prev[g % 2] = yi_mm
            if g == G - 1:
                self._epi_chunk_yi[c % 2] = yi_mm

            def emit_evac(g=g, c=c, j0=j0, yi_mm=yi_mm,
                          silz_ld=(silz_lo if g < 4 else silz_hi),
                          ye_bank=ye_bank, silzc=silzc):
                @blk.vector
                def _(v):
                    self.t_sem.wait(v, yi_mm)
                    (self.s_sz if g < 4 else self.s_sz2).wait(v, silz_ld)
                    self.s_sp.wait(v, self.y1_ld)
                    for pr in range(2):
                        p_ = 2 * g + pr
                        sl = slice(p_ * 512 + j0, p_ * 512 + j0 + 128)
                        w = v.tensor_tensor(self.SCR[:, 0:128],
                                            ye_bank[:, pr * 128:pr * 128 + 128],
                                            silzc[:, p_ * 128:p_ * 128 + 128],
                                            OP.mult)
                        self.v_sem.chain(v, w)
                        w = v.tensor_tensor(self.Y1B[:, sl], self.Y1B[:, sl],
                                            self.SCR[:, 0:128], OP.add)
                        self.v_sem.chain(v, w)
                if g == 3:
                    self._epi_ev_g3 = self.v_sem.n
                elif g == G - 1:
                    self._epi_ev_chunk = self.v_sem.n

            self._epi_pending.append(emit_evac)
            while len(self._epi_pending) > 2:
                self._epi_pending.pop(0)()

    def ph_epi_flush(self):
        while self._epi_pending:
            self._epi_pending.pop(0)()

    def ph_norm_wout_block(self, b):
        blk = self.blk
        pb = self.pb
        col0, ncols, own0, nown = SBLK[b]

        for fc in range(NF):
            @blk.vector
            def _(v, fc=fc):
                if fc == 0:
                    self.v_sem.wait(v)
                else:
                    self.t_sem.wait(v)
                w = v.tensor_tensor(self.YSQ[:, 0:512],
                                    self.Y1B[:, fc * 512:(fc + 1) * 512],
                                    self.Y1B[:, fc * 512:(fc + 1) * 512],
                                    OP.mult)
                self.v_sem.inc(w)
            ysq_done = self.v_sem.n

            @blk.tensor
            def _(t, fc=fc, ysq_done=ysq_done):
                self.v_sem.wait(t, ysq_done)
                w = t.matmul(pb[6][0:KEXP, 0:512],
                             self.GINDs[:, fc * KEXP:(fc + 1) * KEXP],
                             self.YSQ[:, 0:512],
                             start=(fc == 0), stop=(fc == NF - 1))
                self.t_sem.inc(w)
        var_mm = self.t_sem.n

        @blk.vector
        def _(v):
            self.t_sem.wait(v, var_mm)
            w = v.tensor_scalar(self.RSTD[0:KEXP, 0:512], pb[6][0:KEXP, 0:512],
                                1.0 / 256.0, None, OP.mult)
            self.v_sem.inc(w)
        var_done = self.v_sem.n

        @blk.scalar
        def _(sc):
            self.v_sem.wait(sc, var_done)
            w = sc.activation(self.RSTD[0:KEXP, 0:512], self.RSTD[0:KEXP, 0:512],
                              AF.Ln, bias=self.MISC[0:KEXP, 0:1], scale=1.0)
            self.c_sem.chain(sc, w)
            w = sc.activation(self.RSTD[0:KEXP, 0:512], self.RSTD[0:KEXP, 0:512],
                              AF.Exp, bias=0.0, scale=-0.5)
            self.c_sem.inc(w)
        rstd_done0 = self.c_sem.n

        @blk.vector
        def _(v):
            self.c_sem.wait(v, rstd_done0)
            w = v.tensor_copy(self.RSTDB[0:KEXP, 0:512], self.RSTD[0:KEXP, 0:512])
            self.v_sem.inc(w)
        rstd_done = self.v_sem.n

        for fc in range(NF):
            @blk.tensor
            def _(t, fc=fc):
                if fc == 0:
                    self.v_sem.wait(t, rstd_done)
                else:
                    self.v_sem.wait(t)
                w = t.matmul(pb[4 + fc % 2][:, 0:512],
                             self.GSELs[0:KEXP, (fc // 2) * 128:(fc // 2) * 128 + 128],
                             self.RSTDB[0:KEXP, 0:512], start=True, stop=True)
                self.t_sem.inc(w)
            rb_mm = self.t_sem.n

            @blk.vector
            def _(v, fc=fc, rb_mm=rb_mm):
                self.t_sem.wait(v, rb_mm)
                w = v.scalar_tensor_tensor(
                    self.XNW[:, fc * BW: fc * BW + 512],
                    self.Y1B[:, fc * 512:(fc + 1) * 512],
                    self.NW[:, fc:fc + 1],
                    pb[4 + fc % 2][:, 0:512], OP.mult, OP.mult)
                self.v_sem.inc(w)
        yn_done = self.v_sem.n

        wo_mm = [None] * NF
        wo_ev = [None] * NF
        for wo in range(NF):
            buf = (self.WA, self.WB)[wo % 2]
            bsem = (self.s_wa, self.s_wb)[wo % 2]
            ld = self.stream_w(
                self.i_Wout[wo], buf[:], bsem,
                prior_t=wo_mm[wo - 2] if wo >= 2 else
                (self.last_wa_t, self.last_wb_t)[wo % 2])

            @blk.tensor
            def _(t, wo=wo, buf=buf, bsem=bsem, ld=ld):
                bsem.wait(t, ld)
                if wo == 0:
                    self.v_sem.wait(t, yn_done)
                if wo >= 2:
                    self.v_sem.wait(t, wo_ev[wo - 2])
                tgt = pb[2 * (wo % 2)]
                for kc in range(NF):
                    w = t.matmul(tgt[:, 0:512],
                                 buf[:, kc * 128:(kc + 1) * 128],
                                 self.XNW[:, kc * BW: kc * BW + 512],
                                 start=(kc == 0), stop=(kc == NF - 1))
                self.t_sem.inc(w)
            wo_mm[wo] = self.t_sem.n

            @blk.vector
            def _(v, wo=wo):
                self.t_sem.wait(v, wo_mm[wo])
                tgt = pb[2 * (wo % 2)]
                a = wo * T + CTX + own0
                w = v.scalar_tensor_tensor(self.X[:, a:a + 512], tgt[:, 0:512],
                                           self.BOUT[:, wo:wo + 1],
                                           self.X[:, a:a + 512], OP.add, OP.add)
                self.v_sem.inc(w)
            wo_ev[wo] = self.v_sem.n
        self.last_wa_t = wo_mm[NF - 2]
        self.last_wb_t = wo_mm[NF - 1]

    def ph_store_block(self, b):
        blk = self.blk
        col0, ncols, own0, nown = SBLK[b]
        vn = self.v_sem.n

        @blk.sync
        def _(s):
            self.v_sem.wait(s, vn)
            w = s.dma_start(
                out=self.o_out.rearrange("(c p) t -> p c t", p=128)
                    [:, :, own0:own0 + nown],
                in_=self.X[:].rearrange("p (c t) -> p c t", c=NF)
                    [:, :, CTX + own0:CTX + own0 + nown])
            self.s_st.inc(w, 16)

    def ph_finish(self):
        blk = self.blk

        @blk.sync
        def _(s):
            self.s_st.wait(s)
            if self.g_ld.n:
                self.g_ld.wait(s)



def _host_prepare(inputs):
    x = np.asarray(inputs["L0_attn_hidden_states"])
    Win = np.asarray(inputs["L0_attn_W_in"])
    b_in = np.asarray(inputs["L0_attn_b_in"])
    mup = np.asarray(inputs["L0_attn_mup_vector"])
    Win_pad = np.zeros((D, PROJ_PAD), np.float32)
    Win_pad[:, :PROJ] = Win
    b_pad = np.zeros((PROJ_PAD,), np.float32)
    b_pad[:PROJ] = b_in
    mup_pad = np.zeros((PROJ_PAD,), np.float32)
    mup_pad[:PROJ] = mup
    bmup = np.ascontiguousarray((b_pad * mup_pad).reshape(NFO, 128).T)
    mupc = np.ascontiguousarray(mup_pad.reshape(NFO, 128).T)

    V = np.asarray(inputs["L0_ffn_V"])
    U = np.asarray(inputs["L0_ffn_U"])
    Vc = np.ascontiguousarray(V.transpose(2, 0, 1).reshape(D, KR))
    Uc = np.ascontiguousarray(U.transpose(0, 2, 1).reshape(KR, D))

    conv_w = np.asarray(inputs["L0_attn_conv_w"])
    convw = np.ascontiguousarray(
        conv_w.reshape(NCONV, 128, 4).transpose(1, 0, 2).reshape(128, NCONV * 4))
    convb = np.ascontiguousarray(
        np.asarray(inputs["L0_attn_conv_b"]).reshape(NCONV, 128).T)

    def col(vec, n):
        return np.ascontiguousarray(np.asarray(vec).reshape(n, 128).T)

    REPS = np.zeros((KEXP, NKR * 128), np.float32)
    for rc in range(NKR):
        for p in range(128):
            REPS[2 * rc + p // 64, rc * 128 + p] = 1.0
    GSEL = np.zeros((KEXP, G * 128), np.float32)
    for g in range(G):
        GSEL[g, g * 128:(g + 1) * 128] = 1.0
    import ml_dtypes
    GIND = np.zeros((128, NF * KEXP), np.float32)
    for fc in range(NF):
        GIND[:, fc * KEXP + fc // 2] = 1.0
    MISCC = np.zeros((128, 8), np.float32)
    MISCC[:, 0] = 1e-5
    MISCC[:, 1] = 1.0
    MISCC[:, 2] = 80.0
    MISCC[:, 3] = 0.5
    E127 = np.zeros((128, 1), np.float32)
    E127[127, 0] = 1.0

    def tile_lhsT(W, nk, nm):
        return np.ascontiguousarray(
            W.reshape(nk, 128, nm, 128).transpose(2, 1, 0, 3).reshape(nm, 128, nk * 128))

    common = dict(
        Win=tile_lhsT(Win_pad, NF, NFO),
        W1=tile_lhsT(np.asarray(inputs["L0_ffn_W1"]), NF, NF),
        W2=np.ascontiguousarray(
            np.asarray(inputs["L0_ffn_W2"]).reshape(NF, 128, KEXP)
            .transpose(1, 0, 2).reshape(128, NF * KEXP)),
        Wout=tile_lhsT(np.asarray(inputs["L0_attn_W_out"]), NF, NF),
        Vc=tile_lhsT(Vc, NF, NKR), Uc=tile_lhsT(Uc, NKR, NF),
        bmup=bmup, mup=mupc,
        b1col=col(inputs["L0_ffn_b1"], NF),
        b2col=np.asarray(inputs["L0_ffn_b2"]).reshape(KEXP, 1).astype(np.float32),
        boutcol=col(inputs["L0_attn_b_out"], NF),
        nwcol=col(inputs["L0_attn_norm_w"], NF),
        convbcol=convb, convwcol=convw,
        Dcol=np.ascontiguousarray(
            np.repeat(np.asarray(inputs["L0_attn_D_param"]), PDIM)
            .reshape(NF, 128).T).astype(np.float32),
        Alogcol=np.asarray(inputs["L0_attn_A_log"]).reshape(NH, 1).astype(np.float32),
        dtbcol=np.asarray(inputs["L0_attn_dt_bias"]).reshape(NH, 1).astype(np.float32),
        maskinf=np.where(np.arange(Q)[:, None] <= np.arange(Q)[None, :], 0.0,
                         -1e30).astype(np.float32),
        triu01=np.where(np.arange(Q)[:, None] <= np.arange(Q)[None, :], 1.0,
                        0.0).astype(np.float32),
        I128=np.eye(128, dtype=np.float32),
        IB128=np.eye(128, dtype=np.float32).astype(ml_dtypes.bfloat16),
        e127=E127, misccol=MISCC, REPS=REPS, GSEL=GSEL.astype(ml_dtypes.bfloat16),
        GIND=GIND.astype(ml_dtypes.bfloat16),
    )

    in_maps = []
    for cidx in range(8):
        bb, hf = cidx // 2, cidx % 2
        lo = hf * TOWN
        xs = x[bb]
        if hf == 0:
            ctx_cols = np.zeros((CTX, D), np.float32)
        else:
            ctx_cols = xs[lo - CTX:lo]
        xl = np.concatenate([ctx_cols, xs[lo:lo + TOWN]], axis=0)
        m = dict(common)
        m["x_fm"] = np.ascontiguousarray(xl.T)
        m["flagcol"] = np.full((128, 1), 1.0 if hf == 1 else 0.0, np.float32)
        m["flaginvcol"] = np.full((128, 1), 0.0 if hf == 1 else 1.0, np.float32)
        in_maps.append(m)
    return in_maps


def _run(debug=()):
    b = Build(debug)
    nc = b.build()
    lower_extended_insts(nc)
    return b, nc


def kernel_dev(**inputs):
    in_maps = _host_prepare(inputs)
    b, nc = _run(_DEBUG)
    res = run_bass_kernel_spmd(nc, in_maps, core_ids=list(range(8)))
    dumps = {}
    for dram_name, name in b.dumps:
        dumps[name] = [res.results[c][dram_name] for c in range(8)]
    return _assemble(res), dumps


def _assemble(res):
    out = np.empty((4, 2048, D), np.float32)
    for c in range(8):
        b, hf = c // 2, c % 2
        out[b, hf * TOWN:(hf + 1) * TOWN] = res.results[c]["out_fm"].T
    return out


def kernel(**inputs):
    in_maps = _host_prepare(inputs)
    b, nc = _run(())
    res = run_bass_kernel_spmd(nc, in_maps, core_ids=list(range(8)))
    return _assemble(res)


# version 11
# speedup vs baseline: 1.0097x; 1.0036x over previous
import numpy as np
from contextlib import ExitStack
import concourse.bass as bass
import concourse.mybir as mybir
from concourse.bass_utils import run_bass_kernel_spmd
from concourse import library_config
from concourse.library_overlay import lower_extended_insts

F32 = mybir.dt.float32
BF16 = mybir.dt.bfloat16
AF = mybir.ActivationFunctionType
OP = mybir.AluOpType
AXX = mybir.AxisListType.X

D = 2048
TOWN = 1024
CTX = 3
T = TOWN + CTX
NF = 16
NFO = 37
PROJ = 4640
PROJ_PAD = NFO * 128
NH, PDIM, G, S = 32, 64, 8, 32
NCONV = 20
NCHUNK = 8
Q = 128
KEXP, RRANK = 8, 64
KR = 512
NKR = 4
BW = 520

SBLK = [(0, 515, 0, 512), (515, 512, 512, 512)]
PIECES = {0: [(0, 387), (387, 128)], 1: [(0, 512)]}

_DEBUG = ()


class SemT:
    def __init__(self, nc, ctx, name):
        self.h = ctx.enter_context(nc.semaphore(name))
        self.n = 0

    def inc(self, bi, k=1):
        bi.then_inc(self.h, k)
        self.n += k
        return self.n

    def wait(self, eng, n=None):
        eng.wait_ge(self.h, self.n if n is None else n)

    def chain(self, eng, bi):
        c = self.inc(bi)
        eng.wait_ge(self.h, c)


def owned_map(a0, n):
    so = max(0, CTX - a0)
    return so, a0 + so - CTX, n - so


class Build:
    def __init__(self, debug=()):
        self.debug = tuple(debug)
        self.nc = bass.Bass("TRN2", target_bir_lowering=False, debug=False)
        self.ctx = ExitStack()
        self.dumps = []

    def sb(self, name, shape, dt=F32):
        return self.ctx.enter_context(self.nc.sbuf_tensor(name, shape, dt))

    def sem(self, name):
        return SemT(self.nc, self.ctx, name)

    def din(self, name, shape, dt=F32):
        return self.nc.dram_tensor(name, shape, dt, kind="ExternalInput").ap()

    def dout(self, name, shape, dt=F32):
        return self.nc.dram_tensor(name, shape, dt, kind="ExternalOutput").ap()

    def want(self, name):
        return name in self.debug

    def dump(self, name, tile, width):
        if not self.want(name):
            return
        nparts = tile[:].shape[0]
        dr = self.dout(f"dbg_{name}", [nparts, width], F32)
        blk = self.blk
        vn, tn, cn, gn = self.v_sem.n, self.t_sem.n, self.c_sem.n, self.g_sem.n

        @blk.gpsimd
        def _(g):
            self.v_sem.wait(g, vn)
            self.t_sem.wait(g, tn)
            self.c_sem.wait(g, cn)
            self.g_sem.wait(g, gn)
            bi = g.dma_start(out=dr, in_=tile[:, 0:width])
            self.g_ld.inc(bi, 16)
        self.dumps.append((f"dbg_{name}", name))

    def barrier(self, engines):
        blk = self.blk
        sems = [self.v_sem, self.t_sem, self.c_sem, self.g_sem]
        cts = [s.n for s in sems]

        spn = self.s_sp.n

        def mk(e):
            for s, c in zip(sems, cts):
                s.wait(e, c)
            e.wait_ge(self.s_sp.h, spn)
        for en in engines:
            if en == "t":
                @blk.tensor
                def _(t):
                    mk(t)
            elif en == "v":
                @blk.vector
                def _(v):
                    mk(v)
            elif en == "c":
                @blk.scalar
                def _(sc):
                    mk(sc)
            elif en == "g":
                @blk.gpsimd
                def _(g):
                    mk(g)

    def build(self):
        nc = self.nc
        self.declare_io()
        self.alloc()
        self.blk = self.ctx.enter_context(nc.Block())
        self.ph_init()
        with ExitStack() as ffn1_scope:
            self._scope = ffn1_scope
            self.VXW = self.scoped("VXW", [128, NKR * BW], BF16)
            for b in range(2):
                self.ph_ln_block(b, self.X, self.XNW)
                self.ph_ffn_block(b, 0.5)
            self.barrier("tvcg")
        self.dump("x1", self.X, NF * T)
        with ExitStack() as prep_scope:
            self._scope = prep_scope
            self.DTLD = self.scoped("DTLD", [NH, TOWN], F32)
            self.LDC = self.scoped("LDC", [NH, 128], F32)
            self.DTOK = self.scoped("DTOK", [128, 4 * 64], F32)
            self.NRT = self.scoped("NRT", [128, 4 * S], F32)
            self.WGT = self.scoped("WGT", [128, 4 * NH], F32)
            self.NLR = self.scoped("NLR", [1, 64], F32)
            self.BTW = self.scoped("BTW", [128, 2 * 512], BF16)
            self.CARR = self.scoped("CARR", [128, NCONV * 4], F32)
            self.CACC = self.scoped("CACC", [128, 512], F32)
            self.CSTR = self.scoped("CSTR", [128, 528], F32)
            self.XTOK = self.scoped("XTOK", [128, 2 * D], BF16)
            self.BW4 = self.scoped("BW4", [128, 128], BF16)
            self.ERS = self.scoped("ERS", [S, Q], F32)
            self.RB4 = self.scoped("RB4", [128, 512], F32)
            self.MT = self.ABC[:, 256:512].bitcast(BF16)
            self.RFMs = self.CSTR
            self.DTT = self.CACC
            for b in range(2):
                self.barrier("tvcg")
                self.ph_ln_block(b, self.X, self.XNW)
                self.ph_win_block(b)
                self.ph_dt_block(b)
                self.ph_r_block(b)
                for c in range(4 * b, 4 * b + 4):
                    self.ph_ssd_chunk(b, c)
                self.ph_spill_block(b)
            self.dump("dtf", self.DTLD, TOWN)
            self.dump("convx", self.HW, NF * BW)
            self.dump("ct", self.CT, 2 * TOWN)
            self.dump("sst", self.SST, NCHUNK * 512)
            self.barrier("tvcg")
        self.ph_state_collective()
        with ExitStack() as epi_scope:
            self._scope = epi_scope
            self.SCAR = self.scoped("SCAR", [128, 512], F32)
            self.EBT = self.scoped("EBT", [128, 512], F32)
            self.CDT = self.scoped("CDT", [128, 512], BF16)
            self.CDT2 = self.scoped("CDT2", [128, 512], BF16)
            self.EBT2 = self.scoped("EBT2", [128, 512], F32)
            self.DTT = self.scoped("DTT2", [128, 512], F32)
            self.SUSED = self.scoped("SUSED", [128, 512], BF16)
            self.SUSED2 = self.scoped("SUSED2", [128, 512], BF16)
            self.RCHL1 = self.scoped("RCHL1", [1, 1024], F32)
            self.RSTD = self.scoped("RSTD", [KEXP, 512], F32)
            self.RSTDB = self.scoped("RSTDB", [KEXP, 512], BF16)
            self.UNB = self.scoped("UNB", [128, NKR * BW], BF16)
            self.SILZC = self.UNB
            self.VXW = self.UNB
            self.ph_cc_recv()
            for b in range(2):
                self.barrier("tvcg")
                self.ph_y1_load(b)
                for c in range(4 * b, 4 * b + 4):
                    self.ph_epi_chunk(b, c)
                self.ph_epi_flush()
                if b == 0:
                    self.dump("ebt", self.EBT, 512)
                    self.dump("scar", self.SCAR, 512)
                    self.dump("y1b0", self.Y1B, NF * 512)
                    self.dump("sused", self.SUSED, 512)
                    self.dump("cdt", self.CDT, 512)
                self.ph_norm_wout_block(b)
                if b == 0:
                    self.dump("rstd", self.RSTD, 512)
                    self.dump("ynw", self.XNW, NF * BW)
                if b == 1:
                    self.dump("x2", self.X, NF * T)
                self.barrier("tvcg")
                self.ph_ln_block(b, self.X, self.XNW)
                self.ph_ffn_block(b, 0.5)
                self.ph_store_block(b)
        self.ph_finish()
        self.ctx.close()
        return nc

    def scoped(self, name, shape, dt):
        return self._scope.enter_context(self.nc.sbuf_tensor(name, shape, dt))

    def declare_io(self):
        d = self.din
        self.i_x = d("x_fm", [D, T])
        self.i_Win = d("Win", [NFO, 128, NF * 128])
        self.i_W1 = d("W1", [NF, 128, NF * 128])
        self.i_W2 = d("W2", [128, NF * KEXP])
        self.i_Wout = d("Wout", [NF, 128, NF * 128])
        self.i_Vc = d("Vc", [NKR, 128, NF * 128])
        self.i_Uc = d("Uc", [NF, 128, NKR * 128])
        self.i_bmup = d("bmup", [128, NFO])
        self.i_mup = d("mup", [128, NFO])
        self.i_b1 = d("b1col", [128, NF])
        self.i_b2 = d("b2col", [KEXP, 1])
        self.i_bout = d("boutcol", [128, NF])
        self.i_nw = d("nwcol", [128, NF])
        self.i_convb = d("convbcol", [128, NCONV])
        self.i_convw = d("convwcol", [128, NCONV * 4])
        self.i_Dcol = d("Dcol", [128, NF])
        self.i_Alog = d("Alogcol", [NH, 1])
        self.i_dtb = d("dtbcol", [NH, 1])
        self.i_mask = d("maskinf", [Q, Q])
        self.i_triu = d("triu01", [Q, Q])
        self.i_I128 = d("I128", [128, 128])
        self.i_IB128 = d("IB128", [128, 128], BF16)
        self.i_e127 = d("e127", [128, 1])
        self.i_flag = d("flagcol", [128, 1])
        self.i_flaginv = d("flaginvcol", [128, 1])
        self.i_misc = d("misccol", [128, 8])
        self.i_REPS = d("REPS", [KEXP, NKR * 128])
        self.i_GSEL = d("GSEL", [KEXP, G * 128], BF16)
        self.i_GIND = d("GIND", [128, NF * KEXP], BF16)
        self.h_r = self.nc.dram_tensor("r_scratch", [NCHUNK, NH * Q], F32,
                                       kind="Internal").ap()
        self.h_er = self.nc.dram_tensor("er_scratch", [NCHUNK, NH * Q], F32,
                                        kind="Internal").ap()
        self.h_silz = self.nc.dram_tensor("silz_hbm", [D, TOWN], BF16,
                                          kind="Internal").ap()
        self.h_y1 = self.nc.dram_tensor("y1_hbm", [D, TOWN], BF16,
                                        kind="Internal").ap()
        self.h_ccin = self.nc.dram_tensor("cc_in", [128, 512], F32,
                                          kind="Internal").ap()
        self.h_ccout = self.nc.dram_tensor("cc_out", [128, 512], F32,
                                           kind="Internal").ap()
        self.o_out = self.dout("out_fm", [D, TOWN])

    def alloc(self):
        sb = self.sb
        self.X = sb("X", [128, NF * T], F32)
        self.SILZB = sb("SILZB", [128, NF * 512], BF16)
        self.Y1B = sb("Y1B", [128, NF * 512], BF16)
        self.SST = sb("SST", [128, NCHUNK * 512], BF16)
        self.CT = sb("CT", [128, 2 * TOWN], BF16)
        self.DECROW = sb("DECROW", [1, NCHUNK * NH], F32)
        self.WA = sb("WA", [128, NF * 128], BF16)
        self.WB = sb("WB", [128, NF * 128], BF16)
        self.W2B = sb("W2B", [128, NF * KEXP], BF16)
        self.MASK = sb("MASK", [Q, Q], F32)
        self.TRIU = sb("TRIU", [Q, Q], F32)
        self.I128s = sb("I128s", [128, 128], F32)
        self.IB128s = sb("IB128s", [128, 128], BF16)
        self.E127 = sb("E127", [128, 1], F32)
        self.ONESB = sb("ONESB", [128, 1], BF16)
        self.FLAG = sb("FLAG", [128, 1], F32)
        self.FLAGI = sb("FLAGI", [128, 1], F32)
        self.MISC = sb("MISC", [128, 8], F32)
        self.BMUP = sb("BMUP", [128, NFO], F32)
        self.MUP = sb("MUPc", [128, NFO], F32)
        self.B1 = sb("B1c", [128, NF], F32)
        self.B2 = sb("B2c", [KEXP, 1], F32)
        self.BOUT = sb("BOUTc", [128, NF], F32)
        self.NW = sb("NWc", [128, NF], F32)
        self.CONVB = sb("CONVBc", [128, NCONV], F32)
        self.CONVW = sb("CONVWc", [128, NCONV * 4], F32)
        self.DCOL = sb("DCOLc", [128, NF], F32)
        self.ALOG = sb("ALOGc", [NH, 1], F32)
        self.DTB = sb("DTBc", [NH, 1], F32)
        self.REPSs = sb("REPSs", [KEXP, NKR * 128], F32)
        self.GSELs = sb("GSELs", [KEXP, G * 128], BF16)
        self.GINDs = sb("GINDs", [128, NF * KEXP], BF16)
        self.ROWS = sb("ROWSr", [1, 2048], F32)
        self.ARS = sb("ARSr", [KEXP, BW], F32)
        self.SMX = sb("SMXr", [128, 64], F32)
        self.ABC = sb("ABC", [128, 2 * 512], F32)
        self.SCR = sb("SCRs", [128, 512], F32)
        self.SCR2a = sb("SCR2a", [128, 2 * 512], BF16)
        self.XNW = sb("XNW", [128, NF * BW], BF16)
        self.HW = sb("HW", [128, NF * BW], BF16)
        self.YSQ = sb("YSQ", [128, 2 * 512], BF16)
        self.RCHL0 = sb("RCHL0", [S, 1024], F32)
        self.pb = [self.ctx.enter_context(
            self.nc.psum_tensor(f"pb{i}", [128, 512], F32)) for i in range(8)]
        self.s_ld = self.sem("s_ld")
        self.s_st = self.sem("s_st")
        self.g_ld = self.sem("g_ld")
        self.s_wa = self.sem("s_wa")
        self.s_wb = self.sem("s_wb")
        self.s_w2 = self.sem("s_w2")
        self.s_r = self.sem("s_r")
        self.s_rq = self.sem("s_rq")
        self.s_rq2 = self.sem("s_rq2")
        self.s_sp = self.sem("s_sp")
        self.s_sz = self.sem("s_sz")
        self.s_sz2 = self.sem("s_sz2")
        self.t_sem = self.sem("t_sem")
        self.v_sem = self.sem("v_sem")
        self.c_sem = self.sem("c_sem")
        self.g_sem = self.sem("g_sem")

    def ph_init(self):
        blk = self.blk

        @blk.gpsimd
        def _(g):
            g.load_library(library_config.proxy)

        @blk.sync
        def _(s):
            bi = s.dma_start(
                out=self.X[:].rearrange("p (c t) -> p c t", c=NF),
                in_=self.i_x.rearrange("(c p) t -> p c t", p=128))
            self.s_ld.inc(bi, 16)
            for ap_in, tile in [
                (self.i_mask, self.MASK), (self.i_triu, self.TRIU),
                (self.i_I128, self.I128s), (self.i_IB128, self.IB128s),
                (self.i_e127, self.E127),
                (self.i_flag, self.FLAG), (self.i_flaginv, self.FLAGI),
                (self.i_misc, self.MISC),
                (self.i_bmup, self.BMUP), (self.i_mup, self.MUP),
                (self.i_b1, self.B1), (self.i_b2, self.B2),
                (self.i_bout, self.BOUT), (self.i_nw, self.NW),
                (self.i_convb, self.CONVB), (self.i_convw, self.CONVW),
                (self.i_Dcol, self.DCOL), (self.i_Alog, self.ALOG),
                (self.i_dtb, self.DTB), (self.i_REPS, self.REPSs),
                (self.i_GSEL, self.GSELs), (self.i_GIND, self.GINDs),
            ]:
                bi = s.dma_start(out=tile[:], in_=ap_in)
                self.s_ld.inc(bi, 16)
        self.init_ld = self.s_ld.n

        @blk.vector
        def _(v):
            self.s_ld.wait(v)
            w = v.memset(self.ONESB[:], 1.0)
            self.v_sem.inc(w)

        @blk.scalar
        def _(sc):
            self.s_ld.wait(sc, self.init_ld)
            w = sc.activation(self.MISC[0:NH, 4:5], self.ALOG[0:NH, 0:1],
                              AF.Exp, bias=0.0, scale=1.0)
            self.c_sem.inc(w)

    def ph_ln_block(self, b, src, win):
        blk = self.blk
        pb = self.pb
        col0 = SBLK[b][0]
        RM, RE, RR, RB_ = 0, 512, 1024, 1536
        self.ln_win_g = {}
        for (po, n) in PIECES[b]:
            a0 = col0 + po

            @blk.vector
            def _(v, a0=a0, po=po, n=n):
                self.v_sem.wait(v)
                self.t_sem.wait(v)
                self.g_sem.wait(v)
                for fc in range(NF):
                    w = v.tensor_copy(win[:, fc * BW + po: fc * BW + po + n],
                                      src[:, fc * T + a0: fc * T + a0 + n])
                self.v_sem.inc(w)
            cast_done = self.v_sem.n

            @blk.tensor
            def _(t, po=po, n=n, cast_done=cast_done):
                self.v_sem.wait(t, cast_done)
                for fc in range(NF):
                    w = t.matmul(pb[6][0:1, 0:n], self.ONESB[:],
                                 win[:, fc * BW + po: fc * BW + po + n],
                                 start=(fc == 0), stop=(fc == NF - 1))
                self.t_sem.inc(w)

            sq_mms = [None] * NF
            for fc in range(NF):
                @blk.vector
                def _(v, fc=fc, po=po, n=n, cast_done=cast_done):
                    if fc == 0:
                        self.v_sem.wait(v, cast_done)
                    elif fc >= 2:
                        self.t_sem.wait(v, sq_mms[fc - 2])
                    w = v.tensor_tensor(self.YSQ[:, (fc % 2) * 512:(fc % 2) * 512 + n],
                                        win[:, fc * BW + po: fc * BW + po + n],
                                        win[:, fc * BW + po: fc * BW + po + n],
                                        OP.mult)
                    self.v_sem.inc(w)
                sq_done = self.v_sem.n

                @blk.tensor
                def _(t, fc=fc, n=n, sq_done=sq_done):
                    self.v_sem.wait(t, sq_done)
                    w = t.matmul(pb[7][0:1, 0:n], self.ONESB[:],
                                 self.YSQ[:, (fc % 2) * 512:(fc % 2) * 512 + n],
                                 start=(fc == 0), stop=(fc == NF - 1))
                    self.t_sem.inc(w)
                sq_mms[fc] = self.t_sem.n
            sq_mm = self.t_sem.n

            @blk.vector
            def _(v, n=n, sq_mm=sq_mm):
                self.t_sem.wait(v, sq_mm)
                w = v.tensor_scalar(self.ROWS[0:1, RM:RM + n],
                                    pb[6][0:1, 0:n], 1.0 / D, None, OP.mult)
                self.v_sem.chain(v, w)
                w = v.tensor_scalar(self.ROWS[0:1, RE:RE + n],
                                    pb[7][0:1, 0:n], 1.0 / D, None, OP.mult)
                self.v_sem.chain(v, w)
                w = v.tensor_tensor(self.ROWS[0:1, RR:RR + n],
                                    self.ROWS[0:1, RM:RM + n],
                                    self.ROWS[0:1, RM:RM + n], OP.mult)
                self.v_sem.chain(v, w)
                w = v.tensor_tensor(self.ROWS[0:1, RE:RE + n],
                                    self.ROWS[0:1, RE:RE + n],
                                    self.ROWS[0:1, RR:RR + n], OP.subtract)
                self.v_sem.inc(w)
            var_done = self.v_sem.n

            @blk.scalar
            def _(sc, n=n, var_done=var_done):
                self.v_sem.wait(sc, var_done)
                w = sc.activation(self.ROWS[0:1, RR:RR + n],
                                  self.ROWS[0:1, RE:RE + n],
                                  AF.Ln, bias=self.MISC[0:1, 0:1], scale=1.0)
                self.c_sem.chain(sc, w)
                w = sc.activation(self.ROWS[0:1, RR:RR + n],
                                  self.ROWS[0:1, RR:RR + n],
                                  AF.Exp, bias=0.0, scale=-0.5)
                self.c_sem.inc(w)
            rstd_done = self.c_sem.n

            @blk.vector
            def _(v, n=n, rstd_done=rstd_done):
                self.c_sem.wait(v, rstd_done)
                w = v.scalar_tensor_tensor(self.ROWS[0:1, RB_:RB_ + n],
                                           self.ROWS[0:1, RM:RM + n], -1.0,
                                           self.ROWS[0:1, RR:RR + n],
                                           OP.mult, OP.mult)
                self.v_sem.inc(w)
            rows_done = self.v_sem.n

            @blk.gpsimd
            def _(g, n=n, rows_done=rows_done):
                self.v_sem.wait(g, rows_done)
                self.g_sem.wait(g)
                g.partition_broadcast(self.ABC[:, 0:n],
                                      self.ROWS[0:1, RR:RR + n], 128)
                w = g.partition_broadcast(self.ABC[:, 512:512 + n],
                                          self.ROWS[0:1, RB_:RB_ + n], 128)
                self.g_sem.inc(w)
            bc_done = self.g_sem.n

            v_base = self.v_sem.n
            g_base = self.g_sem.n

            @blk.vector
            def _(v, a0=a0, po=po, n=n, bc_done=bc_done, g_base=g_base):
                self.g_sem.wait(v, bc_done)
                for fc in range(NF):
                    if fc >= 2:
                        v.wait_ge(self.g_sem.h, g_base + fc - 1)
                    w = v.tensor_tensor(self.SCR2a[:, (fc % 2) * 512:(fc % 2) * 512 + n],
                                        src[:, fc * T + a0: fc * T + a0 + n],
                                        self.ABC[:, 0:n], OP.mult)
                    self.v_sem.inc(w)

            @blk.gpsimd
            def _(g_, a0=a0, po=po, n=n, v_base=v_base):
                for fc in range(NF):
                    g_.wait_ge(self.v_sem.h, v_base + fc + 1)
                    w = g_.tensor_tensor(win[:, fc * BW + po: fc * BW + po + n],
                                         self.SCR2a[:, (fc % 2) * 512:(fc % 2) * 512 + n],
                                         self.ABC[:, 512:512 + n], OP.add)
                    self.g_sem.inc(w)
                    self.ln_win_g[fc] = self.g_sem.n
        self.ln_done = self.v_sem.n
        self.ln_done_g = self.g_sem.n

    def stream_w(self, hbm_tile_ap, buf_slice, sem, prior_t=None):
        blk = self.blk

        @blk.gpsimd
        def _(g):
            if prior_t is not None:
                self.t_sem.wait(g, prior_t)
            w = g.dma_start(out=buf_slice, in_=hbm_tile_ap)
            sem.inc(w, 16)
        return sem.n

    def ph_ffn_block(self, b, scale):
        blk = self.blk
        pb = self.pb
        col0, ncols, _, _ = SBLK[b]
        pieces = PIECES[b]
        xn_ready = self.ln_done

        mm_done = [None] * NF
        ev_done = [None] * NF
        for fo in range(NF):
            buf = (self.WA, self.WB)[fo % 2]
            bsem = (self.s_wa, self.s_wb)[fo % 2]
            ld = self.stream_w(
                self.i_W1[fo], buf[:], bsem,
                prior_t=mm_done[fo - 2] if fo >= 2 else
                getattr(self, ('last_wa_t', 'last_wb_t')[fo % 2], 0))

            @blk.tensor
            def _(t, fo=fo, buf=buf, bsem=bsem, ld=ld):
                bsem.wait(t, ld)
                if fo >= 2:
                    self.v_sem.wait(t, ev_done[fo - 2])
                for pi, (po, n) in enumerate(pieces):
                    tgt = pb[2 * (fo % 2) + pi]
                    for kc in range(NF):
                        if fo == 0 and pi == 0:
                            self.g_sem.wait(t, self.ln_win_g[kc])
                        w = t.matmul(tgt[:, 0:n],
                                     buf[:, kc * 128:(kc + 1) * 128],
                                     self.XNW[:, kc * BW + po: kc * BW + po + n],
                                     start=(kc == 0), stop=(kc == NF - 1))
                self.t_sem.inc(w)
            mm_done[fo] = self.t_sem.n

            @blk.vector
            def _(v, fo=fo):
                self.t_sem.wait(v, mm_done[fo])
                for pi, (po, n) in enumerate(pieces):
                    tgt = pb[2 * (fo % 2) + pi]
                    w = v.tensor_scalar(
                        self.HW[:, fo * BW + po: fo * BW + po + n],
                        tgt[:, 0:n], self.B1[:, fo:fo + 1], 0.0, OP.add, OP.max)
                self.v_sem.inc(w)
            ev_done[fo] = self.v_sem.n
        h_done = self.v_sem.n

        w2_ld = self.stream_w(
            self.i_W2.rearrange("(c p) m -> p c m", p=128),
            self.W2B[:].rearrange("p (c m) -> p c m", c=NF), self.s_w2)

        @blk.tensor
        def _(t):
            self.s_w2.wait(t, w2_ld)
            self.v_sem.wait(t, h_done)
            for pi, (po, n) in enumerate(pieces):
                tgt = (pb[4], pb[6])[pi]
                for kc in range(NF):
                    w = t.matmul(tgt[0:KEXP, 0:n],
                                 self.W2B[:, kc * KEXP:(kc + 1) * KEXP],
                                 self.HW[:, kc * BW + po: kc * BW + po + n],
                                 start=(kc == 0), stop=(kc == NF - 1))
            self.t_sem.inc(w)
        w2mm = self.t_sem.n

        @blk.vector
        def _(v):
            self.t_sem.wait(v, w2mm)
            for pi, (po, n) in enumerate(pieces):
                tgt = (pb[4], pb[6])[pi]
                w = v.tensor_scalar(self.ARS[0:KEXP, po:po + n], tgt[0:KEXP, 0:n],
                                    self.B2[0:KEXP, 0:1], None, OP.add)
            self.v_sem.inc(w)
        ar_done = self.v_sem.n

        nj = (ncols + 127) // 128

        @blk.tensor
        def _(t):
            self.v_sem.wait(t, ar_done)
            for j in range(nj):
                nt = min(128, ncols - j * 128)
                w = t.transpose(pb[5][0:nt, j * 8:(j + 1) * 8],
                                self.ARS[0:KEXP, j * 128:j * 128 + nt],
                                self.I128s[0:KEXP, 0:KEXP])
            self.t_sem.inc(w)
        tf_done = self.t_sem.n

        @blk.vector
        def _(v):
            self.t_sem.wait(v, tf_done)
            for j in range(nj):
                nt = min(128, ncols - j * 128)
                w = v.reduce_max(self.SMX[0:nt, 32 + j:33 + j],
                                 pb[5][0:nt, j * 8:(j + 1) * 8], AXX)
            self.v_sem.chain(v, w)
            for j in range(nj):
                nt = min(128, ncols - j * 128)
                w = v.tensor_scalar(self.SMX[0:nt, 40 + j:41 + j],
                                    self.SMX[0:nt, 32 + j:33 + j],
                                    -1.0, None, OP.mult)
            self.v_sem.inc(w)
        neg_done = self.v_sem.n

        @blk.scalar
        def _(sc):
            self.v_sem.wait(sc, neg_done)
            for j in range(nj):
                nt = min(128, ncols - j * 128)
                w = sc.activation(self.SMX[0:nt, j * 8:(j + 1) * 8],
                                  pb[5][0:nt, j * 8:(j + 1) * 8], AF.Exp,
                                  bias=self.SMX[0:nt, 40 + j:41 + j], scale=1.0)
            self.c_sem.inc(w)
        exp_done = self.c_sem.n

        @blk.vector
        def _(v):
            self.c_sem.wait(v, exp_done)
            for j in range(nj):
                nt = min(128, ncols - j * 128)
                w = v.reduce_sum(self.SMX[0:nt, 48 + j:49 + j],
                                 self.SMX[0:nt, j * 8:(j + 1) * 8], AXX)
            self.v_sem.chain(v, w)
            for j in range(nj):
                nt = min(128, ncols - j * 128)
                w = v.reciprocal(self.SMX[0:nt, 56 + j:57 + j],
                                 self.SMX[0:nt, 48 + j:49 + j])
            self.v_sem.chain(v, w)
            for j in range(nj):
                nt = min(128, ncols - j * 128)
                w = v.tensor_scalar(self.SMX[0:nt, j * 8:(j + 1) * 8],
                                    self.SMX[0:nt, j * 8:(j + 1) * 8],
                                    self.SMX[0:nt, 56 + j:57 + j], None, OP.mult)
            self.v_sem.inc(w)
        at_done = self.v_sem.n

        @blk.tensor
        def _(t):
            self.v_sem.wait(t, at_done)
            for j in range(nj):
                nt = min(128, ncols - j * 128)
                if j < 4:
                    w = t.transpose(pb[7][0:KEXP, j * 128:j * 128 + nt],
                                    self.SMX[0:nt, j * 8:(j + 1) * 8],
                                    self.I128s[0:nt, 0:nt])
                else:
                    w = t.transpose(pb[6][0:KEXP, 128:128 + nt],
                                    self.SMX[0:nt, j * 8:(j + 1) * 8],
                                    self.I128s[0:nt, 0:nt])
            self.t_sem.inc(w)
        tb_done = self.t_sem.n

        @blk.vector
        def _(v):
            self.t_sem.wait(v, tb_done)
            w = v.tensor_copy(self.ARS[0:KEXP, 0:min(ncols, 512)],
                              pb[7][0:KEXP, 0:min(ncols, 512)])
            if ncols > 512:
                self.v_sem.chain(v, w)
                w = v.tensor_copy(self.ARS[0:KEXP, 512:ncols],
                                  pb[6][0:KEXP, 128:128 + ncols - 512])
            self.v_sem.inc(w)
        alpha_done = self.v_sem.n

        vx_mms = []
        vx_evs = []
        for rc in range(NKR):
            buf = (self.WA, self.WB)[rc % 2]
            bsem = (self.s_wa, self.s_wb)[rc % 2]
            ld = self.stream_w(
                self.i_Vc[rc], buf[:], bsem,
                prior_t=vx_mms[rc - 2] if rc >= 2 else mm_done[NF - 2 + rc])

            @blk.tensor
            def _(t, rc=rc, buf=buf, bsem=bsem, ld=ld):
                bsem.wait(t, ld)
                if rc == 0:
                    self.v_sem.wait(t, h_done)
                if rc >= 2:
                    self.v_sem.wait(t, vx_evs[rc - 2])
                for pi, (po, n) in enumerate(pieces):
                    tgt = pb[2 * (rc % 2) + pi]
                    for kc in range(NF):
                        w = t.matmul(tgt[:, 0:n],
                                     buf[:, kc * 128:(kc + 1) * 128],
                                     self.XNW[:, kc * BW + po: kc * BW + po + n],
                                     start=(kc == 0), stop=(kc == NF - 1))
                self.t_sem.inc(w)
            vx_mms.append(self.t_sem.n)

            for pi, (po, n) in enumerate(pieces):
                @blk.tensor
                def _(t, rc=rc, po=po, n=n, pi=pi):
                    if rc == 0 and pi == 0:
                        self.v_sem.wait(t, alpha_done)
                    else:
                        self.v_sem.wait(t)
                    w = t.matmul(pb[4][:, 0:n],
                                 self.REPSs[0:KEXP, rc * 128:(rc + 1) * 128],
                                 self.ARS[0:KEXP, po:po + n],
                                 start=True, stop=True)
                    self.t_sem.inc(w)
                ab_mm = self.t_sem.n

                @blk.vector
                def _(v, rc=rc, po=po, n=n, pi=pi, ab_mm=ab_mm):
                    self.t_sem.wait(v, ab_mm)
                    w = v.tensor_copy(self.SCR[:, 0:n], pb[4][:, 0:n])
                    self.v_sem.chain(v, w)
                    tgt = pb[2 * (rc % 2) + pi]
                    w = v.tensor_tensor(
                        self.VXW[:, rc * BW + po: rc * BW + po + n],
                        tgt[:, 0:n], self.SCR[:, 0:n], OP.mult)
                    self.v_sem.inc(w)
            vx_evs.append(self.v_sem.n)
        vxs_done = self.v_sem.n

        s4_mm = [None] * NF
        s4_ev = [None] * NF
        for fo in range(NF):
            buf = (self.WA, self.WB)[fo % 2]
            bsem = (self.s_wa, self.s_wb)[fo % 2]
            ld = self.stream_w(
                self.i_Uc[fo], buf[:, 0:NKR * 128], bsem,
                prior_t=s4_mm[fo - 2] if fo >= 2 else vx_mms[NKR - 2 + fo])

            @blk.tensor
            def _(t, fo=fo, buf=buf, bsem=bsem, ld=ld):
                bsem.wait(t, ld)
                if fo == 0:
                    self.v_sem.wait(t, vxs_done)
                if fo >= 2:
                    self.v_sem.wait(t, s4_ev[fo - 2])
                for pi, (po, n) in enumerate(pieces):
                    tgt = pb[2 * (fo % 2) + pi]
                    for rc in range(NKR):
                        w = t.matmul(tgt[:, 0:n],
                                     buf[:, rc * 128:(rc + 1) * 128],
                                     self.VXW[:, rc * BW + po: rc * BW + po + n],
                                     start=(rc == 0), stop=(rc == NKR - 1))
                self.t_sem.inc(w)
            s4_mm[fo] = self.t_sem.n

            @blk.vector
            def _(v, fo=fo):
                self.t_sem.wait(v, s4_mm[fo])
                for pi, (po, n) in enumerate(pieces):
                    a0 = col0 + po
                    tgt = pb[2 * (fo % 2) + pi]
                    w = v.scalar_tensor_tensor(
                        self.X[:, fo * T + a0: fo * T + a0 + n],
                        tgt[:, 0:n], scale,
                        self.X[:, fo * T + a0: fo * T + a0 + n],
                        OP.mult, OP.add)
                self.v_sem.inc(w)
            s4_ev[fo] = self.v_sem.n
        self.last_wa_t = s4_mm[NF - 2]
        self.last_wb_t = s4_mm[NF - 1]

    def ph_win_block(self, b):
        blk = self.blk
        pb = self.pb
        col0, ncols, own0, nown = SBLK[b]
        pieces = PIECES[b]
        xn_ready = self.ln_done
        wi_mm = [None] * NFO
        wi_ev = [None] * NFO
        for wo in range(NFO):
            buf = (self.WA, self.WB)[wo % 2]
            bsem = (self.s_wa, self.s_wb)[wo % 2]
            ld = self.stream_w(
                self.i_Win[wo], buf[:], bsem,
                prior_t=wi_mm[wo - 2] if wo >= 2 else
                (self.last_wa_t, self.last_wb_t)[wo % 2])

            @blk.tensor
            def _(t, wo=wo, buf=buf, bsem=bsem, ld=ld):
                bsem.wait(t, ld)
                if wo >= 2 and wi_ev[wo - 2] is not None:
                    vv, cc_ = wi_ev[wo - 2]
                    self.v_sem.wait(t, vv)
                    self.c_sem.wait(t, cc_)
                for pi, (po, n) in enumerate(pieces):
                    tgt = pb[2 * (wo % 2) + pi]
                    for kc in range(NF):
                        if wo == 0 and pi == 0:
                            self.g_sem.wait(t, self.ln_win_g[kc])
                        w = t.matmul(tgt[:, 0:n],
                                     buf[:, kc * 128:(kc + 1) * 128],
                                     self.XNW[:, kc * BW + po: kc * BW + po + n],
                                     start=(kc == 0), stop=(kc == NF - 1))
                self.t_sem.inc(w)
            wi_mm[wo] = self.t_sem.n
            self.win_evac(b, wo, pieces, col0, own0, wi_mm[wo])
            wi_ev[wo] = (self.v_sem.n, self.c_sem.n)
        self.last_wa_t = wi_mm[NFO - 1 - (NFO - 1) % 2]
        self.last_wb_t = wi_mm[NFO - 1 - (NFO % 2)]

    def win_evac(self, b, wo, pieces, col0, own0, mm):
        blk = self.blk
        pb = self.pb
        if wo < NF:
            @blk.scalar
            def _(sc, wo=wo):
                self.t_sem.wait(sc, mm)
                for pi, (po, n) in enumerate(pieces):
                    a0 = col0 + po
                    so, o0, ln = owned_map(a0, n)
                    w = sc.activation(
                        self.SILZB[:, wo * 512 + (o0 - own0):
                                   wo * 512 + (o0 - own0) + ln],
                        pb[2 * (wo % 2) + pi][:, so:so + ln],
                        AF.Silu, bias=self.BMUP[:, wo:wo + 1],
                        scale=self.MUP[:, wo:wo + 1])
                self.c_sem.inc(w)
        elif wo == NFO - 1:
            @blk.vector
            def _(v, wo=wo):
                self.t_sem.wait(v, mm)
                for pi, (po, n) in enumerate(pieces):
                    a0 = col0 + po
                    so, o0, ln = owned_map(a0, n)
                    w = v.tensor_scalar(self.DTLD[0:NH, o0:o0 + ln],
                                        pb[2 * (wo % 2) + pi][0:NH, so:so + ln],
                                        self.MUP[0:NH, wo:wo + 1],
                                        self.BMUP[0:NH, wo:wo + 1],
                                        OP.mult, OP.add)
                self.v_sem.inc(w)
        else:
            cc = wo - NF
            for pi, (po, n) in enumerate(pieces):
                a0 = col0 + po
                tgt = pb[2 * (wo % 2) + pi]
                so, o0, ln = owned_map(a0, n)
                sbase = o0 - a0 + CTX

                @blk.vector
                def _(v, cc=cc, wo=wo, a0=a0, n=n, tgt=tgt, so=so, o0=o0,
                      ln=ln, sbase=sbase):
                    self.t_sem.wait(v, mm)
                    self.c_sem.wait(v)
                    if a0 > 0:
                        w = v.tensor_copy(self.CSTR[:, 0:CTX],
                                          self.CARR[:, cc * 4:cc * 4 + CTX])
                        self.v_sem.chain(v, w)
                    w = v.tensor_scalar(self.CSTR[:, CTX:CTX + n], tgt[:, 0:n],
                                        self.MUP[:, wo:wo + 1],
                                        self.BMUP[:, wo:wo + 1], OP.mult, OP.add)
                    self.v_sem.chain(v, w)
                    if a0 == 0:
                        w = v.tensor_scalar(self.CSTR[:, CTX:2 * CTX],
                                            self.CSTR[:, CTX:2 * CTX],
                                            self.FLAG[:], None, OP.mult)
                        self.v_sem.chain(v, w)
                    w = v.tensor_copy(self.CARR[:, cc * 4:cc * 4 + CTX],
                                      self.CSTR[:, n:n + CTX])
                    self.v_sem.chain(v, w)
                    w = v.tensor_scalar(self.CACC[:, 0:ln],
                                        self.CSTR[:, sbase:sbase + ln],
                                        self.CONVW[:, cc * 4:cc * 4 + 1], None,
                                        OP.mult)
                    self.v_sem.chain(v, w)
                    for k in range(1, 4):
                        w = v.scalar_tensor_tensor(
                            self.CACC[:, 0:ln],
                            self.CSTR[:, sbase + k:sbase + k + ln],
                            self.CONVW[:, cc * 4 + k:cc * 4 + k + 1],
                            self.CACC[:, 0:ln], OP.mult, OP.add)
                        self.v_sem.chain(v, w)
                vdone = self.v_sem.n

                @blk.scalar
                def _(sc, cc=cc, o0=o0, ln=ln, vdone=vdone, own0=own0):
                    self.v_sem.wait(sc, vdone)
                    if cc < NF:
                        dst = self.HW[:, cc * BW + (o0 - own0):
                                      cc * BW + (o0 - own0) + ln]
                    elif cc < 18:
                        dst = self.BTW[:, (cc - NF) * 512 + (o0 - own0):
                                       (cc - NF) * 512 + (o0 - own0) + ln]
                    else:
                        dst = self.CT[:, (cc - 18) * TOWN + o0:
                                      (cc - 18) * TOWN + o0 + ln]
                    w = sc.activation(dst, self.CACC[:, 0:ln], AF.Silu,
                                      bias=self.CONVB[:, cc:cc + 1], scale=1.0)
                    self.c_sem.inc(w)

    def ph_dt_block(self, b):
        blk = self.blk
        col0, ncols, own0, nown = SBLK[b]

        @blk.vector
        def _(v):
            self.c_sem.wait(v)
            self.v_sem.wait(v)
            w = v.tensor_scalar(self.SCR[0:NH, 0:nown],
                                self.DTLD[0:NH, own0:own0 + nown],
                                self.DTB[0:NH, 0:1], 80.0, OP.add, OP.min)
            self.v_sem.inc(w)
        sp1 = self.v_sem.n

        @blk.scalar
        def _(sc):
            self.v_sem.wait(sc, sp1)
            w = sc.activation(self.ABC[0:NH, 0:nown], self.SCR[0:NH, 0:nown],
                              AF.Exp, bias=0.0, scale=1.0)
            self.c_sem.chain(sc, w)
            w = sc.activation(self.DTLD[0:NH, own0:own0 + nown],
                              self.ABC[0:NH, 0:nown],
                              AF.Ln, bias=self.MISC[0:NH, 1:2], scale=1.0)
            self.c_sem.inc(w)
        self.ld_done = self.c_sem.n

    def ph_r_block(self, b):
        blk = self.blk
        pb = self.pb
        c_base = self.c_sem.n
        dec_pred = lambda c4: c_base + 3 * c4 + 1
        ers_pred = lambda c4: c_base + 3 * c4 + 2
        wg2_pred = lambda c4: c_base + 3 * c4 + 3
        if not hasattr(self, "_rb_done"):
            self._rb_done = {}
            self._wgt_done = {}
        nl_dones, rev_dones, wg1s, wgt_tts, nlbs = {}, {}, {}, {}, {}
        tt_prev = None
        for c4 in range(4):
            c = 4 * b + c4

            @blk.vector
            def _(v, c=c, c4=c4, tt_prev=tt_prev):
                self.c_sem.wait(v, self.ld_done)
                if c4 == 0:
                    self.v_sem.wait(v)
                    self.t_sem.wait(v)
                else:
                    self.t_sem.wait(v, tt_prev)
                w = v.tensor_scalar(self.LDC[0:NH, 0:128],
                                    self.DTLD[0:NH, c * 128:c * 128 + 128],
                                    self.MISC[0:NH, 4:5], -1.0, OP.mult, OP.mult)
                self.v_sem.inc(w)
            ldc_done = self.v_sem.n

            @blk.tensor
            def _(t, c=c, c4=c4, ldc_done=ldc_done):
                self.v_sem.wait(t, ldc_done)
                t.transpose(pb[4][:, 0:S], self.DTLD[0:NH, c * 128:c * 128 + 128],
                            self.I128s[0:NH, 0:NH])
                w = t.transpose(pb[4][:, S:2 * S], self.LDC[0:NH, 0:128],
                                self.I128s[0:NH, 0:NH])
                self.t_sem.inc(w)
            tt1 = self.t_sem.n
            tt_prev = tt1

            @blk.vector
            def _(v, c4=c4, tt1=tt1):
                self.t_sem.wait(v, tt1)
                w = v.tensor_copy(self.DTOK[:, c4 * 64:c4 * 64 + 64],
                                  pb[4][:, 0:64])
                self.v_sem.inc(w)
            dtok_done = self.v_sem.n

            @blk.tensor
            def _(t, c4=c4, dtok_done=dtok_done):
                self.v_sem.wait(t, dtok_done)
                t.matmul(pb[5][:, 0:S], self.TRIU[:],
                         self.DTOK[:, c4 * 64 + S:c4 * 64 + 64],
                         start=True, stop=True)
                w = t.matmul(pb[5][0:S, 64:64 + 128],
                             self.DTOK[:, c4 * 64 + S:c4 * 64 + 64],
                             self.TRIU[:], start=True, stop=True)
                self.t_sem.inc(w)
            rmm = self.t_sem.n

            @blk.vector
            def _(v, c=c, c4=c4, rmm=rmm):
                self.t_sem.wait(v, rmm)
                if c4 > 0:
                    self.c_sem.wait(v, ers_pred(c4 - 1))
                    self.s_r.wait(v, self._rb_done[c - 1])
                w = v.tensor_scalar(self.NRT[:, c4 * S:(c4 + 1) * S],
                                    pb[5][:, 0:S], -1.0, None, OP.mult)
                self.v_sem.chain(v, w)
                w = v.tensor_copy(self.RFMs[0:S, 0:Q], pb[5][0:S, 64:64 + 128])
                self.v_sem.inc(w)
            rev_done = self.v_sem.n
            rev_dones[c4] = rev_done

            @blk.tensor
            def _(t, c4=c4, rev_done=rev_done):
                self.v_sem.wait(t, rev_done)
                w = t.matmul(pb[4][0:1, 64:64 + S], self.E127[:],
                             self.NRT[:, c4 * S:(c4 + 1) * S],
                             start=True, stop=True)
                self.t_sem.inc(w)
            nl_mm = self.t_sem.n

            @blk.vector
            def _(v, c4=c4, nl_mm=nl_mm):
                self.t_sem.wait(v, nl_mm)
                if c4 > 0:
                    self.g_sem.wait(v, nlbs[c4 - 1])
                w = v.tensor_copy(self.NLR[0:1, 0:S], pb[4][0:1, 64:64 + S])
                self.v_sem.inc(w)
            nl_done = self.v_sem.n
            nl_dones[c4] = nl_done

            @blk.sync
            def _(s, c=c, c4=c4):
                self.c_sem.wait(s, ers_pred(c4))
                w = s.dma_start(
                    out=self.h_r[c:c + 1, :].rearrange("r (s q) -> (r s) q", s=S),
                    in_=self.RFMs[0:S, 0:Q])
                self.s_r.inc(w, 16)
                w = s.dma_start(
                    out=self.h_er[c:c + 1, :].rearrange("r (s q) -> (r s) q", s=S),
                    in_=self.ERS[0:S, 0:Q])
                self.s_r.inc(w, 16)
            self._rb_done[c] = self.s_r.n

            @blk.gpsimd
            def _(g, c4=c4):
                self.c_sem.wait(g, dec_pred(c4))
                if c4 > 0:
                    self.v_sem.wait(g, wg1s[c4 - 1])
                else:
                    self.v_sem.wait(g)
                w = g.partition_broadcast(self.ABC[:, 512:512 + S],
                                          self.NLR[0:1, 0:S], 128)
                self.g_sem.inc(w)
            nlb_done = self.g_sem.n
            nlbs[c4] = nlb_done

            @blk.vector
            def _(v, c4=c4, nlb_done=nlb_done):
                self.g_sem.wait(v, nlb_done)
                if c4 > 0:
                    self.c_sem.wait(v, wg2_pred(c4 - 1))
                w = v.tensor_tensor(self.SCR[:, 0:S],
                                    self.NRT[:, c4 * S:(c4 + 1) * S],
                                    self.ABC[:, 512:512 + S], OP.subtract)
                self.v_sem.inc(w)
            wg1 = self.v_sem.n
            wg1s[c4] = wg1

            @blk.vector
            def _(v, c=c, c4=c4):
                self.c_sem.wait(v, wg2_pred(c4))
                w = v.tensor_tensor(self.WGT[:, c4 * NH:(c4 + 1) * NH],
                                    self.SCR[:, S:2 * S],
                                    self.DTOK[:, c4 * 64:c4 * 64 + S], OP.mult)
                self.v_sem.inc(w)
            self._wgt_done[c] = self.v_sem.n
            wgt_tts[c4] = self.v_sem.n

        @blk.scalar
        def _(sc):
            for c4 in range(4):
                c = 4 * b + c4
                sc.wait_ge(self.v_sem.h, nl_dones[c4])
                w = sc.activation(self.DECROW[0:1, c * NH:(c + 1) * NH],
                                  self.NLR[0:1, 0:S], AF.Exp, bias=0.0,
                                  scale=-1.0)
                self.c_sem.inc(w)
                sc.wait_ge(self.v_sem.h, rev_dones[c4])
                if c4 > 0:
                    self.s_r.wait(sc, self._rb_done[c - 1])
                w = sc.activation(self.ERS[0:S, 0:Q], self.RFMs[0:S, 0:Q],
                                  AF.Exp, bias=0.0, scale=1.0)
                self.c_sem.inc(w)
                sc.wait_ge(self.v_sem.h, wg1s[c4])
                if c4 > 0:
                    sc.wait_ge(self.v_sem.h, wgt_tts[c4 - 1])
                w = sc.activation(self.SCR[:, S:2 * S], self.SCR[:, 0:S],
                                  AF.Exp, bias=0.0, scale=1.0)
                self.c_sem.inc(w)
        assert self.c_sem.n == c_base + 12

    def ph_ssd_chunk(self, b, c):
        blk = self.blk
        pb = self.pb
        col0, ncols, own0, nown = SBLK[b]
        j0 = c * 128 - own0
        c4 = c % 4
        c_base = self.c_sem.n
        e_pred = lambda g, hp: c_base + 4 * g + hp + 1

        xb = (c % 2) * D
        copy_prev = getattr(self, "_xtok_prev", None)
        for jj in range(4):
            @blk.tensor
            def _(t, jj=jj, c4=c4, copy_prev=copy_prev):
                if jj == 0 and c4 == 0:
                    self.c_sem.wait(t)
                    self.v_sem.wait(t)
                elif copy_prev is not None:
                    self.v_sem.wait(t, copy_prev)
                pbv = pb[4][:, 0:256].bitcast(BF16)
                for fc in range(4 * jj, 4 * jj + 4):
                    w = t.transpose(pbv[:, (fc % 4) * 128:(fc % 4) * 128 + 128],
                                    self.HW[:, fc * BW + j0: fc * BW + j0 + 128],
                                    self.IB128s[:])
                self.t_sem.inc(w)
            xt_mm = self.t_sem.n

            @blk.vector
            def _(v, jj=jj, xt_mm=xt_mm):
                self.t_sem.wait(v, xt_mm)
                w = v.tensor_copy(self.XTOK[:, xb + jj * 512: xb + (jj + 1) * 512],
                                  pb[4][:, 0:256].bitcast(BF16))
                self.v_sem.inc(w)
            copy_prev = self.v_sem.n
        xtok_done = self.v_sem.n
        self._xtok_prev = xtok_done

        st_bank = pb[6] if c % 2 == 0 else pb[3]
        bw_prev = None
        mt_last = {}
        gt_mms = {}
        rbm_dones = {}
        mtds = {}
        vcur0, ccur0 = self.v_sem.n, self.c_sem.n
        for g in range(G):
            i, gq = g % 4, g // 4
            gt_bank = pb[5] if g % 2 == 0 else pb[2]
            y_bank = pb[7] if g % 2 == 0 else pb[0]

            @blk.tensor
            def _(t, g=g, i=i, gq=gq, bw_prev=bw_prev):
                if g == 0:
                    self.v_sem.wait(t, vcur0)
                    self.c_sem.wait(t, ccur0)
                else:
                    self.v_sem.wait(t, bw_prev)
                w = t.transpose(pb[1][:, 0:S // 2].bitcast(BF16),
                                self.BTW[32 * i:32 * i + 32,
                                         gq * 512 + j0: gq * 512 + j0 + 128],
                                self.IB128s[32 * i:32 * i + 32, 32 * i:32 * i + 32],
                                tile_position=(32 * i, 0))
                self.t_sem.inc(w)
            bt_mm = self.t_sem.n

            @blk.vector
            def _(v, g=g, bt_mm=bt_mm):
                self.t_sem.wait(v, bt_mm)
                self.v_sem.wait(v, self._wgt_done[c])
                w = v.tensor_tensor(
                    self.BW4[:].rearrange("p (a f) -> p a f", a=4),
                    pb[1][:, 0:S // 2].bitcast(BF16)
                        .unsqueeze(1).broadcast_to([128, 4, S]),
                    self.WGT[:, c4 * NH + 4 * g:c4 * NH + 4 * g + 4]
                        .unsqueeze(2).broadcast_to([128, 4, S]),
                    OP.mult)
                self.v_sem.inc(w)
            bw_done = self.v_sem.n
            bw_prev = bw_done

            @blk.tensor
            def _(t, g=g, i=i, gq=gq, gt_bank=gt_bank):
                if g >= 2 and mt_last.get(g - 2):
                    self.v_sem.wait(t, mt_last[g - 2])
                w = t.matmul(gt_bank[:, 0:128],
                             self.BTW[32 * i:32 * i + 32,
                                      gq * 512 + j0: gq * 512 + j0 + 128],
                             self.CT[32 * i:32 * i + 32,
                                     gq * TOWN + c * 128: gq * TOWN + c * 128 + 128],
                             start=True, stop=True,
                             tile_position=(32 * i, 0))
                self.t_sem.inc(w)
            gt_mm = self.t_sem.n
            gt_mms[g] = gt_mm

            if g % 2 == 0:
                @blk.sync
                def _(s, g=g):
                    self.s_r.wait(s, self._rb_done[c])
                    self.g_sem.wait(s)
                    w = s.dma_start(
                        out=self.RCHL0[0:1, :],
                        in_=self.h_r[c:c + 1, (g // 2) * 1024:
                                     (g // 2) * 1024 + 1024])
                    self.s_rq.inc(w, 16)
                self.rq_ld = self.s_rq.n

            @blk.gpsimd
            def _(g_, g=g):
                g_.wait_ge(self.s_rq.h, self.rq_ld)
                self.c_sem.wait(g_, c_base + 4 * g)
                w = g_.partition_broadcast(
                    self.RB4[:],
                    self.RCHL0[0:1, (g % 2) * 512:(g % 2) * 512 + 512], 128)
                self.g_sem.inc(w)
            rb_done = self.g_sem.n

            @blk.vector
            def _(v, g=g, rb_done=rb_done):
                self.g_sem.wait(v, rb_done)
                w = v.tensor_tensor(
                    self.RB4[:].rearrange("p (a f) -> p a f", a=4),
                    self.MASK[:].unsqueeze(1).broadcast_to([128, 4, 128]),
                    self.RB4[:].rearrange("p (a f) -> p a f", a=4),
                    OP.add)
                self.v_sem.inc(w)
            rbm_done = self.v_sem.n
            rbm_dones[g] = rbm_done

            mtd = []
            for hp in range(4):
                h = 4 * g + hp

                @blk.vector
                def _(v, g=g, hp=hp, h=h, gt_bank=gt_bank,
                      gt_mm=gt_mm):
                    self.c_sem.wait(v, e_pred(g, hp))
                    self.t_sem.wait(v, gt_mm)
                    w = v.scalar_tensor_tensor(
                        self.MT[:, hp * 128:(hp + 1) * 128],
                        self.ABC[:, (hp % 2) * 128:(hp % 2) * 128 + 128],
                        self.DTOK[:, c4 * 64 + h:c4 * 64 + h + 1],
                        gt_bank[:, 0:128], OP.mult, OP.mult)
                    self.v_sem.inc(w)
                mt_done = self.v_sem.n
                mtd.append(mt_done)
                mtds[(g, hp)] = mt_done

                @blk.tensor
                def _(t, g=g, i=i, gq=gq, hp=hp, h=h, mt_done=mt_done, xb=xb,
                      y_bank=y_bank):
                    self.v_sem.wait(t, mt_done)
                    self.v_sem.wait(t, xtok_done)
                    pr = hp // 2
                    ph = h % 2
                    t.matmul(y_bank[64 * ph:64 * ph + 64, pr * 128:pr * 128 + 128],
                             self.XTOK[:, xb + h * 64: xb + h * 64 + 64],
                             self.MT[:, hp * 128:(hp + 1) * 128],
                             start=True, stop=True,
                             tile_position=(0, 64 * ph))
                    w = t.matmul(
                        st_bank[32 * i:32 * i + 32,
                                gq * 256 + hp * 64: gq * 256 + hp * 64 + 64],
                        self.BW4[:, hp * S:(hp + 1) * S],
                        self.XTOK[:, xb + h * 64: xb + h * 64 + 64],
                        start=True, stop=True,
                        tile_position=(0, 32 * i))
                    self.t_sem.inc(w)
            mt_last[g] = self.v_sem.n
            y_mm = self.t_sem.n

            @blk.vector
            def _(v, g=g, c=c, j0=j0, y_mm=y_mm, y_bank=y_bank):
                self.t_sem.wait(v, y_mm)
                if g == 0 and c % 4 == 0:
                    self.s_sp.wait(v)
                for pr in range(2):
                    p_ = 2 * g + pr
                    w = v.scalar_tensor_tensor(
                        self.SCR[:, 0:128],
                        self.HW[:, p_ * BW + j0: p_ * BW + j0 + 128],
                        self.DCOL[:, p_:p_ + 1],
                        y_bank[:, pr * 128:pr * 128 + 128], OP.mult, OP.add)
                    self.v_sem.chain(v, w)
                    w = v.tensor_tensor(
                        self.Y1B[:, p_ * 512 + j0: p_ * 512 + j0 + 128],
                        self.SCR[:, 0:128],
                        self.SILZB[:, p_ * 512 + j0: p_ * 512 + j0 + 128],
                        OP.mult)
                    self.v_sem.chain(v, w)

        @blk.scalar
        def _(sc):
            for g in range(G):
                for hp in range(4):
                    h = 4 * g + hp
                    if hp == 0:
                        self.v_sem.wait(sc, rbm_dones[g])
                    if hp >= 2:
                        self.v_sem.wait(sc, mtds[(g, hp - 2)])
                    w = sc.activation(
                        self.ABC[:, (hp % 2) * 128:(hp % 2) * 128 + 128],
                        self.RB4[:, hp * 128:(hp + 1) * 128],
                        AF.Exp, bias=self.NRT[:, c4 * S + h:c4 * S + h + 1],
                        scale=1.0)
                    self.c_sem.inc(w)
        assert self.c_sem.n == c_base + 4 * G

        all_tmm = self.t_sem.n

        dtt_done = self.build_dtt(c)

        @blk.vector
        def _(v, c=c, dtt_done=dtt_done):
            self.v_sem.wait(v, dtt_done)
            self.t_sem.wait(v, all_tmm)
            if c == 0:
                w = v.tensor_copy(self.SST[:, 0:512], st_bank[:, 0:512])
            else:
                w = v.tensor_tensor(self.SCR[:, 0:512],
                                    self.SST[:, (c - 1) * 512: c * 512],
                                    self.DTT[:], OP.mult)
                self.v_sem.chain(v, w)
                w = v.tensor_tensor(self.SST[:, c * 512:(c + 1) * 512],
                                    self.SCR[:, 0:512], st_bank[:, 0:512], OP.add)
            self.v_sem.inc(w)

    def build_dtt(self, c):
        blk = self.blk

        @blk.gpsimd
        def _(g_):
            self.c_sem.wait(g_)
            prev = getattr(self, "_dtt_prev", None)
            if prev is not None:
                self.v_sem.wait(g_, prev)
            else:
                self.v_sem.wait(g_)
                self.g_sem.wait(g_)
            w = g_.partition_broadcast(self.SCR[:, 128:128 + NH],
                                       self.DECROW[0:1, c * NH:(c + 1) * NH], 128)
            self.g_sem.inc(w)
        d128 = self.g_sem.n

        @blk.vector
        def _(v):
            self.g_sem.wait(v, d128)
            base = self.SCR[:]
            for i in range(4):
                src_ap = bass.AP(base.tensor, 32 * i * 512 + 4 * i + 128,
                                 [[512, 32], [16, 2], [1, 4], [0, 64]])
                w = v.tensor_copy(
                    self.DTT[32 * i:32 * i + 32, :]
                        .rearrange("p (a b f) -> p a b f", a=2, b=4),
                    src_ap)
                self.v_sem.chain(v, w)
        self._dtt_prev = self.v_sem.n
        return self.v_sem.n

    def ph_spill_block(self, b):
        blk = self.blk
        col0, ncols, own0, nown = SBLK[b]
        vn, cn = self.v_sem.n, self.c_sem.n

        @blk.sync
        def _(s):
            self.v_sem.wait(s, vn)
            self.c_sem.wait(s, cn)
            w = s.dma_start(
                out=self.h_silz.rearrange("(c p) t -> p c t", p=128)
                    [:, :, own0:own0 + 512],
                in_=self.SILZB[:].rearrange("p (c t) -> p c t", c=NF))
            self.s_sp.inc(w, 16)
            w = s.dma_start(
                out=self.h_y1.rearrange("(c p) t -> p c t", p=128)
                    [:, :, own0:own0 + 512],
                in_=self.Y1B[:].rearrange("p (c t) -> p c t", c=NF))
            self.s_sp.inc(w, 16)

    def ph_y1_load(self, b):
        blk = self.blk
        col0, ncols, own0, nown = SBLK[b]
        vn = self.v_sem.n

        @blk.sync
        def _(s):
            self.v_sem.wait(s, vn)
            w = s.dma_start(
                out=self.Y1B[:].rearrange("p (c t) -> p c t", c=NF),
                in_=self.h_y1.rearrange("(c p) t -> p c t", p=128)
                    [:, :, own0:own0 + 512])
            self.s_sp.inc(w, 16)
        self.y1_ld = self.s_sp.n

    def ph_state_collective(self):
        blk = self.blk

        @blk.vector
        def _(v):
            self.v_sem.wait(v)
            w = v.tensor_scalar(self.SCR[:, 0:512], self.SST[:, 7 * 512:8 * 512],
                                self.FLAGI[:], None, OP.mult)
            self.v_sem.inc(w)
        contrib = self.v_sem.n

        @blk.scalar
        def _(sc):
            self.v_sem.wait(sc, contrib)
            w = sc.dma_start(out=self.h_ccin, in_=self.SCR[:, 0:512])
            self.s_r.inc(w, 16)
        ccw = self.s_r.n

        @blk.gpsimd
        def _(g_):
            g_.wait_ge(self.s_r.h, ccw)
            w = g_.collective_compute(
                "AllReduce", OP.add,
                replica_groups=[[0, 1], [2, 3], [4, 5], [6, 7]],
                ins=[self.h_ccin], outs=[self.h_ccout])
            self.g_sem.inc(w)
        cc_done = self.g_sem.n

        self.cc_done = cc_done

    def ph_cc_recv(self):
        blk = self.blk

        @blk.scalar
        def _(sc):
            self.g_sem.wait(sc, self.cc_done)
            w = sc.dma_start(out=self.SCAR[:], in_=self.h_ccout)
            self.s_r.inc(w, 16)
        scar_ld = self.s_r.n

        @blk.vector
        def _(v):
            v.wait_ge(self.s_r.h, scar_ld)
            w = v.tensor_scalar(self.SCAR[:], self.SCAR[:], self.FLAG[:],
                                None, OP.mult)
            self.v_sem.inc(w)

    def ph_epi_chunk(self, b, c):
        blk = self.blk
        pb = self.pb
        own0 = SBLK[b][2]
        j0 = c * 128 - own0
        if not hasattr(self, "_epi_cdt_prev"):
            self._epi_cdt_prev = [None, None]
            self._epi_yi_prev = [None, None]
            self._epi_chunk_yi = [None, None]
            self._epi_ev_g3 = None
            self._epi_ev_chunk = None
            self._epi_pending = []
            self._epi_eb_last = [None, None]

        silzc = self.SILZC
        sused = (self.SUSED, self.SUSED2)[c % 2]

        @blk.sync
        def _(s, c=c, silzc=silzc):
            prev = self._epi_ev_g3
            if prev is not None:
                self.v_sem.wait(s, prev)
            elif c == 0:
                self.v_sem.wait(s)
            if c == 4:
                self.t_sem.wait(s)
            w = s.dma_start(
                out=silzc[:, 0:8 * 128].rearrange("p (c t) -> p c t", c=8),
                in_=self.h_silz.rearrange("(c p) t -> p c t", p=128)
                    [:, 0:8, c * 128:c * 128 + 128])
            self.s_sz.inc(w, 16)
        silz_lo = self.s_sz.n
        silz_hi = None

        dtt_done = self.build_dtt(c)

        @blk.vector
        def _(v, c=c, dtt_done=dtt_done, sused=sused):
            self.v_sem.wait(v, dtt_done)
            prev_yi = self._epi_chunk_yi[c % 2]
            if prev_yi is not None:
                self.t_sem.wait(v, prev_yi)
            if c == 0:
                w = v.tensor_copy(sused[:], self.SCAR[:])
            else:
                w = v.tensor_tensor(sused[:], self.SCAR[:],
                                    self.SST[:, (c - 1) * 512: c * 512], OP.add)
            self.v_sem.chain(v, w)
            w = v.tensor_tensor(self.SCAR[:], self.SCAR[:], self.DTT[:], OP.mult)
            self.v_sem.inc(w)
        sused_done = self.v_sem.n

        YROT = (pb[7], pb[0], pb[1], pb[2])
        for g in range(G):
            i, gq = g % 4, g // 4

            if g == 2:
                @blk.sync
                def _(s, c=c, silzc=silzc):
                    prev = self._epi_ev_chunk
                    if prev is not None:
                        self.v_sem.wait(s, prev)
                    elif c == 0:
                        self.v_sem.wait(s)
                    if c == 4:
                        self.t_sem.wait(s)
                    w = s.dma_start(
                        out=silzc[:, 8 * 128:16 * 128]
                            .rearrange("p (c t) -> p c t", c=8),
                        in_=self.h_silz.rearrange("(c p) t -> p c t", p=128)
                            [:, 8:16, c * 128:c * 128 + 128])
                    self.s_sz2.inc(w, 16)
                silz_hi = self.s_sz2.n

            if g % 2 == 0:
                li = g // 2
                rbuf = (self.RCHL0, self.RCHL1)[li % 2]

                rsem = (self.s_rq, self.s_rq2)[li % 2]

                @blk.scalar
                def _(sc, g=g, c=c, li=li, rbuf=rbuf, rsem=rsem):
                    prev = self._epi_eb_last[li % 2]
                    if prev is None:
                        self.g_sem.wait(sc)
                    else:
                        self.g_sem.wait(sc, prev)
                    w = sc.dma_start(
                        out=rbuf[0:1, :],
                        in_=self.h_er[c:c + 1, (g // 2) * 1024:
                                      (g // 2) * 1024 + 1024])
                    rsem.inc(w, 16)
                self._rq_epi = ((self.s_rq, self.s_rq.n) if li % 2 == 0
                                else (self.s_rq2, self.s_rq2.n))

            ebt_b = (self.EBT, self.EBT2)[g % 2]
            cdt_b = (self.CDT, self.CDT2)[g % 2]

            rq_sem, rq_n = self._rq_epi

            @blk.gpsimd
            def _(g_, g=g, ebt_b=ebt_b, rq_sem=rq_sem, rq_n=rq_n):
                g_.wait_ge(rq_sem.h, rq_n)
                prev = self._epi_cdt_prev[g % 2]
                if prev is None:
                    self.v_sem.wait(g_)
                else:
                    self.v_sem.wait(g_, prev)
                rbuf = (self.RCHL0, self.RCHL1)[(g // 2) % 2]
                w = g_.partition_broadcast(
                    ebt_b[:],
                    rbuf[0:1, (g % 2) * 512:(g % 2) * 512 + 512], 128)
                self.g_sem.inc(w)
            eb_done = self.g_sem.n
            if g % 2 == 1:
                self._epi_eb_last[(g // 2) % 2] = eb_done

            @blk.vector
            def _(v, g=g, i=i, gq=gq, c=c, eb_done=eb_done, ebt_b=ebt_b,
                  cdt_b=cdt_b):
                self.g_sem.wait(v, eb_done)
                prev_yi = self._epi_yi_prev[g % 2]
                if prev_yi is not None:
                    self.t_sem.wait(v, prev_yi)
                w = v.tensor_tensor(
                    cdt_b[32 * i:32 * i + 32, :]
                        .rearrange("p (a f) -> p a f", a=4),
                    self.CT[32 * i:32 * i + 32,
                            gq * TOWN + c * 128: gq * TOWN + c * 128 + 128]
                        .unsqueeze(1).broadcast_to([32, 4, 128]),
                    ebt_b[32 * i:32 * i + 32, :]
                        .rearrange("p (a f) -> p a f", a=4),
                    OP.mult)
                self.v_sem.inc(w)
            cdt_done = self.v_sem.n
            self._epi_cdt_prev[g % 2] = cdt_done

            ye_bank = YROT[g % 4]

            @blk.tensor
            def _(t, g=g, i=i, gq=gq, cdt_done=cdt_done, ye_bank=ye_bank,
                  cdt_b=cdt_b, sused=sused):
                self.v_sem.wait(t, cdt_done)
                self.v_sem.wait(t, sused_done)
                for hp in range(4):
                    h = 4 * g + hp
                    pr, ph = hp // 2, h % 2
                    w = t.matmul(
                        ye_bank[64 * ph:64 * ph + 64, pr * 128:pr * 128 + 128],
                        sused[32 * i:32 * i + 32,
                              gq * 256 + hp * 64: gq * 256 + hp * 64 + 64],
                        cdt_b[32 * i:32 * i + 32, hp * 128:(hp + 1) * 128],
                        start=True, stop=True,
                        tile_position=(32 * i, 64 * ph))
                self.t_sem.inc(w)
            yi_mm = self.t_sem.n
            self._epi_yi_prev[g % 2] = yi_mm
            if g == G - 1:
                self._epi_chunk_yi[c % 2] = yi_mm

            def emit_evac(g=g, c=c, j0=j0, yi_mm=yi_mm,
                          silz_ld=(silz_lo if g < 4 else silz_hi),
                          ye_bank=ye_bank, silzc=silzc):
                @blk.vector
                def _(v):
                    self.t_sem.wait(v, yi_mm)
                    (self.s_sz if g < 4 else self.s_sz2).wait(v, silz_ld)
                    self.s_sp.wait(v, self.y1_ld)
                    for pr in range(2):
                        p_ = 2 * g + pr
                        sl = slice(p_ * 512 + j0, p_ * 512 + j0 + 128)
                        w = v.tensor_tensor(self.SCR[:, 0:128],
                                            ye_bank[:, pr * 128:pr * 128 + 128],
                                            silzc[:, p_ * 128:p_ * 128 + 128],
                                            OP.mult)
                        self.v_sem.chain(v, w)
                        w = v.tensor_tensor(self.Y1B[:, sl], self.Y1B[:, sl],
                                            self.SCR[:, 0:128], OP.add)
                        self.v_sem.chain(v, w)
                if g == 3:
                    self._epi_ev_g3 = self.v_sem.n
                elif g == G - 1:
                    self._epi_ev_chunk = self.v_sem.n

            self._epi_pending.append(emit_evac)
            while len(self._epi_pending) > 2:
                self._epi_pending.pop(0)()

    def ph_epi_flush(self):
        while self._epi_pending:
            self._epi_pending.pop(0)()

    def ph_norm_wout_block(self, b):
        blk = self.blk
        pb = self.pb
        col0, ncols, own0, nown = SBLK[b]

        for fc in range(NF):
            @blk.vector
            def _(v, fc=fc):
                if fc == 0:
                    self.v_sem.wait(v)
                else:
                    self.t_sem.wait(v)
                w = v.tensor_tensor(self.YSQ[:, 0:512],
                                    self.Y1B[:, fc * 512:(fc + 1) * 512],
                                    self.Y1B[:, fc * 512:(fc + 1) * 512],
                                    OP.mult)
                self.v_sem.inc(w)
            ysq_done = self.v_sem.n

            @blk.tensor
            def _(t, fc=fc, ysq_done=ysq_done):
                self.v_sem.wait(t, ysq_done)
                w = t.matmul(pb[6][0:KEXP, 0:512],
                             self.GINDs[:, fc * KEXP:(fc + 1) * KEXP],
                             self.YSQ[:, 0:512],
                             start=(fc == 0), stop=(fc == NF - 1))
                self.t_sem.inc(w)
        var_mm = self.t_sem.n

        @blk.vector
        def _(v):
            self.t_sem.wait(v, var_mm)
            w = v.tensor_scalar(self.RSTD[0:KEXP, 0:512], pb[6][0:KEXP, 0:512],
                                1.0 / 256.0, None, OP.mult)
            self.v_sem.inc(w)
        var_done = self.v_sem.n

        @blk.scalar
        def _(sc):
            self.v_sem.wait(sc, var_done)
            w = sc.activation(self.RSTD[0:KEXP, 0:512], self.RSTD[0:KEXP, 0:512],
                              AF.Ln, bias=self.MISC[0:KEXP, 0:1], scale=1.0)
            self.c_sem.chain(sc, w)
            w = sc.activation(self.RSTD[0:KEXP, 0:512], self.RSTD[0:KEXP, 0:512],
                              AF.Exp, bias=0.0, scale=-0.5)
            self.c_sem.inc(w)
        rstd_done0 = self.c_sem.n

        @blk.vector
        def _(v):
            self.c_sem.wait(v, rstd_done0)
            w = v.tensor_copy(self.RSTDB[0:KEXP, 0:512], self.RSTD[0:KEXP, 0:512])
            self.v_sem.inc(w)
        rstd_done = self.v_sem.n

        for fc in range(NF):
            @blk.tensor
            def _(t, fc=fc):
                if fc == 0:
                    self.v_sem.wait(t, rstd_done)
                else:
                    self.v_sem.wait(t)
                w = t.matmul(pb[4 + fc % 2][:, 0:512],
                             self.GSELs[0:KEXP, (fc // 2) * 128:(fc // 2) * 128 + 128],
                             self.RSTDB[0:KEXP, 0:512], start=True, stop=True)
                self.t_sem.inc(w)
            rb_mm = self.t_sem.n

            @blk.vector
            def _(v, fc=fc, rb_mm=rb_mm):
                self.t_sem.wait(v, rb_mm)
                w = v.scalar_tensor_tensor(
                    self.XNW[:, fc * BW: fc * BW + 512],
                    self.Y1B[:, fc * 512:(fc + 1) * 512],
                    self.NW[:, fc:fc + 1],
                    pb[4 + fc % 2][:, 0:512], OP.mult, OP.mult)
                self.v_sem.inc(w)
        yn_done = self.v_sem.n

        wo_mm = [None] * NF
        wo_ev = [None] * NF
        for wo in range(NF):
            buf = (self.WA, self.WB)[wo % 2]
            bsem = (self.s_wa, self.s_wb)[wo % 2]
            ld = self.stream_w(
                self.i_Wout[wo], buf[:], bsem,
                prior_t=wo_mm[wo - 2] if wo >= 2 else
                (self.last_wa_t, self.last_wb_t)[wo % 2])

            @blk.tensor
            def _(t, wo=wo, buf=buf, bsem=bsem, ld=ld):
                bsem.wait(t, ld)
                if wo == 0:
                    self.v_sem.wait(t, yn_done)
                if wo >= 2:
                    self.v_sem.wait(t, wo_ev[wo - 2])
                tgt = pb[2 * (wo % 2)]
                for kc in range(NF):
                    w = t.matmul(tgt[:, 0:512],
                                 buf[:, kc * 128:(kc + 1) * 128],
                                 self.XNW[:, kc * BW: kc * BW + 512],
                                 start=(kc == 0), stop=(kc == NF - 1))
                self.t_sem.inc(w)
            wo_mm[wo] = self.t_sem.n

            @blk.vector
            def _(v, wo=wo):
                self.t_sem.wait(v, wo_mm[wo])
                tgt = pb[2 * (wo % 2)]
                a = wo * T + CTX + own0
                w = v.scalar_tensor_tensor(self.X[:, a:a + 512], tgt[:, 0:512],
                                           self.BOUT[:, wo:wo + 1],
                                           self.X[:, a:a + 512], OP.add, OP.add)
                self.v_sem.inc(w)
            wo_ev[wo] = self.v_sem.n
        self.last_wa_t = wo_mm[NF - 2]
        self.last_wb_t = wo_mm[NF - 1]

    def ph_store_block(self, b):
        blk = self.blk
        col0, ncols, own0, nown = SBLK[b]
        vn = self.v_sem.n

        @blk.sync
        def _(s):
            self.v_sem.wait(s, vn)
            w = s.dma_start(
                out=self.o_out.rearrange("(c p) t -> p c t", p=128)
                    [:, :, own0:own0 + nown],
                in_=self.X[:].rearrange("p (c t) -> p c t", c=NF)
                    [:, :, CTX + own0:CTX + own0 + nown])
            self.s_st.inc(w, 16)

    def ph_finish(self):
        blk = self.blk

        @blk.sync
        def _(s):
            self.s_st.wait(s)
            if self.g_ld.n:
                self.g_ld.wait(s)



def _host_prepare(inputs):
    x = np.asarray(inputs["L0_attn_hidden_states"])
    Win = np.asarray(inputs["L0_attn_W_in"])
    b_in = np.asarray(inputs["L0_attn_b_in"])
    mup = np.asarray(inputs["L0_attn_mup_vector"])
    Win_pad = np.zeros((D, PROJ_PAD), np.float32)
    Win_pad[:, :PROJ] = Win
    b_pad = np.zeros((PROJ_PAD,), np.float32)
    b_pad[:PROJ] = b_in
    mup_pad = np.zeros((PROJ_PAD,), np.float32)
    mup_pad[:PROJ] = mup
    bmup = np.ascontiguousarray((b_pad * mup_pad).reshape(NFO, 128).T)
    mupc = np.ascontiguousarray(mup_pad.reshape(NFO, 128).T)

    V = np.asarray(inputs["L0_ffn_V"])
    U = np.asarray(inputs["L0_ffn_U"])
    Vc = np.ascontiguousarray(V.transpose(2, 0, 1).reshape(D, KR))
    Uc = np.ascontiguousarray(U.transpose(0, 2, 1).reshape(KR, D))

    conv_w = np.asarray(inputs["L0_attn_conv_w"])
    convw = np.ascontiguousarray(
        conv_w.reshape(NCONV, 128, 4).transpose(1, 0, 2).reshape(128, NCONV * 4))
    convb = np.ascontiguousarray(
        np.asarray(inputs["L0_attn_conv_b"]).reshape(NCONV, 128).T)

    def col(vec, n):
        return np.ascontiguousarray(np.asarray(vec).reshape(n, 128).T)

    REPS = np.zeros((KEXP, NKR * 128), np.float32)
    for rc in range(NKR):
        for p in range(128):
            REPS[2 * rc + p // 64, rc * 128 + p] = 1.0
    GSEL = np.zeros((KEXP, G * 128), np.float32)
    for g in range(G):
        GSEL[g, g * 128:(g + 1) * 128] = 1.0
    import ml_dtypes
    GIND = np.zeros((128, NF * KEXP), np.float32)
    for fc in range(NF):
        GIND[:, fc * KEXP + fc // 2] = 1.0
    MISCC = np.zeros((128, 8), np.float32)
    MISCC[:, 0] = 1e-5
    MISCC[:, 1] = 1.0
    MISCC[:, 2] = 80.0
    MISCC[:, 3] = 0.5
    E127 = np.zeros((128, 1), np.float32)
    E127[127, 0] = 1.0

    def tile_lhsT(W, nk, nm):
        return np.ascontiguousarray(
            W.reshape(nk, 128, nm, 128).transpose(2, 1, 0, 3).reshape(nm, 128, nk * 128))

    common = dict(
        Win=tile_lhsT(Win_pad, NF, NFO),
        W1=tile_lhsT(np.asarray(inputs["L0_ffn_W1"]), NF, NF),
        W2=np.ascontiguousarray(
            np.asarray(inputs["L0_ffn_W2"]).reshape(NF, 128, KEXP)
            .transpose(1, 0, 2).reshape(128, NF * KEXP)),
        Wout=tile_lhsT(np.asarray(inputs["L0_attn_W_out"]), NF, NF),
        Vc=tile_lhsT(Vc, NF, NKR), Uc=tile_lhsT(Uc, NKR, NF),
        bmup=bmup, mup=mupc,
        b1col=col(inputs["L0_ffn_b1"], NF),
        b2col=np.asarray(inputs["L0_ffn_b2"]).reshape(KEXP, 1).astype(np.float32),
        boutcol=col(inputs["L0_attn_b_out"], NF),
        nwcol=col(inputs["L0_attn_norm_w"], NF),
        convbcol=convb, convwcol=convw,
        Dcol=np.ascontiguousarray(
            np.repeat(np.asarray(inputs["L0_attn_D_param"]), PDIM)
            .reshape(NF, 128).T).astype(np.float32),
        Alogcol=np.asarray(inputs["L0_attn_A_log"]).reshape(NH, 1).astype(np.float32),
        dtbcol=np.asarray(inputs["L0_attn_dt_bias"]).reshape(NH, 1).astype(np.float32),
        maskinf=np.where(np.arange(Q)[:, None] <= np.arange(Q)[None, :], 0.0,
                         -1e30).astype(np.float32),
        triu01=np.where(np.arange(Q)[:, None] <= np.arange(Q)[None, :], 1.0,
                        0.0).astype(np.float32),
        I128=np.eye(128, dtype=np.float32),
        IB128=np.eye(128, dtype=np.float32).astype(ml_dtypes.bfloat16),
        e127=E127, misccol=MISCC, REPS=REPS, GSEL=GSEL.astype(ml_dtypes.bfloat16),
        GIND=GIND.astype(ml_dtypes.bfloat16),
    )

    in_maps = []
    for cidx in range(8):
        bb, hf = cidx // 2, cidx % 2
        lo = hf * TOWN
        xs = x[bb]
        if hf == 0:
            ctx_cols = np.zeros((CTX, D), np.float32)
        else:
            ctx_cols = xs[lo - CTX:lo]
        xl = np.concatenate([ctx_cols, xs[lo:lo + TOWN]], axis=0)
        m = dict(common)
        m["x_fm"] = np.ascontiguousarray(xl.T)
        m["flagcol"] = np.full((128, 1), 1.0 if hf == 1 else 0.0, np.float32)
        m["flaginvcol"] = np.full((128, 1), 0.0 if hf == 1 else 1.0, np.float32)
        in_maps.append(m)
    return in_maps


def _run(debug=()):
    b = Build(debug)
    nc = b.build()
    lower_extended_insts(nc)
    return b, nc


def kernel_dev(**inputs):
    in_maps = _host_prepare(inputs)
    b, nc = _run(_DEBUG)
    res = run_bass_kernel_spmd(nc, in_maps, core_ids=list(range(8)))
    dumps = {}
    for dram_name, name in b.dumps:
        dumps[name] = [res.results[c][dram_name] for c in range(8)]
    return _assemble(res), dumps


def _assemble(res):
    out = np.empty((4, 2048, D), np.float32)
    for c in range(8):
        b, hf = c // 2, c % 2
        out[b, hf * TOWN:(hf + 1) * TOWN] = res.results[c]["out_fm"].T
    return out


def kernel(**inputs):
    in_maps = _host_prepare(inputs)
    b, nc = _run(())
    res = run_bass_kernel_spmd(nc, in_maps, core_ids=list(range(8)))
    return _assemble(res)
